# version 38
# baseline (speedup 1.0000x reference)
import numpy as np
from contextlib import ExitStack

import concourse.bass as bass
import concourse.mybir as mybir
from concourse.bass_utils import run_bass_kernel_spmd

F32 = mybir.dt.float32
BF16 = mybir.dt.bfloat16
I32 = mybir.dt.int32
AF = mybir.ActivationFunctionType
ALU = mybir.AluOpType
AX = mybir.AxisListType

BIG_NEG = -1.0e9



def _merge_engine_blocks(nc):
    f = nc.m.functions[0]
    blocks = list(f.blocks)
    if len(blocks) <= 3:
        return
    by_name = {b.name: b for b in blocks}
    head, tail = blocks[0], blocks[-1]
    heads = [i.target for i in head.instructions
             if type(i).__name__ == "InstUnconditionalBranch"]
    new_blocks = [head]
    consumed = set()
    for h in heads:
        merged, cur, last_branch = [], h, None
        first = by_name[h]
        while cur != tail.name:
            b = by_name[cur]
            consumed.add(cur)
            nxt = None
            for i in b.instructions:
                if type(i).__name__ == "InstUnconditionalBranch":
                    nxt = i.target
                    last_branch = i
                else:
                    merged.append(i)
            cur = nxt
        merged.append(last_branch)
        first.instructions = merged
        new_blocks.append(first)
    for b in blocks[1:-1]:
        if b.name not in consumed and b not in new_blocks:
            new_blocks.append(b)
    new_blocks.append(tail)
    f.blocks = new_blocks


class Sem:
    def __init__(self, nc, ctx, name):
        self.h = ctx.enter_context(nc.semaphore(name))
        self.n = 0

    def inc(self, inst, k=1):
        inst.then_inc(self.h, k)
        self.n += k
        return self.n


def _pieces(D, cap=512):
    out, o = [], 0
    while o < D:
        w = min(cap, D - o)
        out.append((o, o + w))
        o += w
    return out


def build_nc(T=2048, D=1024, NR=4, NCORES=8):
    DC = D // 128
    NT = T // 128
    EW = D + 4
    TGW = min(512, T, D)
    NTG = T // TGW
    NH = max(1, D // TGW)
    WW = D // NH
    PCS = _pieces(D)

    nc = bass.Bass("TRN2", target_bir_lowering=False, detect_race_conditions=False)
    dp = nc.declare_dram_parameter

    x_in = dp("x", [T, D], F32, isOutput=False)
    wqt = dp("wqt", [D, D], F32, isOutput=False)
    wkt = dp("wkt", [D, D], F32, isOutput=False)
    wvt = dp("wvt", [D, D], F32, isOutput=False)
    wat = dp("wat", [D, D], F32, isOutput=False)
    rhst = dp("rhst", [D, D], F32, isOutput=False)
    bq_fm = dp("bq_fm", [128, DC], F32, isOutput=False)
    bk_fm = dp("bk_fm", [128, DC], F32, isOutput=False)
    ba_row = dp("ba_row", [1, D], F32, isOutput=False)
    bv_row = dp("bv_row", [1, D], F32, isOutput=False)
    tri_lo = dp("tri_lo", [128, 128], BF16, isOutput=False)
    triu_m = dp("triu_m", [128, 128], F32, isOutput=False)
    ident_f = dp("ident_f", [128, 128], F32, isOutput=False)
    ident_b = dp("ident_b", [128, 128], BF16, isOutput=False)
    pm_in = dp("pm", [5, 5], F32, isOutput=False)
    y_out = dp("y", [T, D], F32, isOutput=True)

    xn_sp = nc.dram_tensor("xn_sp", [DC, 128, T], BF16)
    qt_sp = nc.dram_tensor("qt_sp", [DC, 128, T], BF16)
    ql_sp = nc.dram_tensor("ql_sp", [DC, 128, T], BF16)
    rsc_dr = nc.dram_tensor("rsc_dr", [1, D], F32)
    cc1_in = nc.dram_tensor("cc1_in", [1, D], F32)
    cc1_out = nc.dram_tensor("cc1_out", [NR, D], F32)
    cc2_in = nc.dram_tensor("cc2_in", [DC * 128, EW], BF16)
    HROWS = (DC // 2) * 128
    cc2a_out = nc.dram_tensor("cc2a_out", [NR * HROWS, EW], BF16)
    cc2b_out = nc.dram_tensor("cc2b_out", [NR * HROWS, EW], BF16)

    ctx = ExitStack()
    sb = lambda name, shape, dt: ctx.enter_context(nc.sbuf_tensor(name, shape, dt))
    pst = lambda name, shape, dt=F32: ctx.enter_context(nc.psum_tensor(name, shape, dt))

    wk_fm = sb("wk_fm", [128, DC, T], BF16)
    v_aug = sb("v_aug", [128, NT, EW], BF16)
    qrhs = sb("qrhs", [128, DC, D], BF16)
    ptl = sb("ptl", [128, DC, EW], BF16)
    arena = sb("arena", [128, max(NT * D, DC * EW)], BF16)
    wwt = sb("wwt", [128, 2, DC, WW], BF16)
    xstr = sb("xstr", [128, 2, D], F32)
    outb = sb("outb", [128, 1, D], F32)
    xnb = sb("xnb", [128, 2, DC, TGW], BF16)
    etl = sb("etl", [128, DC, NT], F32)
    stg2 = sb("stg2", [128, 2, DC * 128], BF16)
    ub = sb("ub", [128, 2, D], F32)
    cb = sb("cb", [128, 2, D], F32)
    zi_t = sb("zi_t", [128, D], I32)
    qtmp = sb("qtmp", [128, D], BF16)
    qtmp2 = sb("qtmp2", [128, D], BF16)
    tri_s = sb("tri_s", [128, 128], BF16)
    triu_s = sb("triu_s", [128, 128], F32)
    idf_s = sb("idf_s", [5, 5], F32)
    idb_s = sb("idb_s", [128, 128], BF16)
    ones_c = sb("ones_c", [128, 1], BF16)
    ones_rb = sb("ones_rb", [1, 128], BF16)
    bqf_s = sb("bqf_s", [128, DC], F32)
    bkf_s = sb("bkf_s", [128, DC], F32)
    bqs_s = sb("bqs_s", [128, DC], F32)
    totals = sb("totals", [128, DC, NT], F32)
    tails = sb("tails", [128, DC, NT], F32)
    ctd_s = sb("ctd_s", [128, DC], F32)
    pm_s = sb("pm_s", [5, 5], F32)
    scl_fm = sb("scl_fm", [128, DC, 5], F32)
    lsc = sb("lsc", [128, NT], F32)
    rsc_fm = sb("rsc_fm", [128, DC], F32)
    rs_bc = sb("rs_bc", [128, D], F32)
    sm1 = sb("sm1", [128, 8, 1], F32)
    sm2 = sb("sm2", [128, 1], F32)
    sm3 = sb("sm3", [128, 1], F32)
    sm4 = sb("sm4", [128, 2], F32)
    smr = sb("smr", [128, 2], F32)
    c1em5 = sb("c1em5", [128, 1], F32)
    c_invD3 = sb("c_invD3", [128, 1], F32)
    c1em6 = sb("c1em6", [128, 1], F32)
    ct_aug = ub.ap()[0:5, 0, :]
    stg2f = stg2.ap().rearrange("p b n -> p (b n)")
    if 2 * DC * 128 >= 2048 and DC * 128 >= TGW:
        stg = stg2.ap()[:, :, 0:TGW]
        msb = stg2f[:, 0:1040].rearrange("p (b n) -> p b n", b=2)
        gsc = stg2f[:, 0:2048].rearrange("p (b n) -> p b n", b=2)
        scb = stg2f[:, 0:512].rearrange("p (b n) -> p b n", b=4)
    else:
        stg = sb("stg_t", [128, 2, TGW], BF16).ap()[:, :, :]
        msb = sb("msb_t", [128, 2, 520], BF16).ap()[:, :, :]
        gsc = sb("gsc_t", [128, 2, 512], BF16).ap()[:, :, :]
        scb = sb("scb_t", [128, 4, 128], BF16).ap()[:, :, :]
    if WW >= 512:
        qtb_s = lambda s: wwt.ap()[:, s // 2, :, (s % 2) * 128:(s % 2) * 128 + 128]
        qlb_s = lambda s: wwt.ap()[:, s // 2, :, 256 + (s % 2) * 128:256 + (s % 2) * 128 + 128]
    else:
        _qtb_t = sb("qtb_t", [128, 4, DC, 128], BF16)
        _qlb_t = sb("qlb_t", [128, 4, DC, 128], BF16)
        qtb_s = lambda s: _qtb_t.ap()[:, s, :, :]
        qlb_s = lambda s: _qlb_t.ap()[:, s, :, :]
    sc_exp = ub.ap()[0:5, 1, :]
    rs_row = ub.ap()[0:1, 1, :]
    b_row = qtmp.ap()[0:1, :]

    num_ps = pst("num_ps", [128, D], F32)
    ffn_ps = pst("ffn_ps", [128, D], F32)
    den_ps = pst("den_ps", [128, 128], F32)
    sc_ps0 = pst("sc_ps0", [128, 128], F32)
    sc_ps1 = pst("sc_ps1", [128, 128], F32)
    tp_ps = pst("tp_ps", [128, DC * 128], BF16)

    dSY = Sem(nc, ctx, "dSY")
    dXL = Sem(nc, ctx, "dXL")
    dSC = Sem(nc, ctx, "dSC")
    dGP = Sem(nc, ctx, "dGP")
    sCC = Sem(nc, ctx, "sCC")
    sT = Sem(nc, ctx, "sT")
    sV = Sem(nc, ctx, "sV")
    sS = Sem(nc, ctx, "sS")
    sG = Sem(nc, ctx, "sG")

    blk = ctx.enter_context(nc.Block())

    def wge(e, sem, n=None):
        v = sem.n if n is None else n
        if v > 0:
            e.wait_ge(sem.h, v)

    @blk.sync
    def _(e):
        for dst, src in ((tri_s, tri_lo), (triu_s, triu_m), (idb_s, ident_b), (bqf_s, bq_fm), (bkf_s, bk_fm),
                         (pm_s, pm_in)):
            i = e.dma_start(out=dst[:, :], in_=src.ap())
            dSY.inc(i, 16)
        i = e.dma_start(out=idf_s[:, :], in_=ident_f.ap()[0:5, 0:5])
        dSY.inc(i, 16)

    @blk.gpsimd
    def _(e):
        for t_, val in ((ones_c, 1.0), (ones_rb, 1.0)):
            i = e.memset(t_[:, :], val)
            sG.inc(i)
        i = e.memset(v_aug[:, :, :], 0.0)
        sG.inc(i)
        i = e.memset(ptl[:, :, :], 0.0)
        sG.inc(i)
        i = e.memset(c1em5[:, :], float(np.float32(1e-5)) * (2.0 ** 20))
        sG.inc(i)
        i = e.memset(c1em6[:, :], 1e-6)
        sG.inc(i)
        i = e.memset(c_invD3[:, :], 1.0 / float(D))
        sG.inc(i)

    def round_cast(e_s, u_ap, c_ap):
        e_s.copy(zi_t[:, :], u_ap)
        return e_s.copy(c_ap, zi_t[:, :])

    def round_chain(e, u_ap, c_ap, out_ap):
        e.tensor_sub(u_ap, u_ap, c_ap)
        e.scalar_tensor_tensor(c_ap, u_ap, 0.5, c_ap,
                               ALU.is_gt, ALU.add)
        e.tensor_single_scalar(u_ap, u_ap, -0.5, ALU.is_lt)
        return e.tensor_sub(out_ap, c_ap, u_ap)

    def emit_rhs_quant():
        for jc in range(DC):
            @blk.sync
            def _(e, jc=jc):
                wge(e, sV)
                i = e.dma_start(out=ub[:, 1, :], in_=rhst.ap()[jc * 128:(jc + 1) * 128, :])
                dSY.inc(i, 16)

            @blk.vector
            def _(e, jc=jc):
                wge(e, dSY)
                i = e.reduce_max(sm1[:, :, :], ub[:, 1, :].rearrange("p (a b) -> p a b", a=8),
                                 axis=AX.X, apply_absolute_value=True)
                wge(e, sV, sV.inc(i))
                i = e.reduce_max(sm2[:, :], sm1[:, :, 0], axis=AX.X, apply_absolute_value=True)
                wge(e, sV, sV.inc(i))
                i = e.tensor_scalar_max(sm2[:, :], sm2[:, :], 1e-12)
                wge(e, sV, sV.inc(i))
                e.tensor_scalar_mul(rsc_fm[:, jc:jc + 1], sm2[:, :], 1.0 / 127.0)
                i = e.reciprocal(sm3[:, :], sm2[:, :])
                wge(e, sV, sV.inc(i))
                i = e.tensor_scalar_mul(sm3[:, :], sm3[:, :], 127.0)
                sV.inc(i)

            @blk.scalar
            def _(e, jc=jc):
                wge(e, sV)
                e.activation(ub[:, 0, :], ub[:, 1, :], AF.Copy, scale=sm3[:, 0:1])
                i = round_cast(e, ub[:, 0, :], cb[:, 0, :])
                sS.inc(i)

            @blk.vector
            def _(e, jc=jc):
                wge(e, sS)
                i = round_chain(e, ub[:, 0, :], cb[:, 0, :], qtmp[:, :])
                sV.inc(i)

            @blk.tensor
            def _(e, jc=jc):
                wge(e, sV)
                wge(e, dSY)
                if jc > 0:
                    pass
                for dc in range(DC):
                    i = e.transpose(tp_ps[:, dc * 128:(dc + 1) * 128],
                                    qtmp[:, dc * 128:(dc + 1) * 128], idb_s[:, :])
                sT.inc(i)

            @blk.vector
            def _(e, jc=jc):
                wge(e, sT)
                for dc in range(DC):
                    i = e.tensor_copy(qrhs[:, dc, jc * 128:(jc + 1) * 128],
                                      tp_ps[:, dc * 128:(dc + 1) * 128])
                sV.inc(i)


    p1_V = []
    ev_S = {}
    ql_dma, xn_dma = {}, {}
    xcp_mark, sq_mark, xl_mark, sqrt_mark, r_mark = {}, {}, {}, {}, {}
    cast1_mark, G_mark, xn_mark, tql_mark, txn_mark = {}, {}, {}, {}, {}
    G0 = sG.n
    D0 = dSY.n

    def w_load(e, w_dram, h, wb):
        i = e.dma_start(out=wwt[:, wb, :, :],
                        in_=w_dram.ap()[:, h * WW:(h + 1) * WW]
                        .rearrange("(c p) o -> p c o", p=128))
        return dGP.inc(i, 16)

    def x_load(e, tg, xb):
        i = e.dma_start(out=xnb[:, xb, :, :],
                        in_=xn_sp.ap()[:, :, tg * TGW:(tg + 1) * TGW]
                        .rearrange("c p t -> p c t"))
        return dSY.inc(i, 16)

    def k_evac(e, og, tg, psl):
        i = e.activation(wk_fm[:, og, tg * TGW:(tg + 1) * TGW], psl, AF.Relu,
                         bias=bkf_s[:, og:og + 1])
        sS.inc(i)

    k_tm, k_pe, k_w, k_x = {}, {}, {}, {}

    def emit_kproj_w():
        @blk.gpsimd
        def _(e):
            for h in range(NH):
                k_w[h] = w_load(e, wkt, h, h % 2)

    def emit_kproj_x(tg):
        @blk.gpsimd
        def _(e, tg=tg):
            wge(e, dSC, xn_dma[min(NT - 1, (tg + 1) * (TGW // 128) - 1)])
            if tg >= 2:
                wge(e, sT, k_tm[(NH - 1, tg - 2)])
            i = e.dma_start(out=xnb[:, tg % 2, :, :],
                            in_=xn_sp.ap()[:, :, tg * TGW:(tg + 1) * TGW]
                            .rearrange("c p t -> p c t"))
            k_x[tg] = dGP.inc(i, 16)

    def emit_kproj_h(tg, h):
        for oc in range(WW // 128):
            og = h * (WW // 128) + oc
            gi = (tg * NH + h) * (WW // 128) + oc
            psl = (num_ps if gi % 2 == 0 else ffn_ps)[:, 0:TGW]

            @blk.tensor
            def _(e, h=h, tg=tg, oc=oc, psl=psl, gi=gi):
                if oc == 0 and h == 0:
                    wge(e, dGP, k_x[tg])
                if gi >= 2:
                    wge(e, sS, k_pe[gi - 2])
                for dc in range(DC):
                    i = e.matmul(psl, wwt[:, h % 2, dc, oc * 128:(oc + 1) * 128],
                                 xnb[:, tg % 2, dc, :], start=(dc == 0), stop=(dc == DC - 1))
                k_tm[(h, tg)] = sT.inc(i)

            @blk.scalar
            def _(e, og=og, tg=tg, psl=psl, gi=gi, h=h):
                if gi == 0:
                    wge(e, dSY, D0)
                wge(e, sT, k_tm[(h, tg)])
                k_evac(e, og, tg, psl)
                k_pe[gi] = sS.n

    emit_kproj_w()

    TPG = TGW // 128
    k_x_at, k_h_at = {}, {}
    for tg in range(NTG):
        base = (tg + 1) * TPG
        k_x_at.setdefault(base, []).append(tg)
        for h in range(NH):
            k_h_at.setdefault(base + 2 + h, []).append((tg, h))

    def round_v(tc):
        @blk.vector
        def _(e, tc=tc):
            wge(e, sS, cast1_mark[tc])
            if tc > 0:
                wge(e, sT, tql_mark[tc - 1])
            p = tc % 2
            i = round_chain(e, ub[:, p, :], cb[:, p, :], qtmp[:, :])
            sV.inc(i)

    def round_stage(tc):
        @blk.tensor
        def _(e, tc=tc):
            wge(e, sV)
            if tc > 0:
                wge(e, sS, ev_S[(tc - 1, 1)])
            for dc in range(DC):
                i = e.transpose(tp_ps[:, dc * 128:(dc + 1) * 128],
                                qtmp[:, dc * 128:(dc + 1) * 128], idb_s[:, :])
            tql_mark[tc] = sT.inc(i)

        @blk.scalar
        def _(e, tc=tc):
            wge(e, sT)
            if tc > 0:
                wge(e, dSC, ql_dma[tc - 1])
            i = e.copy(stg2[:, 1, :], tp_ps[:, :])
            ev_S[(tc, 0)] = sS.inc(i)
            i = e.dma_start(out=ql_sp.ap()[:, :, tc * 128:(tc + 1) * 128].rearrange("c p t -> p c t"),
                            in_=stg2[:, 1, :].rearrange("p (c t) -> p c t", c=DC))
            ql_dma[tc] = dSC.inc(i, 16)

        @blk.tensor
        def _(e, tc=tc):
            wge(e, sS, ev_S[(tc, 0)])
            for dc in range(DC):
                i = e.transpose(tp_ps[:, dc * 128:(dc + 1) * 128],
                                qtmp2[:, dc * 128:(dc + 1) * 128], idb_s[:, :])
            txn_mark[tc] = sT.inc(i)

        @blk.scalar
        def _(e, tc=tc):
            wge(e, sT)
            if tc > 0:
                wge(e, dSC, xn_dma[tc - 1])
            i = e.copy(stg2[:, 0, :], tp_ps[:, :])
            ev_S[(tc, 1)] = sS.inc(i)
            i = e.dma_start(out=xn_sp.ap()[:, :, tc * 128:(tc + 1) * 128].rearrange("c p t -> p c t"),
                            in_=stg2[:, 0, :].rearrange("p (c t) -> p c t", c=DC))
            xn_dma[tc] = dSC.inc(i, 16)

    for tc in range(NT):
        @blk.sync
        def _(e, tc=tc):
            if tc >= 2:
                wge(e, sS, xn_mark[tc - 2])
            i = e.dma_start(out=xstr[:, tc % 2, :], in_=x_in.ap()[tc * 128:(tc + 1) * 128, :])
            xl_mark[tc] = dXL.inc(i, 16)

        @blk.vector
        def _(e, tc=tc):
            wge(e, dXL, xl_mark[tc])
            xv = xstr[:, tc % 2, :]
            xcv = xv
            i = e.reduce_sum(sm2[:, :], xv, axis=AX.X)
            wge(e, sV, sV.inc(i))
            i = e.tensor_scalar(xcv, xv, float(D), sm2[:, 0:1],
                                ALU.mult, ALU.subtract)
            xcp_mark[tc] = sV.inc(i)
            i = e.reduce_max(sm2[:, :], xcv, axis=AX.X, apply_absolute_value=True)
            wge(e, sV, sV.inc(i))
            i = e.reciprocal(smr[:, tc % 2:tc % 2 + 1], sm2[:, :])
            wge(e, sV, sV.inc(i))
            i = e.tensor_scalar_mul(smr[:, tc % 2:tc % 2 + 1], smr[:, tc % 2:tc % 2 + 1], 127.0)
            wge(e, sV, sV.inc(i))

        @blk.scalar
        def _(e, tc=tc):
            wge(e, sV, xcp_mark[tc])
            wge(e, sG, G0)
            i = e.activation(outb[:, 0, :], xstr[:, tc % 2, :], AF.Square,
                             accum_out=sm3[:, 0:1])
            sq_mark[tc] = sS.inc(i)
            wge(e, sS, sq_mark[tc])
            i = e.activation(sm4[:, tc % 2:tc % 2 + 1], sm3[:, :], AF.Sqrt,
                             bias=c1em5[:, 0:1], scale=c_invD3[:, 0:1])
            sqrt_mark[tc] = sS.inc(i)

        @blk.vector
        def _(e, tc=tc):
            wge(e, sS, sqrt_mark[tc])
            i = e.reciprocal(sm4[:, tc % 2:tc % 2 + 1], sm4[:, tc % 2:tc % 2 + 1])
            r_mark[tc] = sV.inc(i)
            wge(e, sV, r_mark[tc])
            i = e.tensor_scalar(lsc[:, tc:tc + 1], sm2[:, :], sm4[:, tc % 2:tc % 2 + 1],
                                1.0 / 127.0, ALU.mult, ALU.mult)
            sV.inc(i)

        @blk.scalar
        def _(e, tc=tc):
            wge(e, sV, r_mark[tc])
            p = tc % 2
            e.activation(ub[:, p, :], xstr[:, p, :], AF.Copy, scale=smr[:, p:p + 1])
            e.copy(zi_t[:, :], ub[:, p, :])
            i = e.copy(cb[:, p, :], zi_t[:, :])
            cast1_mark[tc] = sS.inc(i)

        if tc > 0:
            round_v(tc - 1)
            round_stage(tc - 1)

        @blk.scalar
        def _(e, tc=tc):
            p = tc % 2
            if tc > 0:
                wge(e, sT, txn_mark[tc - 1])
            i = e.activation(qtmp2[:, :], xstr[:, p, :], AF.Copy, scale=sm4[:, p:p + 1])
            xn_mark[tc] = sS.inc(i)

        for tg in k_x_at.get(tc, []):
            emit_kproj_x(tg)
        for tg, h in k_h_at.get(tc, []):
            emit_kproj_h(tg, h)

    round_v(NT - 1)
    round_stage(NT - 1)
    for t in range(NT, NT + TPG + NH + 3):
        for tg in k_x_at.get(t, []):
            emit_kproj_x(tg)
        for tg, h in k_h_at.get(t, []):
            emit_kproj_h(tg, h)

    def fm_projection(w_dram, evac, head_wait=None, pre_w=None, pre_x0=None):
        units = [(h, tg) for h in range(NH) for tg in range(NTG)]
        t_marks = {}
        pe_marks = {}
        w_marks = {}
        x_marks = {}
        for ui, (h, tg) in enumerate(units):
            wb, xb = h % 2, ui % 2
            if ui == 0:
                if pre_w is not None:
                    for hh in range(NH):
                        w_marks[hh] = pre_w[hh]
                else:
                    @blk.gpsimd
                    def _(e, h=h):
                        wge(e, sT)
                        w_marks[h] = w_load(e, w_dram, h, h % 2)
                if pre_x0 is not None:
                    x_marks[0] = pre_x0
                else:
                    @blk.sync
                    def _(e, tg=tg, xb=xb):
                        wge(e, sT)
                        wge(e, dSC, xn_dma[NT - 1])
                        x_marks[0] = x_load(e, tg, xb)
            if ui + 1 < len(units):
                h2, tg2 = units[ui + 1]
                if h2 != h and pre_w is None:
                    @blk.gpsimd
                    def _(e, h2=h2):
                        if h2 >= 2:
                            wge(e, sT, t_marks[(h2 - 2, NTG - 1)])
                        w_marks[h2] = w_load(e, w_dram, h2, h2 % 2)
                @blk.sync
                def _(e, ui=ui, tg2=tg2, xb2=((ui + 1) % 2)):
                    if ui >= 1:
                        wge(e, sT, t_marks[units[ui - 1]])
                    x_marks[ui + 1] = x_load(e, tg2, xb2)
            for oc in range(WW // 128):
                og = h * (WW // 128) + oc
                psl = (num_ps if oc % 2 == 0 else ffn_ps)[:, 0:TGW]
                @blk.tensor
                def _(e, h=h, tg=tg, oc=oc, og=og, psl=psl, wb=wb, xb=xb, ui=ui):
                    if oc == 0:
                        wge(e, dSY, x_marks[ui])
                        wge(e, dGP, w_marks[h])
                    gi = ui * (WW // 128) + oc
                    if gi == 0:
                        if head_wait is None:
                            wge(e, sS)
                            wge(e, sV)
                        else:
                            head_wait(e)
                    elif gi >= 2:
                        wge(e, sS, pe_marks[gi - 2])
                    for dc in range(DC):
                        i = e.matmul(psl, wwt[:, h % 2, dc, oc * 128:(oc + 1) * 128],
                                     xnb[:, xb, dc, :], start=(dc == 0), stop=(dc == DC - 1))
                    t_marks[(h, tg)] = sT.inc(i)
                @blk.scalar
                def _(e, og=og, tg=tg, psl=psl, ui=ui, oc=oc):
                    wge(e, sT)
                    evac(e, og, tg, psl)
                    pe_marks[ui * (WW // 128) + oc] = sS.n
        return t_marks

    def tm_projection(w_dram, b_row_s, evac):
        units = [(h, tg) for h in range(NH) for tg in range(NTG)]
        t_marks = {}
        pe_marks = {}
        w_marks = {}
        x_marks = {}
        for ui, (h, tg) in enumerate(units):
            wb, xb = h % 2, ui % 2
            if ui == 0:
                @blk.gpsimd
                def _(e, h=h):
                    wge(e, sT)
                    w_marks[h] = w_load(e, w_dram, h, h % 2)
                @blk.sync
                def _(e, tg=tg, xb=xb):
                    wge(e, sT)
                    wge(e, dSC, xn_dma[NT - 1])
                    x_marks[0] = x_load(e, tg, xb)
            if ui + 1 < len(units):
                h2, tg2 = units[ui + 1]
                if h2 != h:
                    @blk.gpsimd
                    def _(e, h2=h2):
                        if h2 >= 2:
                            wge(e, sT, t_marks[(h2 - 2, NTG - 1)])
                        w_marks[h2] = w_load(e, w_dram, h2, h2 % 2)
                @blk.sync
                def _(e, ui=ui, tg2=tg2):
                    if ui >= 1:
                        wge(e, sT, t_marks[units[ui - 1]])
                    x_marks[ui + 1] = x_load(e, tg2, (ui + 1) % 2)
            for tw in range(TGW // 128):
                tc = tg * (TGW // 128) + tw
                psl = (num_ps if tc % 2 == 0 else ffn_ps)[:, 0:WW]
                @blk.tensor
                def _(e, h=h, tg=tg, tw=tw, psl=psl, wb=wb, xb=xb, ui=ui):
                    if tw == 0:
                        wge(e, dSY, x_marks[ui])
                        wge(e, dGP, w_marks[h])
                        wge(e, sG)
                    wge(e, sS)
                    for dc in range(DC):
                        e.matmul(psl, xnb[:, xb, dc, tw * 128:(tw + 1) * 128],
                                 wwt[:, h % 2, dc, :], start=(dc == 0), stop=False)
                    i = e.matmul(psl, ones_rb[:, :], b_row_s[:, h * WW:(h + 1) * WW],
                                 start=False, stop=True)
                    t_marks[(h, tg)] = sT.inc(i)
                @blk.scalar
                def _(e, tc=tc, h=h, psl=psl):
                    wge(e, sT)
                    evac(e, tc, h, psl)

    rs_dma = []

    def emit_rs_spill():
        @blk.scalar
        def _(e):
            wge(e, sV)
            i = e.dma_start(out=rsc_dr.ap().rearrange("o (c p) -> p (o c)", p=128),
                            in_=rsc_fm[:, :], allow_slow_non_contiguous=True)
            dSC.inc(i, 16)
            wge(e, dSC)
            i = e.dma_start(out=rs_row, in_=rsc_dr.ap())
            rs_dma.append(dSC.inc(i, 16))

    emit_rhs_quant()
    emit_rs_spill()


    arena_la = arena.ap()[:, 0:NT * D].rearrange("p (c d) -> p c d", c=NT)

    @blk.gpsimd
    def _(e):
        wge(e, sT)
        i = e.dma_start(out=b_row, in_=ba_row.ap())
        dGP.inc(i, 16)

    def a_evac(e, tc, h, psl):
        i = e.activation(arena_la[:, tc, h * WW:(h + 1) * WW], psl, AF.Sigmoid)
        sS.inc(i)
    tm_projection(wat, b_row, a_evac)

    for tc in range(NT):
        @blk.scalar
        def _(e, tc=tc):
            i = e.activation(arena_la[:, tc, 0:D], arena_la[:, tc, 0:D], AF.Ln,
                             bias=c1em6[:, 0:1])
            sS.inc(i)

    @blk.gpsimd
    def _(e):
        wge(e, sT)
        i = e.dma_start(out=b_row, in_=bv_row.ap())
        dGP.inc(i, 16)

    def v_evac(e, tc, h, psl):
        i = e.activation(v_aug[:, tc, h * WW:(h + 1) * WW], psl, AF.Copy)
        sS.inc(i)
    tm_projection(wvt, b_row, v_evac)

    @blk.vector
    def _(e):
        wge(e, sG)
        for tc in range(NT):
            i = e.memset(v_aug[:, tc, D:D + 1], 1.0)
            sV.inc(i)

    q_w0, q_x0 = [], []

    @blk.gpsimd
    def _(e):
        wge(e, sT)
        for h in range(NH):
            q_w0.append(w_load(e, wqt, h, h % 2))

    @blk.sync
    def _(e):
        wge(e, sT)
        q_x0.append(x_load(e, 0, 0))


    @blk.tensor
    def _(e):
        wge(e, sS)
        wge(e, sG)
        for dc in range(DC):
            for tc in range(NT):
                i = e.matmul(den_ps[:, dc * NT + tc:dc * NT + tc + 1],
                             arena_la[:, tc, dc * 128:(dc + 1) * 128], ones_c[:, :],
                             start=True, stop=True)
        sT.inc(i)

    @blk.vector
    def _(e):
        wge(e, sT)
        i = e.tensor_copy(totals[:, :, :],
                          den_ps[:, 0:DC * NT].rearrange("p (c t) -> p c t", c=DC))
        wge(e, sV, sV.inc(i))
        i = e.memset(tails[:, :, NT - 1:NT], 0.0)
        wge(e, sV, sV.inc(i))
        for c in range(NT - 2, -1, -1):
            i = e.tensor_add(tails[:, :, c:c + 1], tails[:, :, c + 1:c + 2],
                             totals[:, :, c + 1:c + 2])
            wge(e, sV, sV.inc(i))
        i = e.tensor_add(ctd_s[:, :], tails[:, :, 0], totals[:, :, 0])
        sV.inc(i)

    @blk.gpsimd
    def _(e):
        wge(e, sV)
        i = e.dma_start(out=cc1_in.ap().rearrange("o (c p) -> p (o c)", p=128),
                        in_=ctd_s[:, :], allow_slow_non_contiguous=True)
        dGP.inc(i, 16)
        wge(e, dGP)
        i = e.collective_compute(
            "AllGather", ALU.bypass,
            replica_groups=[[g * NR + r for r in range(NR)] for g in range(NCORES // NR)],
            ins=[cc1_in.ap().opt()], outs=[cc1_out.ap().opt()])
        sCC.inc(i)

    def emit_rs_bcast():
        @blk.gpsimd
        def _(e):
            i = e.memset(ct_aug, 1.0)
            sG.inc(i)

        @blk.tensor
        def _(e):
            wge(e, dSC, rs_dma[0])
            wge(e, sG)
            for (a, b) in PCS:
                i = e.matmul(num_ps[:, a:b], ub.ap()[0:1, 0, 0:128], rs_row[:, a:b],
                             start=True, stop=True)
            sT.inc(i)

        @blk.vector
        def _(e):
            wge(e, sT)
            i = e.tensor_copy(rs_bc[:, :], num_ps[:, :])
            sV.inc(i)


    @blk.scalar
    def _(e):
        wge(e, sV)
        i = e.activation(etl[:, :, :], tails[:, :, :], AF.Exp)
        sS.inc(i)
    GW = max(g for g in (4, 2, 1) if NT % g == 0 and g * 128 <= D)
    groups = [(dc, tc0) for dc in range(DC) for tc0 in range(0, NT, GW)]
    exp_S = {}
    for gi, (dc, tc0) in enumerate(groups):
        psl = (num_ps if gi % 2 == 0 else ffn_ps)
        @blk.tensor
        def _(e, dc=dc, tc0=tc0, psl=psl, gi=gi):
            if gi < 2:
                wge(e, sS)
            else:
                wge(e, sS, exp_S[gi - 2])
            for tw in range(GW):
                i = e.matmul(psl[:, tw * 128:(tw + 1) * 128],
                             arena_la[:, tc0 + tw, dc * 128:(dc + 1) * 128], tri_s[:, :],
                             start=True, stop=True)
            sT.inc(i)
        @blk.scalar
        def _(e, dc=dc, tc0=tc0, psl=psl, gi=gi):
            wge(e, sT)
            if gi >= 2:
                wge(e, sV)
            i = e.activation(gsc[:, gi % 2, 0:GW * 128], psl[:, 0:GW * 128], AF.Exp)
            exp_S[gi] = sS.inc(i)
        @blk.vector
        def _(e, dc=dc, tc0=tc0, gi=gi):
            wge(e, sS, exp_S[gi])
            for tw in range(GW):
                tc = tc0 + tw
                i = e.scalar_tensor_tensor(
                    wk_fm[:, dc, tc * 128:(tc + 1) * 128],
                    gsc[:, gi % 2, tw * 128:(tw + 1) * 128].rearrange("p a b -> p (a b)") if False else gsc[:, gi % 2, tw * 128:(tw + 1) * 128],
                    etl[:, dc, tc:tc + 1],
                    wk_fm[:, dc, tc * 128:(tc + 1) * 128],
                    ALU.mult, ALU.mult)
            sV.inc(i)

    arena_wt = arena.ap()[:, 0:NT * D].rearrange("p (c d) -> p c d", c=NT)
    for tc in range(NT):
        @blk.tensor
        def _(e, tc=tc):
            wge(e, sV)
            for dc in range(DC):
                i = e.transpose(tp_ps[:, dc * 128:(dc + 1) * 128],
                                wk_fm[:, dc, tc * 128:(tc + 1) * 128], idb_s[:, :])
            sT.inc(i)

        @blk.vector
        def _(e, tc=tc):
            wge(e, sT)
            i = e.tensor_copy(arena_wt[:, tc, :], tp_ps[:, :])
            sV.inc(i)

    m_pieces = PCS + [(D, EW)]
    m_units = [(dc, a, b) for dc in range(DC) for (a, b) in m_pieces]
    mV, mG = {}, {}
    for mi, (dc, a, b) in enumerate(m_units):
        w = b - a
        pb = mi % 2
        mps = (num_ps if pb == 0 else ffn_ps)
        @blk.tensor
        def _(e, dc=dc, a=a, b=b, w=w, mi=mi, pb=pb, mps=mps):
            if mi == 0:
                wge(e, sV)
                wge(e, sS)
            if mi >= 2:
                wge(e, sV, mV[mi - 2])
            for tc in range(NT):
                i = e.matmul(mps[:, 0:w],
                             arena_wt[:, tc, dc * 128:(dc + 1) * 128],
                             v_aug[:, tc, a:b], start=(tc == 0), stop=(tc == NT - 1))
            sT.inc(i)

        @blk.vector
        def _(e, w=w, mi=mi, pb=pb, mps=mps):
            wge(e, sT)
            if mi >= 2:
                wge(e, dGP, mG[mi - 2])
            i = e.tensor_copy(msb[:, pb, 0:w], mps[:, 0:w])
            mV[mi] = sV.inc(i)

        @blk.gpsimd
        def _(e, dc=dc, a=a, b=b, w=w, mi=mi, pb=pb):
            wge(e, sV, mV[mi])
            i = e.dma_start(out=cc2_in.ap()[dc * 128:(dc + 1) * 128, a:b],
                            in_=msb[:, pb, 0:w])
            mG[mi] = dGP.inc(i, 16)
            if mi == DC // 2 * len(m_pieces) - 1:
                wge(e, dGP)
                i = e.collective_compute(
                    "AllGather", ALU.bypass,
                    replica_groups=[[g * NR + r for r in range(NR)]
                                    for g in range(NCORES // NR)],
                    ins=[cc2_in.ap()[0:HROWS, :].opt()], outs=[cc2a_out.ap().opt()])
                sCC.inc(i)

    p2g_T = sT.n

    emit_rs_bcast()

    @blk.gpsimd
    def _(e):
        wge(e, dGP, mG[DC * len(m_pieces) - 1])
        i = e.collective_compute(
            "AllGather", ALU.bypass,
            replica_groups=[[g * NR + r for r in range(NR)] for g in range(NCORES // NR)],
            ins=[cc2_in.ap()[HROWS:2 * HROWS, :].opt()], outs=[cc2b_out.ap().opt()])
        sCC.inc(i)


    @blk.sync
    def _(e):
        wge(e, sCC, 1)
        wge(e, sG)
        wge(e, sV)
        i = e.dma_start(out=ct_aug[0:NR, :], in_=cc1_out.ap())
        dSY.inc(i, 16)

    @blk.tensor
    def _(e):
        wge(e, dSY)
        wge(e, sG)
        wge(e, sS)
        wge(e, sV)
        for (a, b) in PCS:
            i = e.matmul(num_ps[0:5, a:b], pm_s[:, :], ct_aug[:, a:b],
                         start=True, stop=True)
        sT.inc(i)

    @blk.scalar
    def _(e):
        wge(e, sT)
        i = e.activation(sc_exp, num_ps[0:5, :], AF.Exp)
        sS.inc(i)

    @blk.tensor
    def _(e):
        wge(e, sS)
        for dc in range(DC):
            i = e.transpose(sc_ps0[:, dc * 8:dc * 8 + 5],
                            sc_exp[:, dc * 128:(dc + 1) * 128], idf_s[:, :])
        sT.inc(i)

    @blk.vector
    def _(e):
        wge(e, sT)
        for dc in range(DC):
            i = e.tensor_copy(scl_fm[:, dc, :], sc_ps0[:, dc * 8:dc * 8 + 5])
        wge(e, sV, sV.inc(i))
        i = e.tensor_mul(bqs_s[:, :], bqf_s[:, :], scl_fm[:, :, 4])
        sV.inc(i)
    scales_V = sV.n

    q_sp_tg0 = []

    def q_evac(e, og, tg, psl):
        wge(e, dSC, max(0, dSC.n - 16))
        wge(e, sV, scales_V)
        i = e.activation(stg[:, tg % 2, :], psl, AF.Relu,
                         scale=scl_fm[:, og, 4:5], bias=bqs_s[:, og:og + 1])
        m = sS.inc(i)
        wge(e, sS, m)
        i = e.dma_start(out=qt_sp.ap()[og, :, tg * TGW:(tg + 1) * TGW],
                        in_=stg[:, tg % 2, :])
        dSC.inc(i, 16)
        if tg == 0 and og == DC - 1:
            q_sp_tg0.append(dSC.n)
    q_tm = fm_projection(wqt, q_evac, pre_w=q_w0, pre_x0=q_x0[0])

    HC = DC // 2
    arena_h = arena.ap()[:, 0:2 * HC * EW].rearrange("p (b c w) -> p b c w", b=2, c=HC)
    jh_G, jh_V = {}, {}
    units_jh = [(j, h) for h in range(2) for j in range(NR)]
    for ui, (j, h) in enumerate(units_jh):
        pb = ui % 2
        @blk.gpsimd
        def _(e, j=j, h=h, ui=ui, pb=pb):
            if ui == 0:
                wge(e, sCC, 2)
                wge(e, sV)
            if ui == 1:
                wge(e, sT, p2g_T)
            if ui == NR:
                wge(e, sCC, 3)
            if ui >= 2:
                wge(e, sV, jh_V[ui - 2])
            cco = cc2a_out if h == 0 else cc2b_out
            i = e.dma_start(
                out=arena_h[:, pb, :, 0:EW],
                in_=cco.ap()[j * HROWS:(j + 1) * HROWS, :]
                .rearrange("(c p) e -> p c e", p=128))
            jh_G[ui] = dGP.inc(i, 16)
        @blk.vector
        def _(e, j=j, h=h, ui=ui, pb=pb):
            wge(e, dGP, jh_G[ui])
            wge(e, sV, scales_V)
            for c in range(HC):
                dc = h * HC + c
                i = e.scalar_tensor_tensor(ptl[:, dc, :], arena_h[:, pb, c, 0:EW],
                                           scl_fm[:, dc, j:j + 1], ptl[:, dc, :],
                                           ALU.mult, ALU.add)
            jh_V[ui] = sV.inc(i)

    p3_T_end, p3_V_end, p3_S_end, p3_den, y_dma, qlb_dma = [], [], [], [], [], []
    p3_xl = {}
    for cq in range(NT):
        @blk.scalar
        def _(e, cq=cq):
            if cq < 2:
                wge(e, dSC, q_sp_tg0[0])
                wge(e, sT, q_tm[(0, NTG - 1)])
            elif cq < 4:
                wge(e, dSC)
            else:
                wge(e, sT, p3_T_end[cq - 4])
            i = e.dma_start(out=qtb_s(cq % 4)[:, :, :],
                            in_=qt_sp.ap()[:, :, cq * 128:(cq + 1) * 128]
                            .rearrange("c p t -> p c t"))
            dSC.inc(i, 16)

        @blk.gpsimd
        def _(e, cq=cq):
            if cq < 2:
                wge(e, dSC, q_sp_tg0[0])
                wge(e, sT, q_tm[(0, NTG - 1)])
            elif cq < 4:
                wge(e, dSC)
            else:
                wge(e, sT, p3_T_end[cq - 4])
            i = e.dma_start(out=qlb_s(cq % 4)[:, :, :],
                            in_=ql_sp.ap()[:, :, cq * 128:(cq + 1) * 128]
                            .rearrange("c p t -> p c t"))
            qlb_dma.append(dGP.inc(i, 16))

        @blk.sync
        def _(e, cq=cq):
            if cq >= 2:
                wge(e, sV, p3_den[cq - 2])
            i = e.dma_start(out=xstr[:, cq % 2, :],
                            in_=x_in.ap()[cq * 128:(cq + 1) * 128, :])
            p3_xl[cq] = dXL.inc(i, 16)

        sc_T, sc_V, use_T = {}, {}, {}

        def emit_score(ck):
            psl = (sc_ps0 if ck % 2 == 0 else sc_ps1)[:, :]
            @blk.tensor
            def _(e, cq=cq, ck=ck, psl=psl):
                if ck == 0:
                    wge(e, dSC)
                if ck >= 2:
                    wge(e, sV, sc_V[ck - 2])
                for dc in range(DC):
                    i = e.matmul(psl, wk_fm[:, dc, ck * 128:(ck + 1) * 128],
                                 qtb_s(cq % 4)[:, dc, :], start=(dc == 0), stop=(dc == DC - 1))
                sc_T[ck] = sT.inc(i)
            @blk.vector
            def _(e, cq=cq, ck=ck, psl=psl):
                wge(e, sT, sc_T[ck])
                if ck >= 4:
                    wge(e, sT, use_T[ck - 4])
                if ck == cq:
                    i = e.tensor_mul(scb[:, ck % 4, :], psl, triu_s[:, :])
                else:
                    i = e.tensor_copy(scb[:, ck % 4, :], psl)
                sc_V[ck] = sV.inc(i)

        emit_score(0)
        if cq >= 1:
            emit_score(1)
        @blk.tensor
        def _(e, cq=cq):
            if cq > 0:
                wge(e, sV, p3_den[cq - 1])
                wge(e, sS, p3_S_end[cq - 1])
            else:
                wge(e, sV)
            for dc in range(DC):
                for (a, b) in PCS:
                    e.matmul(num_ps[:, a:b], qtb_s(cq % 4)[:, dc, :], ptl[:, dc, a:b],
                             start=(dc == 0), stop=False)
                i = e.matmul(den_ps[:, 0:4], qtb_s(cq % 4)[:, dc, :], ptl[:, dc, D:EW],
                             start=(dc == 0), stop=False)
            sT.inc(i)
        for ck in range(cq + 1):
            if ck + 2 <= cq:
                emit_score(ck + 2)
            @blk.tensor
            def _(e, cq=cq, ck=ck):
                wge(e, sV, sc_V[ck])
                last = (ck == cq)
                for (a, b) in PCS:
                    e.matmul(num_ps[:, a:b], scb[:, ck % 4, :], v_aug[:, ck, a:b],
                             start=False, stop=(last and b == D))
                i = e.matmul(den_ps[:, 0:4], scb[:, ck % 4, :], v_aug[:, ck, D:EW],
                             start=False, stop=last)
                use_T[ck] = sT.inc(i)
        @blk.tensor
        def _(e, cq=cq):
            wge(e, dGP, qlb_dma[cq])
            for dc in range(DC):
                for (a, b) in PCS:
                    i = e.matmul(ffn_ps[:, a:b], qlb_s(cq % 4)[:, dc, :], qrhs[:, dc, a:b],
                                 start=(dc == 0), stop=(dc == DC - 1))
            p3_T_end.append(sT.inc(i))

        @blk.vector
        def _(e, cq=cq):
            wge(e, sT, p3_T_end[cq])
            wge(e, dXL, p3_xl[cq])
            i = e.tensor_scalar_add(sm2[:, :], den_ps[:, 0:1], 1e-6)
            wge(e, sV, sV.inc(i))
            i = e.reciprocal(sm2[:, :], sm2[:, :])
            wge(e, sV, sV.inc(i))
            if cq > 0:
                wge(e, dSY, y_dma[cq - 1])
            i = e.scalar_tensor_tensor(outb[:, 0, :], num_ps[:, :], sm2[:, 0:1],
                                       xstr[:, cq % 2, :], ALU.mult, ALU.add)
            p3_den.append(sV.inc(i))

        @blk.scalar
        def _(e, cq=cq):
            wge(e, sT, p3_T_end[cq])
            i = e.activation(ub[:, 0, :], ffn_ps[:, :], AF.Copy, scale=lsc[:, cq:cq + 1])
            p3_S_end.append(sS.inc(i))

        @blk.vector
        def _(e, cq=cq):
            wge(e, sS, p3_S_end[cq])
            e.tensor_mul(ub[:, 1, :], ub[:, 0, :], rs_bc[:, :])
            i = e.tensor_add(outb[:, 0, :], outb[:, 0, :], ub[:, 1, :])
            p3_V_end.append(sV.inc(i))

        @blk.sync
        def _(e, cq=cq):
            wge(e, sV, p3_V_end[cq])
            i = e.dma_start(out=y_out.ap()[cq * 128:(cq + 1) * 128, :],
                            in_=outb[:, 0, :])
            y_dma.append(dSY.inc(i, 16))

    @blk.sync
    def _(e):
        wge(e, dSY)

    ctx.close()
    _merge_engine_blocks(nc)
    return nc


_CACHE = {}


def _get_nc(T, D, NR, NCORES):
    key = (T, D, NR, NCORES)
    if key not in _CACHE:
        _CACHE[key] = build_nc(T, D, NR, NCORES)
    return _CACHE[key]


def make_in_maps(x2d, W_q, b_q, W_k, b_k, W_v, b_v, W_a, b_a, rhs,
                 n_cores=8, NR=4):
    import ml_dtypes
    bf16 = ml_dtypes.bfloat16
    TT, D = x2d.shape
    T = TT // n_cores
    DC = D // 128
    f = np.float32
    shared = dict(
        wqt=np.ascontiguousarray(np.asarray(W_q, dtype=f).T),
        wkt=np.ascontiguousarray(np.asarray(W_k, dtype=f).T),
        wvt=np.ascontiguousarray(np.asarray(W_v, dtype=f).T),
        wat=np.ascontiguousarray(np.asarray(W_a, dtype=f).T),
        rhst=np.ascontiguousarray(np.asarray(rhs, dtype=f).T),
        bq_fm=np.ascontiguousarray(np.asarray(b_q, dtype=f).reshape(DC, 128).T),
        bk_fm=np.ascontiguousarray(np.asarray(b_k, dtype=f).reshape(DC, 128).T),
        ba_row=np.ascontiguousarray(np.asarray(b_a, dtype=f).reshape(1, D)),
        bv_row=np.ascontiguousarray(np.asarray(b_v, dtype=f).reshape(1, D)),
        tri_lo=np.tril(np.ones((128, 128))).astype(bf16),
        triu_m=np.triu(np.ones((128, 128))).astype(f),
        ident_f=np.eye(128, dtype=f),
        ident_b=np.eye(128).astype(bf16),
    )
    in_maps = []
    for c in range(n_cores):
        R = c % NR
        pm = np.zeros((5, 5), dtype=f)
        for kp in range(4):
            for j in range(4):
                if j < kp <= R:
                    pm[kp, j] = 1.0
            pm[kp, 4] = 1.0 if kp > R else 0.0
        for j in range(4):
            if j >= R:
                pm[4, j] = BIG_NEG
        m = dict(shared)
        m["x"] = np.ascontiguousarray(x2d[c * T:(c + 1) * T], dtype=f)
        m["pm"] = pm
        in_maps.append(m)
    return in_maps


def kernel(L0_attn_x_seq, L0_attn_W_q, L0_attn_b_q, L0_attn_W_k, L0_attn_b_k,
           L0_attn_W_v, L0_attn_b_v, L0_attn_W_a, L0_attn_b_a, L0_ffn_rhs):
    B, Tseq, D = L0_attn_x_seq.shape
    n_cores = 8
    NR = n_cores // B
    x2d = np.asarray(L0_attn_x_seq, dtype=np.float32).reshape(B * Tseq, D)
    T = (B * Tseq) // n_cores
    nc = _get_nc(T, D, NR, n_cores)
    in_maps = make_in_maps(
        x2d,
        L0_attn_W_q, L0_attn_b_q, L0_attn_W_k, L0_attn_b_k,
        L0_attn_W_v, L0_attn_b_v, L0_attn_W_a, L0_attn_b_a,
        L0_ffn_rhs, n_cores=n_cores, NR=NR)
    res = run_bass_kernel_spmd(nc, in_maps, core_ids=list(range(n_cores)))
    y = np.concatenate([r["y"] for r in res.results], axis=0)
    return np.ascontiguousarray(y.reshape(B, Tseq, D), dtype=np.float32)


# version 39
# speedup vs baseline: 1.0170x; 1.0170x over previous
import numpy as np
from contextlib import ExitStack

import concourse.bass as bass
import concourse.mybir as mybir
from concourse.bass_utils import run_bass_kernel_spmd

F32 = mybir.dt.float32
BF16 = mybir.dt.bfloat16
I32 = mybir.dt.int32
AF = mybir.ActivationFunctionType
ALU = mybir.AluOpType
AX = mybir.AxisListType

BIG_NEG = -1.0e9



def _merge_engine_blocks(nc):
    f = nc.m.functions[0]
    blocks = list(f.blocks)
    if len(blocks) <= 3:
        return
    by_name = {b.name: b for b in blocks}
    head, tail = blocks[0], blocks[-1]
    heads = [i.target for i in head.instructions
             if type(i).__name__ == "InstUnconditionalBranch"]
    new_blocks = [head]
    consumed = set()
    for h in heads:
        merged, cur, last_branch = [], h, None
        first = by_name[h]
        while cur != tail.name:
            b = by_name[cur]
            consumed.add(cur)
            nxt = None
            for i in b.instructions:
                if type(i).__name__ == "InstUnconditionalBranch":
                    nxt = i.target
                    last_branch = i
                else:
                    merged.append(i)
            cur = nxt
        merged.append(last_branch)
        first.instructions = merged
        new_blocks.append(first)
    for b in blocks[1:-1]:
        if b.name not in consumed and b not in new_blocks:
            new_blocks.append(b)
    new_blocks.append(tail)
    f.blocks = new_blocks


class Sem:
    def __init__(self, nc, ctx, name):
        self.h = ctx.enter_context(nc.semaphore(name))
        self.n = 0

    def inc(self, inst, k=1):
        inst.then_inc(self.h, k)
        self.n += k
        return self.n


def _pieces(D, cap=512):
    out, o = [], 0
    while o < D:
        w = min(cap, D - o)
        out.append((o, o + w))
        o += w
    return out


def build_nc(T=2048, D=1024, NR=4, NCORES=8):
    DC = D // 128
    NT = T // 128
    EW = D + 4
    TGW = min(512, T, D)
    NTG = T // TGW
    NH = max(1, D // TGW)
    WW = D // NH
    PCS = _pieces(D)

    nc = bass.Bass("TRN2", target_bir_lowering=False, detect_race_conditions=False)
    dp = nc.declare_dram_parameter

    x_in = dp("x", [T, D], F32, isOutput=False)
    wqt = dp("wqt", [D, D], F32, isOutput=False)
    wkt = dp("wkt", [D, D], F32, isOutput=False)
    wvt = dp("wvt", [D, D], F32, isOutput=False)
    wat = dp("wat", [D, D], F32, isOutput=False)
    rhst = dp("rhst", [D, D], F32, isOutput=False)
    bq_fm = dp("bq_fm", [128, DC], F32, isOutput=False)
    bk_fm = dp("bk_fm", [128, DC], F32, isOutput=False)
    ba_row = dp("ba_row", [1, D], F32, isOutput=False)
    bv_row = dp("bv_row", [1, D], F32, isOutput=False)
    tri_lo = dp("tri_lo", [128, 128], BF16, isOutput=False)
    triu_m = dp("triu_m", [128, 128], F32, isOutput=False)
    ident_f = dp("ident_f", [128, 128], F32, isOutput=False)
    ident_b = dp("ident_b", [128, 128], BF16, isOutput=False)
    pm_in = dp("pm", [5, 5], F32, isOutput=False)
    y_out = dp("y", [T, D], F32, isOutput=True)

    xn_sp = nc.dram_tensor("xn_sp", [DC, 128, T], BF16)
    qt_sp = nc.dram_tensor("qt_sp", [DC, 128, T], BF16)
    ql_sp = nc.dram_tensor("ql_sp", [DC, 128, T], BF16)
    rsc_dr = nc.dram_tensor("rsc_dr", [1, D], F32)
    cc1_in = nc.dram_tensor("cc1_in", [1, D], F32)
    cc1_out = nc.dram_tensor("cc1_out", [NR, D], F32)
    cc2_in = nc.dram_tensor("cc2_in", [DC * 128, EW], BF16)
    HROWS = (DC // 2) * 128
    cc2a_out = nc.dram_tensor("cc2a_out", [NR * HROWS, EW], BF16)
    cc2b_out = nc.dram_tensor("cc2b_out", [NR * HROWS, EW], BF16)

    ctx = ExitStack()
    sb = lambda name, shape, dt: ctx.enter_context(nc.sbuf_tensor(name, shape, dt))
    pst = lambda name, shape, dt=F32: ctx.enter_context(nc.psum_tensor(name, shape, dt))

    wk_fm = sb("wk_fm", [128, DC, T], BF16)
    v_aug = sb("v_aug", [128, NT, EW], BF16)
    qrhs = sb("qrhs", [128, DC, D], BF16)
    ptl = sb("ptl", [128, DC, EW], BF16)
    arena = sb("arena", [128, max(NT * D, DC * EW)], BF16)
    wwt = sb("wwt", [128, 2, DC, WW], BF16)
    xstr = sb("xstr", [128, 2, D], F32)
    outb = sb("outb", [128, 1, D], F32)
    xnb = sb("xnb", [128, 2, DC, TGW], BF16)
    etl = sb("etl", [128, DC, NT], F32)
    stg2 = sb("stg2", [128, 2, DC * 128], BF16)
    ub = sb("ub", [128, 2, D], F32)
    cb = sb("cb", [128, 2, D], F32)
    zi_t = sb("zi_t", [128, D], I32)
    qtmp = sb("qtmp", [128, D], BF16)
    qtmp2 = sb("qtmp2", [128, D], BF16)
    tri_s = sb("tri_s", [128, 128], BF16)
    triu_s = sb("triu_s", [128, 128], F32)
    idf_s = sb("idf_s", [5, 5], F32)
    idb_s = sb("idb_s", [128, 128], BF16)
    ones_c = sb("ones_c", [128, 1], BF16)
    ones_rb = sb("ones_rb", [1, 128], BF16)
    bqf_s = sb("bqf_s", [128, DC], F32)
    bkf_s = sb("bkf_s", [128, DC], F32)
    bqs_s = sb("bqs_s", [128, DC], F32)
    totals = sb("totals", [128, DC, NT], F32)
    tails = sb("tails", [128, DC, NT], F32)
    ctd_s = sb("ctd_s", [128, DC], F32)
    pm_s = sb("pm_s", [5, 5], F32)
    scl_fm = sb("scl_fm", [128, DC, 5], F32)
    lsc = sb("lsc", [128, NT], F32)
    rsc_fm = sb("rsc_fm", [128, DC], F32)
    rs_bc = sb("rs_bc", [128, D], F32)
    sm1 = sb("sm1", [128, 8, 1], F32)
    sm2 = sb("sm2", [128, 1], F32)
    sm3 = sb("sm3", [128, 1], F32)
    sm4 = sb("sm4", [128, 2], F32)
    smr = sb("smr", [128, 2], F32)
    c1em5 = sb("c1em5", [128, 1], F32)
    c_invD3 = sb("c_invD3", [128, 1], F32)
    c1em6 = sb("c1em6", [128, 1], F32)
    ct_aug = ub.ap()[0:5, 0, :]
    stg2f = stg2.ap().rearrange("p b n -> p (b n)")
    if 2 * DC * 128 >= 2048 and DC * 128 >= TGW:
        stg = stg2.ap()[:, :, 0:TGW]
        msb = stg2f[:, 0:1040].rearrange("p (b n) -> p b n", b=2)
        gsc = stg2f[:, 0:2048].rearrange("p (b n) -> p b n", b=2)
        scb = stg2f[:, 0:512].rearrange("p (b n) -> p b n", b=4)
    else:
        stg = sb("stg_t", [128, 2, TGW], BF16).ap()[:, :, :]
        msb = sb("msb_t", [128, 2, 520], BF16).ap()[:, :, :]
        gsc = sb("gsc_t", [128, 2, 512], BF16).ap()[:, :, :]
        scb = sb("scb_t", [128, 4, 128], BF16).ap()[:, :, :]
    if WW >= 512:
        qtb_s = lambda s: wwt.ap()[:, s // 2, :, (s % 2) * 128:(s % 2) * 128 + 128]
        qlb_s = lambda s: wwt.ap()[:, s // 2, :, 256 + (s % 2) * 128:256 + (s % 2) * 128 + 128]
    else:
        _qtb_t = sb("qtb_t", [128, 4, DC, 128], BF16)
        _qlb_t = sb("qlb_t", [128, 4, DC, 128], BF16)
        qtb_s = lambda s: _qtb_t.ap()[:, s, :, :]
        qlb_s = lambda s: _qlb_t.ap()[:, s, :, :]
    sc_exp = ub.ap()[0:5, 1, :]
    rs_row = ub.ap()[0:1, 1, :]
    b_row = qtmp.ap()[0:1, :]

    num_ps = pst("num_ps", [128, D], F32)
    ffn_ps = pst("ffn_ps", [128, D], F32)
    den_ps = pst("den_ps", [128, 128], F32)
    sc_ps0 = pst("sc_ps0", [128, 128], F32)
    sc_ps1 = pst("sc_ps1", [128, 128], F32)
    tp_ps = pst("tp_ps", [128, DC * 128], BF16)

    dSY = Sem(nc, ctx, "dSY")
    dXL = Sem(nc, ctx, "dXL")
    dSC = Sem(nc, ctx, "dSC")
    dGP = Sem(nc, ctx, "dGP")
    sCC = Sem(nc, ctx, "sCC")
    sT = Sem(nc, ctx, "sT")
    sV = Sem(nc, ctx, "sV")
    sS = Sem(nc, ctx, "sS")
    sG = Sem(nc, ctx, "sG")

    blk = ctx.enter_context(nc.Block())

    def wge(e, sem, n=None):
        v = sem.n if n is None else n
        if v > 0:
            e.wait_ge(sem.h, v)

    @blk.sync
    def _(e):
        for dst, src in ((tri_s, tri_lo), (triu_s, triu_m), (idb_s, ident_b), (bqf_s, bq_fm), (bkf_s, bk_fm),
                         (pm_s, pm_in)):
            i = e.dma_start(out=dst[:, :], in_=src.ap())
            dSY.inc(i, 16)
        i = e.dma_start(out=idf_s[:, :], in_=ident_f.ap()[0:5, 0:5])
        dSY.inc(i, 16)

    @blk.gpsimd
    def _(e):
        for t_, val in ((ones_c, 1.0), (ones_rb, 1.0)):
            i = e.memset(t_[:, :], val)
            sG.inc(i)
        i = e.memset(v_aug[:, :, :], 0.0)
        sG.inc(i)
        i = e.memset(ptl[:, :, :], 0.0)
        sG.inc(i)
        i = e.memset(c1em5[:, :], float(np.float32(1e-5)) * (2.0 ** 20))
        sG.inc(i)
        i = e.memset(c1em6[:, :], 1e-6)
        sG.inc(i)
        i = e.memset(c_invD3[:, :], 1.0 / float(D))
        sG.inc(i)

    def round_cast(e_s, u_ap, c_ap):
        e_s.copy(zi_t[:, :], u_ap)
        return e_s.copy(c_ap, zi_t[:, :])

    def round_chain(e, u_ap, c_ap, out_ap):
        e.tensor_sub(u_ap, u_ap, c_ap)
        e.scalar_tensor_tensor(c_ap, u_ap, 0.5, c_ap,
                               ALU.is_gt, ALU.add)
        e.tensor_single_scalar(u_ap, u_ap, -0.5, ALU.is_lt)
        return e.tensor_sub(out_ap, c_ap, u_ap)

    def emit_rhs_quant():
        for jc in range(DC):
            @blk.sync
            def _(e, jc=jc):
                wge(e, sV)
                i = e.dma_start(out=ub[:, 1, :], in_=rhst.ap()[jc * 128:(jc + 1) * 128, :])
                dSY.inc(i, 16)

            @blk.vector
            def _(e, jc=jc):
                wge(e, dSY)
                i = e.reduce_max(sm1[:, :, :], ub[:, 1, :].rearrange("p (a b) -> p a b", a=8),
                                 axis=AX.X, apply_absolute_value=True)
                wge(e, sV, sV.inc(i))
                i = e.reduce_max(sm2[:, :], sm1[:, :, 0], axis=AX.X, apply_absolute_value=True)
                wge(e, sV, sV.inc(i))
                i = e.tensor_scalar_max(sm2[:, :], sm2[:, :], 1e-12)
                wge(e, sV, sV.inc(i))
                e.tensor_scalar_mul(rsc_fm[:, jc:jc + 1], sm2[:, :], 1.0 / 127.0)
                i = e.reciprocal(sm3[:, :], sm2[:, :])
                wge(e, sV, sV.inc(i))
                i = e.tensor_scalar_mul(sm3[:, :], sm3[:, :], 127.0)
                sV.inc(i)

            @blk.scalar
            def _(e, jc=jc):
                wge(e, sV)
                e.activation(ub[:, 0, :], ub[:, 1, :], AF.Copy, scale=sm3[:, 0:1])
                i = round_cast(e, ub[:, 0, :], cb[:, 0, :])
                sS.inc(i)

            @blk.vector
            def _(e, jc=jc):
                wge(e, sS)
                i = round_chain(e, ub[:, 0, :], cb[:, 0, :], qtmp[:, :])
                sV.inc(i)

            @blk.tensor
            def _(e, jc=jc):
                wge(e, sV)
                wge(e, dSY)
                if jc > 0:
                    pass
                for dc in range(DC):
                    i = e.transpose(tp_ps[:, dc * 128:(dc + 1) * 128],
                                    qtmp[:, dc * 128:(dc + 1) * 128], idb_s[:, :])
                sT.inc(i)

            @blk.vector
            def _(e, jc=jc):
                wge(e, sT)
                for dc in range(DC):
                    i = e.tensor_copy(qrhs[:, dc, jc * 128:(jc + 1) * 128],
                                      tp_ps[:, dc * 128:(dc + 1) * 128])
                sV.inc(i)


    p1_V = []
    ev_S = {}
    ql_dma, xn_dma = {}, {}
    xcp_mark, sq_mark, xl_mark, sqrt_mark, r_mark = {}, {}, {}, {}, {}
    cast1_mark, G_mark, xn_mark, tql_mark, txn_mark = {}, {}, {}, {}, {}
    G0 = sG.n
    D0 = dSY.n

    def w_load(e, w_dram, h, wb):
        i = e.dma_start(out=wwt[:, wb, :, :],
                        in_=w_dram.ap()[:, h * WW:(h + 1) * WW]
                        .rearrange("(c p) o -> p c o", p=128))
        return dGP.inc(i, 16)

    def x_load(e, tg, xb):
        i = e.dma_start(out=xnb[:, xb, :, :],
                        in_=xn_sp.ap()[:, :, tg * TGW:(tg + 1) * TGW]
                        .rearrange("c p t -> p c t"))
        return dSY.inc(i, 16)

    def k_evac(e, og, tg, psl):
        i = e.activation(wk_fm[:, og, tg * TGW:(tg + 1) * TGW], psl, AF.Relu,
                         bias=bkf_s[:, og:og + 1])
        sS.inc(i)

    k_tm, k_pe, k_w, k_x = {}, {}, {}, {}

    def emit_kproj_w():
        @blk.gpsimd
        def _(e):
            for h in range(NH):
                k_w[h] = w_load(e, wkt, h, h % 2)

    def emit_kproj_x(tg):
        @blk.gpsimd
        def _(e, tg=tg):
            wge(e, dSC, xn_dma[min(NT - 1, (tg + 1) * (TGW // 128) - 1)])
            if tg >= 2:
                wge(e, sT, k_tm[(NH - 1, tg - 2)])
            i = e.dma_start(out=xnb[:, tg % 2, :, :],
                            in_=xn_sp.ap()[:, :, tg * TGW:(tg + 1) * TGW]
                            .rearrange("c p t -> p c t"))
            k_x[tg] = dGP.inc(i, 16)

    def emit_kproj_h(tg, h):
        for oc in range(WW // 128):
            og = h * (WW // 128) + oc
            gi = (tg * NH + h) * (WW // 128) + oc
            psl = (num_ps if gi % 2 == 0 else ffn_ps)[:, 0:TGW]

            @blk.tensor
            def _(e, h=h, tg=tg, oc=oc, psl=psl, gi=gi):
                if oc == 0 and h == 0:
                    wge(e, dGP, k_x[tg])
                if gi >= 2:
                    wge(e, sS, k_pe[gi - 2])
                for dc in range(DC):
                    i = e.matmul(psl, wwt[:, h % 2, dc, oc * 128:(oc + 1) * 128],
                                 xnb[:, tg % 2, dc, :], start=(dc == 0), stop=(dc == DC - 1))
                k_tm[(h, tg)] = sT.inc(i)

            @blk.scalar
            def _(e, og=og, tg=tg, psl=psl, gi=gi, h=h):
                if gi == 0:
                    wge(e, dSY, D0)
                wge(e, sT, k_tm[(h, tg)])
                k_evac(e, og, tg, psl)
                k_pe[gi] = sS.n

    emit_kproj_w()

    TPG = TGW // 128
    k_x_at, k_h_at = {}, {}
    for tg in range(NTG):
        base = (tg + 1) * TPG
        k_x_at.setdefault(base, []).append(tg)
        for h in range(NH):
            k_h_at.setdefault(base + 2 + h, []).append((tg, h))

    def round_v(tc):
        @blk.vector
        def _(e, tc=tc):
            wge(e, sS, cast1_mark[tc])
            if tc > 0:
                wge(e, sT, tql_mark[tc - 1])
            p = tc % 2
            i = round_chain(e, ub[:, p, :], cb[:, p, :], qtmp[:, :])
            sV.inc(i)

    def round_stage(tc):
        @blk.tensor
        def _(e, tc=tc):
            wge(e, sV)
            if tc > 0:
                wge(e, sS, ev_S[(tc - 1, 1)])
            for dc in range(DC):
                i = e.transpose(tp_ps[:, dc * 128:(dc + 1) * 128],
                                qtmp[:, dc * 128:(dc + 1) * 128], idb_s[:, :])
            tql_mark[tc] = sT.inc(i)

        @blk.scalar
        def _(e, tc=tc):
            wge(e, sT)
            if tc > 0:
                wge(e, dSC, ql_dma[tc - 1])
            i = e.copy(stg2[:, 1, :], tp_ps[:, :])
            ev_S[(tc, 0)] = sS.inc(i)
            i = e.dma_start(out=ql_sp.ap()[:, :, tc * 128:(tc + 1) * 128].rearrange("c p t -> p c t"),
                            in_=stg2[:, 1, :].rearrange("p (c t) -> p c t", c=DC))
            ql_dma[tc] = dSC.inc(i, 16)

        @blk.tensor
        def _(e, tc=tc):
            wge(e, sS, ev_S[(tc, 0)])
            for dc in range(DC):
                i = e.transpose(tp_ps[:, dc * 128:(dc + 1) * 128],
                                qtmp2[:, dc * 128:(dc + 1) * 128], idb_s[:, :])
            txn_mark[tc] = sT.inc(i)

        @blk.scalar
        def _(e, tc=tc):
            wge(e, sT)
            if tc > 0:
                wge(e, dSC, xn_dma[tc - 1])
            i = e.copy(stg2[:, 0, :], tp_ps[:, :])
            ev_S[(tc, 1)] = sS.inc(i)
            i = e.dma_start(out=xn_sp.ap()[:, :, tc * 128:(tc + 1) * 128].rearrange("c p t -> p c t"),
                            in_=stg2[:, 0, :].rearrange("p (c t) -> p c t", c=DC))
            xn_dma[tc] = dSC.inc(i, 16)

    for tc in range(NT):
        @blk.sync
        def _(e, tc=tc):
            if tc >= 2:
                wge(e, sS, xn_mark[tc - 2])
            i = e.dma_start(out=xstr[:, tc % 2, :], in_=x_in.ap()[tc * 128:(tc + 1) * 128, :])
            xl_mark[tc] = dXL.inc(i, 16)

        @blk.vector
        def _(e, tc=tc):
            wge(e, dXL, xl_mark[tc])
            xv = xstr[:, tc % 2, :]
            xcv = xv
            i = e.reduce_sum(sm2[:, :], xv, axis=AX.X)
            wge(e, sV, sV.inc(i))
            i = e.tensor_scalar(xcv, xv, float(D), sm2[:, 0:1],
                                ALU.mult, ALU.subtract)
            xcp_mark[tc] = sV.inc(i)
            i = e.reduce_max(sm2[:, :], xcv, axis=AX.X, apply_absolute_value=True)
            wge(e, sV, sV.inc(i))
            i = e.reciprocal(smr[:, tc % 2:tc % 2 + 1], sm2[:, :])
            wge(e, sV, sV.inc(i))
            i = e.tensor_scalar_mul(smr[:, tc % 2:tc % 2 + 1], smr[:, tc % 2:tc % 2 + 1], 127.0)
            wge(e, sV, sV.inc(i))

        @blk.scalar
        def _(e, tc=tc):
            wge(e, sV, xcp_mark[tc])
            wge(e, sG, G0)
            i = e.activation(outb[:, 0, :], xstr[:, tc % 2, :], AF.Square,
                             accum_out=sm3[:, 0:1])
            sq_mark[tc] = sS.inc(i)
            wge(e, sS, sq_mark[tc])
            i = e.activation(sm4[:, tc % 2:tc % 2 + 1], sm3[:, :], AF.Sqrt,
                             bias=c1em5[:, 0:1], scale=c_invD3[:, 0:1])
            sqrt_mark[tc] = sS.inc(i)

        @blk.vector
        def _(e, tc=tc):
            wge(e, sS, sqrt_mark[tc])
            i = e.reciprocal(sm4[:, tc % 2:tc % 2 + 1], sm4[:, tc % 2:tc % 2 + 1])
            r_mark[tc] = sV.inc(i)
            wge(e, sV, r_mark[tc])
            i = e.tensor_scalar(lsc[:, tc:tc + 1], sm2[:, :], sm4[:, tc % 2:tc % 2 + 1],
                                1.0 / 127.0, ALU.mult, ALU.mult)
            sV.inc(i)

        @blk.scalar
        def _(e, tc=tc):
            wge(e, sV, r_mark[tc])
            p = tc % 2
            e.activation(ub[:, p, :], xstr[:, p, :], AF.Copy, scale=smr[:, p:p + 1])
            e.copy(zi_t[:, :], ub[:, p, :])
            i = e.copy(cb[:, p, :], zi_t[:, :])
            cast1_mark[tc] = sS.inc(i)

        if tc > 0:
            round_v(tc - 1)
            round_stage(tc - 1)

        @blk.scalar
        def _(e, tc=tc):
            p = tc % 2
            if tc > 0:
                wge(e, sT, txn_mark[tc - 1])
            i = e.activation(qtmp2[:, :], xstr[:, p, :], AF.Copy, scale=sm4[:, p:p + 1])
            xn_mark[tc] = sS.inc(i)

        for tg in k_x_at.get(tc, []):
            emit_kproj_x(tg)
        for tg, h in k_h_at.get(tc, []):
            emit_kproj_h(tg, h)

    round_v(NT - 1)
    round_stage(NT - 1)
    for t in range(NT, NT + TPG + NH + 3):
        for tg in k_x_at.get(t, []):
            emit_kproj_x(tg)
        for tg, h in k_h_at.get(t, []):
            emit_kproj_h(tg, h)

    def fm_projection(w_dram, evac, head_wait=None, pre_w=None, pre_x0=None):
        units = [(h, tg) for h in range(NH) for tg in range(NTG)]
        t_marks = {}
        pe_marks = {}
        w_marks = {}
        x_marks = {}
        for ui, (h, tg) in enumerate(units):
            wb, xb = h % 2, ui % 2
            if ui == 0:
                if pre_w is not None:
                    for hh in range(NH):
                        w_marks[hh] = pre_w[hh]
                else:
                    @blk.gpsimd
                    def _(e, h=h):
                        wge(e, sT)
                        w_marks[h] = w_load(e, w_dram, h, h % 2)
                if pre_x0 is not None:
                    x_marks[0] = pre_x0
                else:
                    @blk.sync
                    def _(e, tg=tg, xb=xb):
                        wge(e, sT)
                        wge(e, dSC, xn_dma[NT - 1])
                        x_marks[0] = x_load(e, tg, xb)
            if ui + 1 < len(units):
                h2, tg2 = units[ui + 1]
                if h2 != h and pre_w is None:
                    @blk.gpsimd
                    def _(e, h2=h2):
                        if h2 >= 2:
                            wge(e, sT, t_marks[(h2 - 2, NTG - 1)])
                        w_marks[h2] = w_load(e, w_dram, h2, h2 % 2)
                @blk.sync
                def _(e, ui=ui, tg2=tg2, xb2=((ui + 1) % 2)):
                    if ui >= 1:
                        wge(e, sT, t_marks[units[ui - 1]])
                    x_marks[ui + 1] = x_load(e, tg2, xb2)
            for oc in range(WW // 128):
                og = h * (WW // 128) + oc
                psl = (num_ps if oc % 2 == 0 else ffn_ps)[:, 0:TGW]
                @blk.tensor
                def _(e, h=h, tg=tg, oc=oc, og=og, psl=psl, wb=wb, xb=xb, ui=ui):
                    if oc == 0:
                        wge(e, dSY, x_marks[ui])
                        wge(e, dGP, w_marks[h])
                    gi = ui * (WW // 128) + oc
                    if gi == 0:
                        if head_wait is None:
                            wge(e, sS)
                            wge(e, sV)
                        else:
                            head_wait(e)
                    elif gi >= 2:
                        wge(e, sS, pe_marks[gi - 2])
                    for dc in range(DC):
                        i = e.matmul(psl, wwt[:, h % 2, dc, oc * 128:(oc + 1) * 128],
                                     xnb[:, xb, dc, :], start=(dc == 0), stop=(dc == DC - 1))
                    t_marks[(h, tg)] = sT.inc(i)
                @blk.scalar
                def _(e, og=og, tg=tg, psl=psl, ui=ui, oc=oc):
                    wge(e, sT)
                    evac(e, og, tg, psl)
                    pe_marks[ui * (WW // 128) + oc] = sS.n
        return t_marks

    def tm_projection(w_dram, b_row_s, evac, w0_wait=None, x0_wait=None,
                      w1_wait=None, x1_wait=None, pre_w0=None):
        units = [(h, tg) for h in range(NH) for tg in range(NTG)]
        t_marks = {}
        pe_marks = {}
        w_marks = {}
        x_marks = {}
        for ui, (h, tg) in enumerate(units):
            wb, xb = h % 2, ui % 2
            if ui == 0:
                if pre_w0 is not None:
                    w_marks[h] = pre_w0
                else:
                    @blk.gpsimd
                    def _(e, h=h):
                        if w0_wait is not None:
                            wge(e, sT, w0_wait)
                        else:
                            wge(e, sT)
                        w_marks[h] = w_load(e, w_dram, h, h % 2)
                @blk.sync
                def _(e, tg=tg, xb=xb):
                    if x0_wait is not None:
                        wge(e, sT, x0_wait)
                    else:
                        wge(e, sT)
                    wge(e, dSC, xn_dma[NT - 1])
                    x_marks[0] = x_load(e, tg, xb)
            if ui + 1 < len(units):
                h2, tg2 = units[ui + 1]
                if h2 != h:
                    @blk.gpsimd
                    def _(e, h2=h2):
                        if h2 == 1 and w1_wait is not None:
                            wge(e, sT, w1_wait)
                        if h2 >= 2:
                            wge(e, sT, t_marks[(h2 - 2, NTG - 1)])
                        w_marks[h2] = w_load(e, w_dram, h2, h2 % 2)
                @blk.sync
                def _(e, ui=ui, tg2=tg2):
                    if ui == 0 and x1_wait is not None:
                        wge(e, sT, x1_wait)
                    if ui >= 1:
                        wge(e, sT, t_marks[units[ui - 1]])
                    x_marks[ui + 1] = x_load(e, tg2, (ui + 1) % 2)
            for tw in range(TGW // 128):
                tc = tg * (TGW // 128) + tw
                psl = (num_ps if tc % 2 == 0 else ffn_ps)[:, 0:WW]
                @blk.tensor
                def _(e, h=h, tg=tg, tw=tw, psl=psl, wb=wb, xb=xb, ui=ui):
                    if tw == 0:
                        wge(e, dSY, x_marks[ui])
                        wge(e, dGP, w_marks[h])
                        wge(e, sG)
                    wge(e, sS)
                    for dc in range(DC):
                        e.matmul(psl, xnb[:, xb, dc, tw * 128:(tw + 1) * 128],
                                 wwt[:, h % 2, dc, :], start=(dc == 0), stop=False)
                    i = e.matmul(psl, ones_rb[:, :], b_row_s[:, h * WW:(h + 1) * WW],
                                 start=False, stop=True)
                    t_marks[(h, tg)] = sT.inc(i)
                @blk.scalar
                def _(e, tc=tc, h=h, psl=psl):
                    wge(e, sT)
                    evac(e, tc, h, psl)
        return t_marks

    rs_dma = []

    def emit_rs_spill():
        @blk.scalar
        def _(e):
            wge(e, sV)
            i = e.dma_start(out=rsc_dr.ap().rearrange("o (c p) -> p (o c)", p=128),
                            in_=rsc_fm[:, :], allow_slow_non_contiguous=True)
            dSC.inc(i, 16)
            wge(e, dSC)
            i = e.dma_start(out=rs_row, in_=rsc_dr.ap())
            rs_dma.append(dSC.inc(i, 16))

    emit_rhs_quant()
    emit_rs_spill()


    arena_la = arena.ap()[:, 0:NT * D].rearrange("p (c d) -> p c d", c=NT)
    rhs_T = sT.n

    @blk.gpsimd
    def _(e):
        wge(e, sT, rhs_T)
        i = e.dma_start(out=b_row, in_=ba_row.ap())
        dGP.inc(i, 16)

    def a_evac(e, tc, h, psl):
        i = e.activation(arena_la[:, tc, h * WW:(h + 1) * WW], psl, AF.Sigmoid)
        sS.inc(i)
    k_done = k_tm[(NH - 1, NTG - 1)]
    a_tm = tm_projection(wat, b_row, a_evac,
                         w0_wait=k_tm[(0, NTG - 1)],
                         x0_wait=k_tm[(NH - 1, max(0, NTG - 2))],
                         w1_wait=k_done, x1_wait=k_done)

    for tc in range(NT):
        @blk.scalar
        def _(e, tc=tc):
            i = e.activation(arena_la[:, tc, 0:D], arena_la[:, tc, 0:D], AF.Ln,
                             bias=c1em6[:, 0:1])
            sS.inc(i)

    a_done = a_tm[(NH - 1, NTG - 1)]
    v_w0m = []

    @blk.gpsimd
    def _(e):
        wge(e, sT, a_tm[(0, NTG - 1)])
        v_w0m.append(w_load(e, wvt, 0, 0))

    @blk.gpsimd
    def _(e):
        wge(e, sT, a_done)
        i = e.dma_start(out=b_row, in_=bv_row.ap())
        dGP.inc(i, 16)

    def v_evac(e, tc, h, psl):
        i = e.activation(v_aug[:, tc, h * WW:(h + 1) * WW], psl, AF.Copy)
        sS.inc(i)
    tm_projection(wvt, b_row, v_evac, pre_w0=v_w0m[0],
                  x0_wait=a_tm[(NH - 1, max(0, NTG - 2))],
                  w1_wait=a_done, x1_wait=a_done)

    @blk.vector
    def _(e):
        wge(e, sG)
        for tc in range(NT):
            i = e.memset(v_aug[:, tc, D:D + 1], 1.0)
            sV.inc(i)

    q_w0, q_x0 = [], []

    @blk.gpsimd
    def _(e):
        wge(e, sT)
        for h in range(NH):
            q_w0.append(w_load(e, wqt, h, h % 2))

    @blk.sync
    def _(e):
        wge(e, sT)
        q_x0.append(x_load(e, 0, 0))


    @blk.tensor
    def _(e):
        wge(e, sS)
        wge(e, sG)
        for dc in range(DC):
            for tc in range(NT):
                i = e.matmul(den_ps[:, dc * NT + tc:dc * NT + tc + 1],
                             arena_la[:, tc, dc * 128:(dc + 1) * 128], ones_c[:, :],
                             start=True, stop=True)
        sT.inc(i)

    @blk.vector
    def _(e):
        wge(e, sT)
        i = e.tensor_copy(totals[:, :, :],
                          den_ps[:, 0:DC * NT].rearrange("p (c t) -> p c t", c=DC))
        wge(e, sV, sV.inc(i))
        i = e.memset(tails[:, :, NT - 1:NT], 0.0)
        wge(e, sV, sV.inc(i))
        for c in range(NT - 2, -1, -1):
            i = e.tensor_add(tails[:, :, c:c + 1], tails[:, :, c + 1:c + 2],
                             totals[:, :, c + 1:c + 2])
            wge(e, sV, sV.inc(i))
        i = e.tensor_add(ctd_s[:, :], tails[:, :, 0], totals[:, :, 0])
        sV.inc(i)

    @blk.gpsimd
    def _(e):
        wge(e, sV)
        i = e.dma_start(out=cc1_in.ap().rearrange("o (c p) -> p (o c)", p=128),
                        in_=ctd_s[:, :], allow_slow_non_contiguous=True)
        dGP.inc(i, 16)
        wge(e, dGP)
        i = e.collective_compute(
            "AllGather", ALU.bypass,
            replica_groups=[[g * NR + r for r in range(NR)] for g in range(NCORES // NR)],
            ins=[cc1_in.ap().opt()], outs=[cc1_out.ap().opt()])
        sCC.inc(i)

    def emit_rs_bcast():
        @blk.gpsimd
        def _(e):
            i = e.memset(ct_aug, 1.0)
            sG.inc(i)

        @blk.tensor
        def _(e):
            wge(e, dSC, rs_dma[0])
            wge(e, sG)
            for (a, b) in PCS:
                i = e.matmul(num_ps[:, a:b], ub.ap()[0:1, 0, 0:128], rs_row[:, a:b],
                             start=True, stop=True)
            sT.inc(i)

        @blk.vector
        def _(e):
            wge(e, sT)
            i = e.tensor_copy(rs_bc[:, :], num_ps[:, :])
            sV.inc(i)


    @blk.scalar
    def _(e):
        wge(e, sV)
        i = e.activation(etl[:, :, :], tails[:, :, :], AF.Exp)
        sS.inc(i)
    GW = max(g for g in (4, 2, 1) if NT % g == 0 and g * 128 <= D)
    groups = [(dc, tc0) for dc in range(DC) for tc0 in range(0, NT, GW)]
    exp_S = {}
    for gi, (dc, tc0) in enumerate(groups):
        psl = (num_ps if gi % 2 == 0 else ffn_ps)
        @blk.tensor
        def _(e, dc=dc, tc0=tc0, psl=psl, gi=gi):
            if gi < 2:
                wge(e, sS)
            else:
                wge(e, sS, exp_S[gi - 2])
            for tw in range(GW):
                i = e.matmul(psl[:, tw * 128:(tw + 1) * 128],
                             arena_la[:, tc0 + tw, dc * 128:(dc + 1) * 128], tri_s[:, :],
                             start=True, stop=True)
            sT.inc(i)
        @blk.scalar
        def _(e, dc=dc, tc0=tc0, psl=psl, gi=gi):
            wge(e, sT)
            if gi >= 2:
                wge(e, sV)
            i = e.activation(gsc[:, gi % 2, 0:GW * 128], psl[:, 0:GW * 128], AF.Exp)
            exp_S[gi] = sS.inc(i)
        @blk.vector
        def _(e, dc=dc, tc0=tc0, gi=gi):
            wge(e, sS, exp_S[gi])
            for tw in range(GW):
                tc = tc0 + tw
                i = e.scalar_tensor_tensor(
                    wk_fm[:, dc, tc * 128:(tc + 1) * 128],
                    gsc[:, gi % 2, tw * 128:(tw + 1) * 128].rearrange("p a b -> p (a b)") if False else gsc[:, gi % 2, tw * 128:(tw + 1) * 128],
                    etl[:, dc, tc:tc + 1],
                    wk_fm[:, dc, tc * 128:(tc + 1) * 128],
                    ALU.mult, ALU.mult)
            sV.inc(i)

    arena_wt = arena.ap()[:, 0:NT * D].rearrange("p (c d) -> p c d", c=NT)
    for tc in range(NT):
        @blk.tensor
        def _(e, tc=tc):
            wge(e, sV)
            for dc in range(DC):
                i = e.transpose(tp_ps[:, dc * 128:(dc + 1) * 128],
                                wk_fm[:, dc, tc * 128:(tc + 1) * 128], idb_s[:, :])
            sT.inc(i)

        @blk.vector
        def _(e, tc=tc):
            wge(e, sT)
            i = e.tensor_copy(arena_wt[:, tc, :], tp_ps[:, :])
            sV.inc(i)

    m_pieces = PCS + [(D, EW)]
    m_units = [(dc, a, b) for dc in range(DC) for (a, b) in m_pieces]
    mV, mG = {}, {}
    for mi, (dc, a, b) in enumerate(m_units):
        w = b - a
        pb = mi % 2
        mps = (num_ps if pb == 0 else ffn_ps)
        @blk.tensor
        def _(e, dc=dc, a=a, b=b, w=w, mi=mi, pb=pb, mps=mps):
            if mi == 0:
                wge(e, sV)
                wge(e, sS)
            if mi >= 2:
                wge(e, sV, mV[mi - 2])
            for tc in range(NT):
                i = e.matmul(mps[:, 0:w],
                             arena_wt[:, tc, dc * 128:(dc + 1) * 128],
                             v_aug[:, tc, a:b], start=(tc == 0), stop=(tc == NT - 1))
            sT.inc(i)

        @blk.vector
        def _(e, w=w, mi=mi, pb=pb, mps=mps):
            wge(e, sT)
            if mi >= 2:
                wge(e, dGP, mG[mi - 2])
            i = e.tensor_copy(msb[:, pb, 0:w], mps[:, 0:w])
            mV[mi] = sV.inc(i)

        @blk.gpsimd
        def _(e, dc=dc, a=a, b=b, w=w, mi=mi, pb=pb):
            wge(e, sV, mV[mi])
            i = e.dma_start(out=cc2_in.ap()[dc * 128:(dc + 1) * 128, a:b],
                            in_=msb[:, pb, 0:w])
            mG[mi] = dGP.inc(i, 16)
            if mi == DC // 2 * len(m_pieces) - 1:
                wge(e, dGP)
                i = e.collective_compute(
                    "AllGather", ALU.bypass,
                    replica_groups=[[g * NR + r for r in range(NR)]
                                    for g in range(NCORES // NR)],
                    ins=[cc2_in.ap()[0:HROWS, :].opt()], outs=[cc2a_out.ap().opt()])
                sCC.inc(i)

    p2g_T = sT.n

    emit_rs_bcast()

    @blk.gpsimd
    def _(e):
        wge(e, dGP, mG[DC * len(m_pieces) - 1])
        i = e.collective_compute(
            "AllGather", ALU.bypass,
            replica_groups=[[g * NR + r for r in range(NR)] for g in range(NCORES // NR)],
            ins=[cc2_in.ap()[HROWS:2 * HROWS, :].opt()], outs=[cc2b_out.ap().opt()])
        sCC.inc(i)


    @blk.sync
    def _(e):
        wge(e, sCC, 1)
        wge(e, sG)
        wge(e, sV)
        i = e.dma_start(out=ct_aug[0:NR, :], in_=cc1_out.ap())
        dSY.inc(i, 16)

    @blk.tensor
    def _(e):
        wge(e, dSY)
        wge(e, sG)
        wge(e, sS)
        wge(e, sV)
        for (a, b) in PCS:
            i = e.matmul(num_ps[0:5, a:b], pm_s[:, :], ct_aug[:, a:b],
                         start=True, stop=True)
        sT.inc(i)

    @blk.scalar
    def _(e):
        wge(e, sT)
        i = e.activation(sc_exp, num_ps[0:5, :], AF.Exp)
        sS.inc(i)

    @blk.tensor
    def _(e):
        wge(e, sS)
        for dc in range(DC):
            i = e.transpose(sc_ps0[:, dc * 8:dc * 8 + 5],
                            sc_exp[:, dc * 128:(dc + 1) * 128], idf_s[:, :])
        sT.inc(i)

    @blk.vector
    def _(e):
        wge(e, sT)
        for dc in range(DC):
            i = e.tensor_copy(scl_fm[:, dc, :], sc_ps0[:, dc * 8:dc * 8 + 5])
        wge(e, sV, sV.inc(i))
        i = e.tensor_mul(bqs_s[:, :], bqf_s[:, :], scl_fm[:, :, 4])
        sV.inc(i)
    scales_V = sV.n

    q_sp_tg0 = []

    def q_evac(e, og, tg, psl):
        wge(e, dSC, max(0, dSC.n - 16))
        wge(e, sV, scales_V)
        i = e.activation(stg[:, tg % 2, :], psl, AF.Relu,
                         scale=scl_fm[:, og, 4:5], bias=bqs_s[:, og:og + 1])
        m = sS.inc(i)
        wge(e, sS, m)
        i = e.dma_start(out=qt_sp.ap()[og, :, tg * TGW:(tg + 1) * TGW],
                        in_=stg[:, tg % 2, :])
        dSC.inc(i, 16)
        if tg == 0 and og == DC - 1:
            q_sp_tg0.append(dSC.n)
    q_tm = fm_projection(wqt, q_evac, pre_w=q_w0, pre_x0=q_x0[0])

    HC = DC // 2
    arena_h = arena.ap()[:, 0:2 * HC * EW].rearrange("p (b c w) -> p b c w", b=2, c=HC)
    jh_G, jh_V = {}, {}
    units_jh = [(j, h) for h in range(2) for j in range(NR)]
    for ui, (j, h) in enumerate(units_jh):
        pb = ui % 2
        @blk.gpsimd
        def _(e, j=j, h=h, ui=ui, pb=pb):
            if ui == 0:
                wge(e, sCC, 2)
                wge(e, sV)
            if ui == 1:
                wge(e, sT, p2g_T)
            if ui == NR:
                wge(e, sCC, 3)
            if ui >= 2:
                wge(e, sV, jh_V[ui - 2])
            cco = cc2a_out if h == 0 else cc2b_out
            i = e.dma_start(
                out=arena_h[:, pb, :, 0:EW],
                in_=cco.ap()[j * HROWS:(j + 1) * HROWS, :]
                .rearrange("(c p) e -> p c e", p=128))
            jh_G[ui] = dGP.inc(i, 16)
        @blk.vector
        def _(e, j=j, h=h, ui=ui, pb=pb):
            wge(e, dGP, jh_G[ui])
            wge(e, sV, scales_V)
            for c in range(HC):
                dc = h * HC + c
                i = e.scalar_tensor_tensor(ptl[:, dc, :], arena_h[:, pb, c, 0:EW],
                                           scl_fm[:, dc, j:j + 1], ptl[:, dc, :],
                                           ALU.mult, ALU.add)
            jh_V[ui] = sV.inc(i)

    p3_T_end, p3_V_end, p3_S_end, p3_den, y_dma, qlb_dma = [], [], [], [], [], []
    p3_xl = {}
    for cq in range(NT):
        @blk.scalar
        def _(e, cq=cq):
            if cq < 2:
                wge(e, dSC, q_sp_tg0[0])
                wge(e, sT, q_tm[(0, NTG - 1)])
            elif cq < 4:
                wge(e, dSC)
            else:
                wge(e, sT, p3_T_end[cq - 4])
            i = e.dma_start(out=qtb_s(cq % 4)[:, :, :],
                            in_=qt_sp.ap()[:, :, cq * 128:(cq + 1) * 128]
                            .rearrange("c p t -> p c t"))
            dSC.inc(i, 16)

        @blk.gpsimd
        def _(e, cq=cq):
            if cq < 2:
                wge(e, dSC, q_sp_tg0[0])
                wge(e, sT, q_tm[(0, NTG - 1)])
            elif cq < 4:
                wge(e, dSC)
            else:
                wge(e, sT, p3_T_end[cq - 4])
            i = e.dma_start(out=qlb_s(cq % 4)[:, :, :],
                            in_=ql_sp.ap()[:, :, cq * 128:(cq + 1) * 128]
                            .rearrange("c p t -> p c t"))
            qlb_dma.append(dGP.inc(i, 16))

        @blk.sync
        def _(e, cq=cq):
            if cq >= 2:
                wge(e, sV, p3_den[cq - 2])
            i = e.dma_start(out=xstr[:, cq % 2, :],
                            in_=x_in.ap()[cq * 128:(cq + 1) * 128, :])
            p3_xl[cq] = dXL.inc(i, 16)

        sc_T, sc_V, use_T = {}, {}, {}

        def emit_score(ck):
            psl = (sc_ps0 if ck % 2 == 0 else sc_ps1)[:, :]
            @blk.tensor
            def _(e, cq=cq, ck=ck, psl=psl):
                if ck == 0:
                    wge(e, dSC)
                if ck >= 2:
                    wge(e, sV, sc_V[ck - 2])
                for dc in range(DC):
                    i = e.matmul(psl, wk_fm[:, dc, ck * 128:(ck + 1) * 128],
                                 qtb_s(cq % 4)[:, dc, :], start=(dc == 0), stop=(dc == DC - 1))
                sc_T[ck] = sT.inc(i)
            @blk.vector
            def _(e, cq=cq, ck=ck, psl=psl):
                wge(e, sT, sc_T[ck])
                if ck >= 4:
                    wge(e, sT, use_T[ck - 4])
                if ck == cq:
                    i = e.tensor_mul(scb[:, ck % 4, :], psl, triu_s[:, :])
                else:
                    i = e.tensor_copy(scb[:, ck % 4, :], psl)
                sc_V[ck] = sV.inc(i)

        emit_score(0)
        if cq >= 1:
            emit_score(1)
        @blk.tensor
        def _(e, cq=cq):
            if cq > 0:
                wge(e, sV, p3_den[cq - 1])
                wge(e, sS, p3_S_end[cq - 1])
            else:
                wge(e, sV)
            for dc in range(DC):
                for (a, b) in PCS:
                    e.matmul(num_ps[:, a:b], qtb_s(cq % 4)[:, dc, :], ptl[:, dc, a:b],
                             start=(dc == 0), stop=False)
                i = e.matmul(den_ps[:, 0:4], qtb_s(cq % 4)[:, dc, :], ptl[:, dc, D:EW],
                             start=(dc == 0), stop=False)
            sT.inc(i)
        for ck in range(cq + 1):
            if ck + 2 <= cq:
                emit_score(ck + 2)
            @blk.tensor
            def _(e, cq=cq, ck=ck):
                wge(e, sV, sc_V[ck])
                last = (ck == cq)
                for (a, b) in PCS:
                    e.matmul(num_ps[:, a:b], scb[:, ck % 4, :], v_aug[:, ck, a:b],
                             start=False, stop=(last and b == D))
                i = e.matmul(den_ps[:, 0:4], scb[:, ck % 4, :], v_aug[:, ck, D:EW],
                             start=False, stop=last)
                use_T[ck] = sT.inc(i)
        @blk.tensor
        def _(e, cq=cq):
            wge(e, dGP, qlb_dma[cq])
            for dc in range(DC):
                for (a, b) in PCS:
                    i = e.matmul(ffn_ps[:, a:b], qlb_s(cq % 4)[:, dc, :], qrhs[:, dc, a:b],
                                 start=(dc == 0), stop=(dc == DC - 1))
            p3_T_end.append(sT.inc(i))

        @blk.vector
        def _(e, cq=cq):
            wge(e, sT, p3_T_end[cq])
            wge(e, dXL, p3_xl[cq])
            i = e.tensor_scalar_add(sm2[:, :], den_ps[:, 0:1], 1e-6)
            wge(e, sV, sV.inc(i))
            i = e.reciprocal(sm2[:, :], sm2[:, :])
            wge(e, sV, sV.inc(i))
            if cq > 0:
                wge(e, dSY, y_dma[cq - 1])
            i = e.scalar_tensor_tensor(outb[:, 0, :], num_ps[:, :], sm2[:, 0:1],
                                       xstr[:, cq % 2, :], ALU.mult, ALU.add)
            p3_den.append(sV.inc(i))

        @blk.scalar
        def _(e, cq=cq):
            wge(e, sT, p3_T_end[cq])
            i = e.activation(ub[:, 0, :], ffn_ps[:, :], AF.Copy, scale=lsc[:, cq:cq + 1])
            p3_S_end.append(sS.inc(i))

        @blk.vector
        def _(e, cq=cq):
            wge(e, sS, p3_S_end[cq])
            e.tensor_mul(ub[:, 1, :], ub[:, 0, :], rs_bc[:, :])
            i = e.tensor_add(outb[:, 0, :], outb[:, 0, :], ub[:, 1, :])
            p3_V_end.append(sV.inc(i))

        @blk.sync
        def _(e, cq=cq):
            wge(e, sV, p3_V_end[cq])
            i = e.dma_start(out=y_out.ap()[cq * 128:(cq + 1) * 128, :],
                            in_=outb[:, 0, :])
            y_dma.append(dSY.inc(i, 16))

    @blk.sync
    def _(e):
        wge(e, dSY)

    ctx.close()
    _merge_engine_blocks(nc)
    return nc


_CACHE = {}


def _get_nc(T, D, NR, NCORES):
    key = (T, D, NR, NCORES)
    if key not in _CACHE:
        _CACHE[key] = build_nc(T, D, NR, NCORES)
    return _CACHE[key]


def make_in_maps(x2d, W_q, b_q, W_k, b_k, W_v, b_v, W_a, b_a, rhs,
                 n_cores=8, NR=4):
    import ml_dtypes
    bf16 = ml_dtypes.bfloat16
    TT, D = x2d.shape
    T = TT // n_cores
    DC = D // 128
    f = np.float32
    shared = dict(
        wqt=np.ascontiguousarray(np.asarray(W_q, dtype=f).T),
        wkt=np.ascontiguousarray(np.asarray(W_k, dtype=f).T),
        wvt=np.ascontiguousarray(np.asarray(W_v, dtype=f).T),
        wat=np.ascontiguousarray(np.asarray(W_a, dtype=f).T),
        rhst=np.ascontiguousarray(np.asarray(rhs, dtype=f).T),
        bq_fm=np.ascontiguousarray(np.asarray(b_q, dtype=f).reshape(DC, 128).T),
        bk_fm=np.ascontiguousarray(np.asarray(b_k, dtype=f).reshape(DC, 128).T),
        ba_row=np.ascontiguousarray(np.asarray(b_a, dtype=f).reshape(1, D)),
        bv_row=np.ascontiguousarray(np.asarray(b_v, dtype=f).reshape(1, D)),
        tri_lo=np.tril(np.ones((128, 128))).astype(bf16),
        triu_m=np.triu(np.ones((128, 128))).astype(f),
        ident_f=np.eye(128, dtype=f),
        ident_b=np.eye(128).astype(bf16),
    )
    in_maps = []
    for c in range(n_cores):
        R = c % NR
        pm = np.zeros((5, 5), dtype=f)
        for kp in range(4):
            for j in range(4):
                if j < kp <= R:
                    pm[kp, j] = 1.0
            pm[kp, 4] = 1.0 if kp > R else 0.0
        for j in range(4):
            if j >= R:
                pm[4, j] = BIG_NEG
        m = dict(shared)
        m["x"] = np.ascontiguousarray(x2d[c * T:(c + 1) * T], dtype=f)
        m["pm"] = pm
        in_maps.append(m)
    return in_maps


def kernel(L0_attn_x_seq, L0_attn_W_q, L0_attn_b_q, L0_attn_W_k, L0_attn_b_k,
           L0_attn_W_v, L0_attn_b_v, L0_attn_W_a, L0_attn_b_a, L0_ffn_rhs):
    B, Tseq, D = L0_attn_x_seq.shape
    n_cores = 8
    NR = n_cores // B
    x2d = np.asarray(L0_attn_x_seq, dtype=np.float32).reshape(B * Tseq, D)
    T = (B * Tseq) // n_cores
    nc = _get_nc(T, D, NR, n_cores)
    in_maps = make_in_maps(
        x2d,
        L0_attn_W_q, L0_attn_b_q, L0_attn_W_k, L0_attn_b_k,
        L0_attn_W_v, L0_attn_b_v, L0_attn_W_a, L0_attn_b_a,
        L0_ffn_rhs, n_cores=n_cores, NR=NR)
    res = run_bass_kernel_spmd(nc, in_maps, core_ids=list(range(n_cores)))
    y = np.concatenate([r["y"] for r in res.results], axis=0)
    return np.ascontiguousarray(y.reshape(B, Tseq, D), dtype=np.float32)


# version 40
# speedup vs baseline: 1.0611x; 1.0434x over previous
import numpy as np
from contextlib import ExitStack

import concourse.bass as bass
import concourse.mybir as mybir
from concourse.bass_utils import run_bass_kernel_spmd

F32 = mybir.dt.float32
BF16 = mybir.dt.bfloat16
I32 = mybir.dt.int32
AF = mybir.ActivationFunctionType
ALU = mybir.AluOpType
AX = mybir.AxisListType

BIG_NEG = -1.0e9



def _merge_engine_blocks(nc):
    f = nc.m.functions[0]
    blocks = list(f.blocks)
    if len(blocks) <= 3:
        return
    by_name = {b.name: b for b in blocks}
    head, tail = blocks[0], blocks[-1]
    heads = [i.target for i in head.instructions
             if type(i).__name__ == "InstUnconditionalBranch"]
    new_blocks = [head]
    consumed = set()
    for h in heads:
        merged, cur, last_branch = [], h, None
        first = by_name[h]
        while cur != tail.name:
            b = by_name[cur]
            consumed.add(cur)
            nxt = None
            for i in b.instructions:
                if type(i).__name__ == "InstUnconditionalBranch":
                    nxt = i.target
                    last_branch = i
                else:
                    merged.append(i)
            cur = nxt
        merged.append(last_branch)
        first.instructions = merged
        new_blocks.append(first)
    for b in blocks[1:-1]:
        if b.name not in consumed and b not in new_blocks:
            new_blocks.append(b)
    new_blocks.append(tail)
    f.blocks = new_blocks


class Sem:
    def __init__(self, nc, ctx, name):
        self.h = ctx.enter_context(nc.semaphore(name))
        self.n = 0

    def inc(self, inst, k=1):
        inst.then_inc(self.h, k)
        self.n += k
        return self.n


def _pieces(D, cap=512):
    out, o = [], 0
    while o < D:
        w = min(cap, D - o)
        out.append((o, o + w))
        o += w
    return out


def build_nc(T=2048, D=1024, NR=4, NCORES=8):
    DC = D // 128
    NT = T // 128
    EW = D + 4
    TGW = min(512, T, D)
    NTG = T // TGW
    NH = max(1, D // TGW)
    WW = D // NH
    PCS = _pieces(D)

    nc = bass.Bass("TRN2", target_bir_lowering=False, detect_race_conditions=False)
    dp = nc.declare_dram_parameter

    x_in = dp("x", [T, D], F32, isOutput=False)
    wqt = dp("wqt", [D, D], F32, isOutput=False)
    wkt = dp("wkt", [D, D], F32, isOutput=False)
    wvt = dp("wvt", [D, D], F32, isOutput=False)
    wat = dp("wat", [D, D], F32, isOutput=False)
    rhst = dp("rhst", [D, D], F32, isOutput=False)
    bq_fm = dp("bq_fm", [128, DC], F32, isOutput=False)
    bk_fm = dp("bk_fm", [128, DC], F32, isOutput=False)
    ba_row = dp("ba_row", [1, D], F32, isOutput=False)
    bv_row = dp("bv_row", [1, D], F32, isOutput=False)
    tri_lo = dp("tri_lo", [128, 128], BF16, isOutput=False)
    triu_m = dp("triu_m", [128, 128], F32, isOutput=False)
    ident_f = dp("ident_f", [128, 128], F32, isOutput=False)
    ident_b = dp("ident_b", [128, 128], BF16, isOutput=False)
    pm_in = dp("pm", [5, 5], F32, isOutput=False)
    y_out = dp("y", [T, D], F32, isOutput=True)

    xn_sp = nc.dram_tensor("xn_sp", [DC, 128, T], BF16)
    qt_sp = nc.dram_tensor("qt_sp", [DC, 128, T], BF16)
    ql_sp = nc.dram_tensor("ql_sp", [DC, 128, T], BF16)
    rsc_dr = nc.dram_tensor("rsc_dr", [1, D], F32)
    cc1_in = nc.dram_tensor("cc1_in", [1, D], F32)
    cc1_out = nc.dram_tensor("cc1_out", [NR, D], F32)
    cc2_in = nc.dram_tensor("cc2_in", [DC * 128, EW], BF16)
    HROWS = (DC // 2) * 128
    cc2a_out = nc.dram_tensor("cc2a_out", [NR * HROWS, EW], BF16)
    cc2b_out = nc.dram_tensor("cc2b_out", [NR * HROWS, EW], BF16)

    ctx = ExitStack()
    sb = lambda name, shape, dt: ctx.enter_context(nc.sbuf_tensor(name, shape, dt))
    pst = lambda name, shape, dt=F32: ctx.enter_context(nc.psum_tensor(name, shape, dt))

    wk_fm = sb("wk_fm", [128, DC, T], BF16)
    v_aug = sb("v_aug", [128, NT, EW], BF16)
    qrhs = sb("qrhs", [128, DC, D], BF16)
    ptl = sb("ptl", [128, DC, EW], BF16)
    arena = sb("arena", [128, max(NT * D, DC * EW)], BF16)
    wwt = sb("wwt", [128, 2, DC, WW], BF16)
    xstr = sb("xstr", [128, 2, D], F32)
    outb = sb("outb", [128, 1, D], F32)
    xnb = sb("xnb", [128, 2, DC, TGW], BF16)
    etl = sb("etl", [128, DC, NT], F32)
    stg2 = sb("stg2", [128, 2, DC * 128], BF16)
    ub = sb("ub", [128, 2, D], F32)
    cb = sb("cb", [128, 2, D], F32)
    zi_t = sb("zi_t", [128, D], I32)
    qtmp = sb("qtmp", [128, D], BF16)
    qtmp2 = sb("qtmp2", [128, D], BF16)
    tri_s = sb("tri_s", [128, 128], BF16)
    triu_s = sb("triu_s", [128, 128], F32)
    idf_s = sb("idf_s", [5, 5], F32)
    idb_s = sb("idb_s", [128, 128], BF16)
    ones_c = sb("ones_c", [128, 1], BF16)
    ones_rb = sb("ones_rb", [1, 128], BF16)
    bqf_s = sb("bqf_s", [128, DC], F32)
    bkf_s = sb("bkf_s", [128, DC], F32)
    bqs_s = sb("bqs_s", [128, DC], F32)
    totals = sb("totals", [128, DC, NT], F32)
    tails = sb("tails", [128, DC, NT], F32)
    ctd_s = sb("ctd_s", [128, DC], F32)
    pm_s = sb("pm_s", [5, 5], F32)
    scl_fm = sb("scl_fm", [128, DC, 5], F32)
    lsc = sb("lsc", [128, NT], F32)
    rsc_fm = sb("rsc_fm", [128, DC], F32)
    rs_bc = sb("rs_bc", [128, D], F32)
    sm1 = sb("sm1", [128, 8, 1], F32)
    sm2 = sb("sm2", [128, 1], F32)
    sm3 = sb("sm3", [128, 1], F32)
    sm4 = sb("sm4", [128, 2], F32)
    smr = sb("smr", [128, 2], F32)
    c1em5 = sb("c1em5", [128, 1], F32)
    c_invD3 = sb("c_invD3", [128, 1], F32)
    c1em6 = sb("c1em6", [128, 1], F32)
    ct_aug = ub.ap()[0:5, 0, :]
    stg2f = stg2.ap().rearrange("p b n -> p (b n)")
    if 2 * DC * 128 >= 2048 and DC * 128 >= TGW:
        stg = stg2.ap()[:, :, 0:TGW]
        msb = stg2f[:, 0:1040].rearrange("p (b n) -> p b n", b=2)
        gsc = stg2f[:, 0:2048].rearrange("p (b n) -> p b n", b=2)
        scb = stg2f[:, 0:512].rearrange("p (b n) -> p b n", b=4)
    else:
        stg = sb("stg_t", [128, 2, TGW], BF16).ap()[:, :, :]
        msb = sb("msb_t", [128, 2, 520], BF16).ap()[:, :, :]
        gsc = sb("gsc_t", [128, 2, 512], BF16).ap()[:, :, :]
        scb = sb("scb_t", [128, 4, 128], BF16).ap()[:, :, :]
    if WW >= 512:
        qtb_s = lambda s: wwt.ap()[:, s // 2, :, (s % 2) * 128:(s % 2) * 128 + 128]
        qlb_s = lambda s: wwt.ap()[:, s // 2, :, 256 + (s % 2) * 128:256 + (s % 2) * 128 + 128]
    else:
        _qtb_t = sb("qtb_t", [128, 4, DC, 128], BF16)
        _qlb_t = sb("qlb_t", [128, 4, DC, 128], BF16)
        qtb_s = lambda s: _qtb_t.ap()[:, s, :, :]
        qlb_s = lambda s: _qlb_t.ap()[:, s, :, :]
    sc_exp = ub.ap()[0:5, 1, :]
    rs_row = ub.ap()[0:1, 1, :]
    b_row = qtmp.ap()[0:1, :]

    num_ps = pst("num_ps", [128, D], F32)
    ffn_ps = pst("ffn_ps", [128, D], F32)
    den_ps = pst("den_ps", [128, 128], F32)
    sc_ps0 = pst("sc_ps0", [128, 128], F32)
    sc_ps1 = pst("sc_ps1", [128, 128], F32)
    tp_ps = pst("tp_ps", [128, DC * 128], BF16)

    dSY = Sem(nc, ctx, "dSY")
    dXL = Sem(nc, ctx, "dXL")
    dSC = Sem(nc, ctx, "dSC")
    dGP = Sem(nc, ctx, "dGP")
    sCC = Sem(nc, ctx, "sCC")
    sT = Sem(nc, ctx, "sT")
    sV = Sem(nc, ctx, "sV")
    sS = Sem(nc, ctx, "sS")
    sG = Sem(nc, ctx, "sG")

    blk = ctx.enter_context(nc.Block())

    def wge(e, sem, n=None):
        v = sem.n if n is None else n
        if v > 0:
            e.wait_ge(sem.h, v)

    @blk.sync
    def _(e):
        for dst, src in ((tri_s, tri_lo), (triu_s, triu_m), (idb_s, ident_b), (bqf_s, bq_fm), (bkf_s, bk_fm),
                         (pm_s, pm_in)):
            i = e.dma_start(out=dst[:, :], in_=src.ap())
            dSY.inc(i, 16)
        i = e.dma_start(out=idf_s[:, :], in_=ident_f.ap()[0:5, 0:5])
        dSY.inc(i, 16)

    @blk.gpsimd
    def _(e):
        for t_, val in ((ones_c, 1.0), (ones_rb, 1.0)):
            i = e.memset(t_[:, :], val)
            sG.inc(i)
        i = e.memset(v_aug[:, :, :], 0.0)
        sG.inc(i)
        i = e.memset(ptl[:, :, :], 0.0)
        sG.inc(i)
        i = e.memset(c1em5[:, :], float(np.float32(1e-5)) * (2.0 ** 20))
        sG.inc(i)
        i = e.memset(c1em6[:, :], 1e-6)
        sG.inc(i)
        i = e.memset(c_invD3[:, :], 1.0 / float(D))
        sG.inc(i)

    def round_cast(e_s, u_ap, c_ap):
        e_s.copy(zi_t[:, :], u_ap)
        return e_s.copy(c_ap, zi_t[:, :])

    def round_chain(e, u_ap, c_ap, out_ap):
        e.tensor_sub(u_ap, u_ap, c_ap)
        e.scalar_tensor_tensor(c_ap, u_ap, 0.5, c_ap,
                               ALU.is_gt, ALU.add)
        e.tensor_single_scalar(u_ap, u_ap, -0.5, ALU.is_lt)
        return e.tensor_sub(out_ap, c_ap, u_ap)

    def emit_rhs_quant():
        for jc in range(DC):
            @blk.sync
            def _(e, jc=jc):
                wge(e, sV)
                i = e.dma_start(out=ub[:, 1, :], in_=rhst.ap()[jc * 128:(jc + 1) * 128, :])
                dSY.inc(i, 16)

            @blk.vector
            def _(e, jc=jc):
                wge(e, dSY)
                i = e.reduce_max(sm1[:, :, :], ub[:, 1, :].rearrange("p (a b) -> p a b", a=8),
                                 axis=AX.X, apply_absolute_value=True)
                wge(e, sV, sV.inc(i))
                i = e.reduce_max(sm2[:, :], sm1[:, :, 0], axis=AX.X, apply_absolute_value=True)
                wge(e, sV, sV.inc(i))
                i = e.tensor_scalar_max(sm2[:, :], sm2[:, :], 1e-12)
                wge(e, sV, sV.inc(i))
                e.tensor_scalar_mul(rsc_fm[:, jc:jc + 1], sm2[:, :], 1.0 / 127.0)
                i = e.reciprocal(sm3[:, :], sm2[:, :])
                wge(e, sV, sV.inc(i))
                i = e.tensor_scalar_mul(sm3[:, :], sm3[:, :], 127.0)
                sV.inc(i)

            @blk.scalar
            def _(e, jc=jc):
                wge(e, sV)
                e.activation(ub[:, 0, :], ub[:, 1, :], AF.Copy, scale=sm3[:, 0:1])
                i = round_cast(e, ub[:, 0, :], cb[:, 0, :])
                sS.inc(i)

            @blk.vector
            def _(e, jc=jc):
                wge(e, sS)
                i = round_chain(e, ub[:, 0, :], cb[:, 0, :], qtmp[:, :])
                sV.inc(i)

            @blk.tensor
            def _(e, jc=jc):
                wge(e, sV)
                wge(e, dSY)
                if jc > 0:
                    pass
                for dc in range(DC):
                    i = e.transpose(tp_ps[:, dc * 128:(dc + 1) * 128],
                                    qtmp[:, dc * 128:(dc + 1) * 128], idb_s[:, :])
                sT.inc(i)

            @blk.vector
            def _(e, jc=jc):
                wge(e, sT)
                for dc in range(DC):
                    i = e.tensor_copy(qrhs[:, dc, jc * 128:(jc + 1) * 128],
                                      tp_ps[:, dc * 128:(dc + 1) * 128])
                sV.inc(i)


    p1_V = []
    ev_S = {}
    ql_dma, xn_dma = {}, {}
    xcp_mark, sq_mark, xl_mark, sqrt_mark, r_mark = {}, {}, {}, {}, {}
    cast1_mark, G_mark, xn_mark, tql_mark, txn_mark = {}, {}, {}, {}, {}
    G0 = sG.n
    D0 = dSY.n

    def w_load(e, w_dram, h, wb):
        i = e.dma_start(out=wwt[:, wb, :, :],
                        in_=w_dram.ap()[:, h * WW:(h + 1) * WW]
                        .rearrange("(c p) o -> p c o", p=128))
        return dGP.inc(i, 16)

    def x_load(e, tg, xb):
        i = e.dma_start(out=xnb[:, xb, :, :],
                        in_=xn_sp.ap()[:, :, tg * TGW:(tg + 1) * TGW]
                        .rearrange("c p t -> p c t"))
        return dSY.inc(i, 16)

    def k_evac(e, og, tg, psl):
        i = e.activation(wk_fm[:, og, tg * TGW:(tg + 1) * TGW], psl, AF.Relu,
                         bias=bkf_s[:, og:og + 1])
        sS.inc(i)

    k_tm, k_pe, k_w, k_x = {}, {}, {}, {}

    def emit_kproj_w():
        @blk.gpsimd
        def _(e):
            for h in range(NH):
                k_w[h] = w_load(e, wkt, h, h % 2)

    def emit_kproj_x(tg, tt):
        @blk.gpsimd
        def _(e, tg=tg, tt=tt):
            z = tg * (TGW // 128) + tt
            wge(e, dSC, xn_dma[z])
            if tt == 0 and tg >= 2:
                wge(e, sT, k_tm[(NH - 1, tg - 2)])
            i = e.dma_start(out=xnb[:, tg % 2, :, tt * 128:(tt + 1) * 128],
                            in_=xn_sp.ap()[:, :, z * 128:(z + 1) * 128]
                            .rearrange("c p t -> p c t"))
            k_x[tg] = dGP.inc(i, 16)

    def emit_kproj_h(tg, h):
        for oc in range(WW // 128):
            og = h * (WW // 128) + oc
            gi = (tg * NH + h) * (WW // 128) + oc
            psl = (num_ps if gi % 2 == 0 else ffn_ps)[:, 0:TGW]

            @blk.tensor
            def _(e, h=h, tg=tg, oc=oc, psl=psl, gi=gi):
                if oc == 0 and h == 0:
                    wge(e, dGP, k_x[tg])
                if gi >= 2:
                    wge(e, sS, k_pe[gi - 2])
                for dc in range(DC):
                    i = e.matmul(psl, wwt[:, h % 2, dc, oc * 128:(oc + 1) * 128],
                                 xnb[:, tg % 2, dc, :], start=(dc == 0), stop=(dc == DC - 1))
                k_tm[(h, tg)] = sT.inc(i)

            @blk.scalar
            def _(e, og=og, tg=tg, psl=psl, gi=gi, h=h):
                if gi == 0:
                    wge(e, dSY, D0)
                wge(e, sT, k_tm[(h, tg)])
                k_evac(e, og, tg, psl)
                k_pe[gi] = sS.n

    emit_kproj_w()

    TPG = TGW // 128
    k_x_at, k_h_at = {}, {}
    for tg in range(NTG):
        for tt in range(TPG):
            k_x_at.setdefault(tg * TPG + tt + 1, []).append((tg, tt))
        base = (tg + 1) * TPG
        for h in range(NH):
            k_h_at.setdefault(base + 2 + h, []).append((tg, h))

    def round_v(tc):
        @blk.vector
        def _(e, tc=tc):
            wge(e, sS, cast1_mark[tc])
            if tc > 0:
                wge(e, sT, tql_mark[tc - 1])
            p = tc % 2
            i = round_chain(e, ub[:, p, :], cb[:, p, :], qtmp[:, :])
            sV.inc(i)

    def round_stage(tc):
        @blk.tensor
        def _(e, tc=tc):
            wge(e, sV)
            if tc > 0:
                wge(e, sS, ev_S[(tc - 1, 1)])
            for dc in range(DC):
                i = e.transpose(tp_ps[:, dc * 128:(dc + 1) * 128],
                                qtmp[:, dc * 128:(dc + 1) * 128], idb_s[:, :])
            tql_mark[tc] = sT.inc(i)

        @blk.scalar
        def _(e, tc=tc):
            wge(e, sT)
            if tc > 0:
                wge(e, dSC, ql_dma[tc - 1])
            i = e.copy(stg2[:, 1, :], tp_ps[:, :])
            ev_S[(tc, 0)] = sS.inc(i)
            i = e.dma_start(out=ql_sp.ap()[:, :, tc * 128:(tc + 1) * 128].rearrange("c p t -> p c t"),
                            in_=stg2[:, 1, :].rearrange("p (c t) -> p c t", c=DC))
            ql_dma[tc] = dSC.inc(i, 16)

        @blk.tensor
        def _(e, tc=tc):
            wge(e, sS, ev_S[(tc, 0)])
            for dc in range(DC):
                i = e.transpose(tp_ps[:, dc * 128:(dc + 1) * 128],
                                qtmp2[:, dc * 128:(dc + 1) * 128], idb_s[:, :])
            txn_mark[tc] = sT.inc(i)

        @blk.scalar
        def _(e, tc=tc):
            wge(e, sT)
            if tc > 0:
                wge(e, dSC, xn_dma[tc - 1])
            i = e.copy(stg2[:, 0, :], tp_ps[:, :])
            ev_S[(tc, 1)] = sS.inc(i)
            i = e.dma_start(out=xn_sp.ap()[:, :, tc * 128:(tc + 1) * 128].rearrange("c p t -> p c t"),
                            in_=stg2[:, 0, :].rearrange("p (c t) -> p c t", c=DC))
            xn_dma[tc] = dSC.inc(i, 16)

    for tc in range(NT):
        @blk.sync
        def _(e, tc=tc):
            if tc >= 2:
                wge(e, sS, xn_mark[tc - 2])
            i = e.dma_start(out=xstr[:, tc % 2, :], in_=x_in.ap()[tc * 128:(tc + 1) * 128, :])
            xl_mark[tc] = dXL.inc(i, 16)

        @blk.vector
        def _(e, tc=tc):
            wge(e, dXL, xl_mark[tc])
            xv = xstr[:, tc % 2, :]
            xcv = xv
            i = e.reduce_sum(sm2[:, :], xv, axis=AX.X)
            wge(e, sV, sV.inc(i))
            i = e.tensor_scalar(xcv, xv, float(D), sm2[:, 0:1],
                                ALU.mult, ALU.subtract)
            xcp_mark[tc] = sV.inc(i)
            i = e.reduce_max(sm2[:, :], xcv, axis=AX.X, apply_absolute_value=True)
            wge(e, sV, sV.inc(i))
            i = e.reciprocal(smr[:, tc % 2:tc % 2 + 1], sm2[:, :])
            wge(e, sV, sV.inc(i))
            i = e.tensor_scalar_mul(smr[:, tc % 2:tc % 2 + 1], smr[:, tc % 2:tc % 2 + 1], 127.0)
            wge(e, sV, sV.inc(i))

        @blk.scalar
        def _(e, tc=tc):
            wge(e, sV, xcp_mark[tc])
            wge(e, sG, G0)
            i = e.activation(outb[:, 0, :], xstr[:, tc % 2, :], AF.Square,
                             accum_out=sm3[:, 0:1])
            sq_mark[tc] = sS.inc(i)
            wge(e, sS, sq_mark[tc])
            i = e.activation(sm4[:, tc % 2:tc % 2 + 1], sm3[:, :], AF.Sqrt,
                             bias=c1em5[:, 0:1], scale=c_invD3[:, 0:1])
            sqrt_mark[tc] = sS.inc(i)

        @blk.vector
        def _(e, tc=tc):
            wge(e, sS, sqrt_mark[tc])
            i = e.reciprocal(sm4[:, tc % 2:tc % 2 + 1], sm4[:, tc % 2:tc % 2 + 1])
            r_mark[tc] = sV.inc(i)
            wge(e, sV, r_mark[tc])
            i = e.tensor_scalar(lsc[:, tc:tc + 1], sm2[:, :], sm4[:, tc % 2:tc % 2 + 1],
                                1.0 / 127.0, ALU.mult, ALU.mult)
            sV.inc(i)

        @blk.scalar
        def _(e, tc=tc):
            wge(e, sV, r_mark[tc])
            p = tc % 2
            e.activation(ub[:, p, :], xstr[:, p, :], AF.Copy, scale=smr[:, p:p + 1])
            e.copy(zi_t[:, :], ub[:, p, :])
            i = e.copy(cb[:, p, :], zi_t[:, :])
            cast1_mark[tc] = sS.inc(i)

        if tc > 0:
            round_v(tc - 1)
            round_stage(tc - 1)

        @blk.scalar
        def _(e, tc=tc):
            p = tc % 2
            if tc > 0:
                wge(e, sT, txn_mark[tc - 1])
            i = e.activation(qtmp2[:, :], xstr[:, p, :], AF.Copy, scale=sm4[:, p:p + 1])
            xn_mark[tc] = sS.inc(i)

        for tg, tt in k_x_at.get(tc, []):
            emit_kproj_x(tg, tt)
        for tg, h in k_h_at.get(tc, []):
            emit_kproj_h(tg, h)

    round_v(NT - 1)
    round_stage(NT - 1)
    for t in range(NT, NT + TPG + NH + 3):
        for tg, tt in k_x_at.get(t, []):
            emit_kproj_x(tg, tt)
        for tg, h in k_h_at.get(t, []):
            emit_kproj_h(tg, h)

    def fm_projection(w_dram, evac, head_wait=None, pre_w=None, pre_x0=None):
        units = [(h, tg) for h in range(NH) for tg in range(NTG)]
        t_marks = {}
        pe_marks = {}
        w_marks = {}
        x_marks = {}
        for ui, (h, tg) in enumerate(units):
            wb, xb = h % 2, ui % 2
            if ui == 0:
                if pre_w is not None:
                    for hh in range(NH):
                        w_marks[hh] = pre_w[hh]
                else:
                    @blk.gpsimd
                    def _(e, h=h):
                        wge(e, sT)
                        w_marks[h] = w_load(e, w_dram, h, h % 2)
                if pre_x0 is not None:
                    x_marks[0] = pre_x0
                else:
                    @blk.sync
                    def _(e, tg=tg, xb=xb):
                        wge(e, sT)
                        wge(e, dSC, xn_dma[NT - 1])
                        x_marks[0] = x_load(e, tg, xb)
            if ui + 1 < len(units):
                h2, tg2 = units[ui + 1]
                if h2 != h and pre_w is None:
                    @blk.gpsimd
                    def _(e, h2=h2):
                        if h2 >= 2:
                            wge(e, sT, t_marks[(h2 - 2, NTG - 1)])
                        w_marks[h2] = w_load(e, w_dram, h2, h2 % 2)
                @blk.sync
                def _(e, ui=ui, tg2=tg2, xb2=((ui + 1) % 2)):
                    if ui >= 1:
                        wge(e, sT, t_marks[units[ui - 1]])
                    x_marks[ui + 1] = x_load(e, tg2, xb2)
            for oc in range(WW // 128):
                og = h * (WW // 128) + oc
                psl = (num_ps if oc % 2 == 0 else ffn_ps)[:, 0:TGW]
                @blk.tensor
                def _(e, h=h, tg=tg, oc=oc, og=og, psl=psl, wb=wb, xb=xb, ui=ui):
                    if oc == 0:
                        wge(e, dSY, x_marks[ui])
                        wge(e, dGP, w_marks[h])
                    gi = ui * (WW // 128) + oc
                    if gi == 0:
                        if head_wait is None:
                            wge(e, sS)
                            wge(e, sV)
                        else:
                            head_wait(e)
                    elif gi >= 2:
                        wge(e, sS, pe_marks[gi - 2])
                    for dc in range(DC):
                        i = e.matmul(psl, wwt[:, h % 2, dc, oc * 128:(oc + 1) * 128],
                                     xnb[:, xb, dc, :], start=(dc == 0), stop=(dc == DC - 1))
                    t_marks[(h, tg)] = sT.inc(i)
                @blk.scalar
                def _(e, og=og, tg=tg, psl=psl, ui=ui, oc=oc):
                    wge(e, sT)
                    evac(e, og, tg, psl)
                    pe_marks[ui * (WW // 128) + oc] = sS.n
        return t_marks

    def tm_projection(w_dram, b_row_s, evac, w0_wait=None, x0_wait=None,
                      w1_wait=None, x1_wait=None, pre_w0=None, psum_wait=None):
        units = [(h, tg) for h in range(NH) for tg in range(NTG)]
        t_marks = {}
        pe_marks = {}
        w_marks = {}
        x_marks = {}
        for ui, (h, tg) in enumerate(units):
            wb, xb = h % 2, ui % 2
            if ui == 0:
                if pre_w0 is not None:
                    w_marks[h] = pre_w0
                else:
                    @blk.gpsimd
                    def _(e, h=h):
                        if w0_wait is not None:
                            wge(e, sT, w0_wait)
                        else:
                            wge(e, sT)
                        w_marks[h] = w_load(e, w_dram, h, h % 2)
                @blk.sync
                def _(e, tg=tg, xb=xb):
                    if x0_wait is not None:
                        wge(e, sT, x0_wait)
                    else:
                        wge(e, sT)
                    wge(e, dSC, xn_dma[NT - 1])
                    x_marks[0] = x_load(e, tg, xb)
            if ui + 1 < len(units):
                h2, tg2 = units[ui + 1]
                if h2 != h:
                    @blk.gpsimd
                    def _(e, h2=h2):
                        if h2 == 1 and w1_wait is not None:
                            wge(e, sT, w1_wait)
                        if h2 >= 2:
                            wge(e, sT, t_marks[(h2 - 2, NTG - 1)])
                        w_marks[h2] = w_load(e, w_dram, h2, h2 % 2)
                @blk.sync
                def _(e, ui=ui, tg2=tg2):
                    if ui == 0 and x1_wait is not None:
                        wge(e, sT, x1_wait)
                    if ui >= 1:
                        wge(e, sT, t_marks[units[ui - 1]])
                    x_marks[ui + 1] = x_load(e, tg2, (ui + 1) % 2)
            for tw in range(TGW // 128):
                tc = tg * (TGW // 128) + tw
                gi = ui * (TGW // 128) + tw
                psl = (num_ps if tc % 2 == 0 else ffn_ps)[:, 0:WW]
                @blk.tensor
                def _(e, h=h, tg=tg, tw=tw, psl=psl, wb=wb, xb=xb, ui=ui, gi=gi):
                    if tw == 0:
                        wge(e, dSY, x_marks[ui])
                        wge(e, dGP, w_marks[h])
                        wge(e, sG)
                    if gi >= 2:
                        wge(e, sS, pe_marks[gi - 2])
                    elif psum_wait is not None:
                        wge(e, sS, psum_wait)
                    else:
                        wge(e, sS)
                    for dc in range(DC):
                        e.matmul(psl, xnb[:, xb, dc, tw * 128:(tw + 1) * 128],
                                 wwt[:, h % 2, dc, :], start=(dc == 0), stop=False)
                    i = e.matmul(psl, ones_rb[:, :], b_row_s[:, h * WW:(h + 1) * WW],
                                 start=False, stop=True)
                    t_marks[(h, tg)] = sT.inc(i)
                @blk.scalar
                def _(e, tc=tc, h=h, psl=psl, gi=gi):
                    wge(e, sT)
                    evac(e, tc, h, psl)
                    pe_marks[gi] = sS.n
        return t_marks, pe_marks

    rs_dma = []

    def emit_rs_spill():
        @blk.scalar
        def _(e):
            wge(e, sV)
            i = e.dma_start(out=rsc_dr.ap().rearrange("o (c p) -> p (o c)", p=128),
                            in_=rsc_fm[:, :], allow_slow_non_contiguous=True)
            dSC.inc(i, 16)
            wge(e, dSC)
            i = e.dma_start(out=rs_row, in_=rsc_dr.ap())
            rs_dma.append(dSC.inc(i, 16))

    emit_rhs_quant()
    emit_rs_spill()


    arena_la = arena.ap()[:, 0:NT * D].rearrange("p (c d) -> p c d", c=NT)
    rhs_T = sT.n

    @blk.gpsimd
    def _(e):
        wge(e, sT, rhs_T)
        i = e.dma_start(out=b_row, in_=ba_row.ap())
        dGP.inc(i, 16)

    def a_evac(e, tc, h, psl):
        i = e.activation(arena_la[:, tc, h * WW:(h + 1) * WW], psl, AF.Sigmoid)
        sS.inc(i)
    k_done = k_tm[(NH - 1, NTG - 1)]
    a_tm, a_pe = tm_projection(wat, b_row, a_evac,
                               w0_wait=k_tm[(0, NTG - 1)],
                               x0_wait=k_tm[(NH - 1, max(0, NTG - 2))],
                               w1_wait=k_done, x1_wait=k_done,
                               psum_wait=k_pe[NTG * NH * (WW // 128) - 1])

    for tc in range(NT):
        @blk.scalar
        def _(e, tc=tc):
            i = e.activation(arena_la[:, tc, 0:D], arena_la[:, tc, 0:D], AF.Ln,
                             bias=c1em6[:, 0:1])
            sS.inc(i)

    a_done = a_tm[(NH - 1, NTG - 1)]
    v_w0m = []

    @blk.gpsimd
    def _(e):
        wge(e, sT, a_tm[(0, NTG - 1)])
        v_w0m.append(w_load(e, wvt, 0, 0))

    @blk.gpsimd
    def _(e):
        wge(e, sT, a_done)
        i = e.dma_start(out=b_row, in_=bv_row.ap())
        dGP.inc(i, 16)

    def v_evac(e, tc, h, psl):
        i = e.activation(v_aug[:, tc, h * WW:(h + 1) * WW], psl, AF.Copy)
        sS.inc(i)
    tm_projection(wvt, b_row, v_evac, pre_w0=v_w0m[0],
                  x0_wait=a_tm[(NH - 1, max(0, NTG - 2))],
                  w1_wait=a_done, x1_wait=a_done,
                  psum_wait=a_pe[NH * NTG * (TGW // 128) - 1])

    @blk.vector
    def _(e):
        wge(e, sG)
        for tc in range(NT):
            i = e.memset(v_aug[:, tc, D:D + 1], 1.0)
            sV.inc(i)

    q_w0, q_x0 = [], []

    @blk.gpsimd
    def _(e):
        wge(e, sT)
        for h in range(NH):
            q_w0.append(w_load(e, wqt, h, h % 2))

    @blk.sync
    def _(e):
        wge(e, sT)
        q_x0.append(x_load(e, 0, 0))


    @blk.tensor
    def _(e):
        wge(e, sS)
        wge(e, sG)
        for dc in range(DC):
            for tc in range(NT):
                i = e.matmul(den_ps[:, dc * NT + tc:dc * NT + tc + 1],
                             arena_la[:, tc, dc * 128:(dc + 1) * 128], ones_c[:, :],
                             start=True, stop=True)
        sT.inc(i)

    @blk.vector
    def _(e):
        wge(e, sT)
        i = e.tensor_copy(totals[:, :, :],
                          den_ps[:, 0:DC * NT].rearrange("p (c t) -> p c t", c=DC))
        wge(e, sV, sV.inc(i))
        i = e.memset(tails[:, :, NT - 1:NT], 0.0)
        wge(e, sV, sV.inc(i))
        for c in range(NT - 2, -1, -1):
            i = e.tensor_add(tails[:, :, c:c + 1], tails[:, :, c + 1:c + 2],
                             totals[:, :, c + 1:c + 2])
            wge(e, sV, sV.inc(i))
        i = e.tensor_add(ctd_s[:, :], tails[:, :, 0], totals[:, :, 0])
        sV.inc(i)

    @blk.gpsimd
    def _(e):
        wge(e, sV)
        i = e.dma_start(out=cc1_in.ap().rearrange("o (c p) -> p (o c)", p=128),
                        in_=ctd_s[:, :], allow_slow_non_contiguous=True)
        dGP.inc(i, 16)
        wge(e, dGP)
        i = e.collective_compute(
            "AllGather", ALU.bypass,
            replica_groups=[[g * NR + r for r in range(NR)] for g in range(NCORES // NR)],
            ins=[cc1_in.ap().opt()], outs=[cc1_out.ap().opt()])
        sCC.inc(i)

    def emit_rs_bcast():
        @blk.gpsimd
        def _(e):
            i = e.memset(ct_aug, 1.0)
            sG.inc(i)

        @blk.tensor
        def _(e):
            wge(e, dSC, rs_dma[0])
            wge(e, sG)
            for (a, b) in PCS:
                i = e.matmul(num_ps[:, a:b], ub.ap()[0:1, 0, 0:128], rs_row[:, a:b],
                             start=True, stop=True)
            sT.inc(i)

        @blk.vector
        def _(e):
            wge(e, sT)
            i = e.tensor_copy(rs_bc[:, :], num_ps[:, :])
            sV.inc(i)


    @blk.scalar
    def _(e):
        wge(e, sV)
        i = e.activation(etl[:, :, :], tails[:, :, :], AF.Exp)
        sS.inc(i)
    GW = max(g for g in (4, 2, 1) if NT % g == 0 and g * 128 <= D)
    groups = [(dc, tc0) for dc in range(DC) for tc0 in range(0, NT, GW)]
    exp_S = {}
    for gi, (dc, tc0) in enumerate(groups):
        psl = (num_ps if gi % 2 == 0 else ffn_ps)
        @blk.tensor
        def _(e, dc=dc, tc0=tc0, psl=psl, gi=gi):
            if gi < 2:
                wge(e, sS)
            else:
                wge(e, sS, exp_S[gi - 2])
            for tw in range(GW):
                i = e.matmul(psl[:, tw * 128:(tw + 1) * 128],
                             arena_la[:, tc0 + tw, dc * 128:(dc + 1) * 128], tri_s[:, :],
                             start=True, stop=True)
            sT.inc(i)
        @blk.scalar
        def _(e, dc=dc, tc0=tc0, psl=psl, gi=gi):
            wge(e, sT)
            if gi >= 2:
                wge(e, sV)
            i = e.activation(gsc[:, gi % 2, 0:GW * 128], psl[:, 0:GW * 128], AF.Exp)
            exp_S[gi] = sS.inc(i)
        @blk.vector
        def _(e, dc=dc, tc0=tc0, gi=gi):
            wge(e, sS, exp_S[gi])
            for tw in range(GW):
                tc = tc0 + tw
                i = e.scalar_tensor_tensor(
                    wk_fm[:, dc, tc * 128:(tc + 1) * 128],
                    gsc[:, gi % 2, tw * 128:(tw + 1) * 128].rearrange("p a b -> p (a b)") if False else gsc[:, gi % 2, tw * 128:(tw + 1) * 128],
                    etl[:, dc, tc:tc + 1],
                    wk_fm[:, dc, tc * 128:(tc + 1) * 128],
                    ALU.mult, ALU.mult)
            sV.inc(i)

    arena_wt = arena.ap()[:, 0:NT * D].rearrange("p (c d) -> p c d", c=NT)
    for tc in range(NT):
        @blk.tensor
        def _(e, tc=tc):
            wge(e, sV)
            for dc in range(DC):
                i = e.transpose(tp_ps[:, dc * 128:(dc + 1) * 128],
                                wk_fm[:, dc, tc * 128:(tc + 1) * 128], idb_s[:, :])
            sT.inc(i)

        @blk.vector
        def _(e, tc=tc):
            wge(e, sT)
            i = e.tensor_copy(arena_wt[:, tc, :], tp_ps[:, :])
            sV.inc(i)

    m_pieces = PCS + [(D, EW)]
    m_units = [(dc, a, b) for dc in range(DC) for (a, b) in m_pieces]
    mV, mG = {}, {}
    for mi, (dc, a, b) in enumerate(m_units):
        w = b - a
        pb = mi % 2
        mps = (num_ps if pb == 0 else ffn_ps)
        @blk.tensor
        def _(e, dc=dc, a=a, b=b, w=w, mi=mi, pb=pb, mps=mps):
            if mi == 0:
                wge(e, sV)
                wge(e, sS)
            if mi >= 2:
                wge(e, sV, mV[mi - 2])
            for tc in range(NT):
                i = e.matmul(mps[:, 0:w],
                             arena_wt[:, tc, dc * 128:(dc + 1) * 128],
                             v_aug[:, tc, a:b], start=(tc == 0), stop=(tc == NT - 1))
            sT.inc(i)

        @blk.vector
        def _(e, w=w, mi=mi, pb=pb, mps=mps):
            wge(e, sT)
            if mi >= 2:
                wge(e, dGP, mG[mi - 2])
            i = e.tensor_copy(msb[:, pb, 0:w], mps[:, 0:w])
            mV[mi] = sV.inc(i)

        @blk.gpsimd
        def _(e, dc=dc, a=a, b=b, w=w, mi=mi, pb=pb):
            wge(e, sV, mV[mi])
            i = e.dma_start(out=cc2_in.ap()[dc * 128:(dc + 1) * 128, a:b],
                            in_=msb[:, pb, 0:w])
            mG[mi] = dGP.inc(i, 16)
            if mi == DC // 2 * len(m_pieces) - 1:
                wge(e, dGP)
                i = e.collective_compute(
                    "AllGather", ALU.bypass,
                    replica_groups=[[g * NR + r for r in range(NR)]
                                    for g in range(NCORES // NR)],
                    ins=[cc2_in.ap()[0:HROWS, :].opt()], outs=[cc2a_out.ap().opt()])
                sCC.inc(i)

    p2g_T = sT.n

    emit_rs_bcast()

    @blk.gpsimd
    def _(e):
        wge(e, dGP, mG[DC * len(m_pieces) - 1])
        i = e.collective_compute(
            "AllGather", ALU.bypass,
            replica_groups=[[g * NR + r for r in range(NR)] for g in range(NCORES // NR)],
            ins=[cc2_in.ap()[HROWS:2 * HROWS, :].opt()], outs=[cc2b_out.ap().opt()])
        sCC.inc(i)


    @blk.sync
    def _(e):
        wge(e, sCC, 1)
        wge(e, sG)
        wge(e, sV)
        i = e.dma_start(out=ct_aug[0:NR, :], in_=cc1_out.ap())
        dSY.inc(i, 16)

    @blk.tensor
    def _(e):
        wge(e, dSY)
        wge(e, sG)
        wge(e, sS)
        wge(e, sV)
        for (a, b) in PCS:
            i = e.matmul(num_ps[0:5, a:b], pm_s[:, :], ct_aug[:, a:b],
                         start=True, stop=True)
        sT.inc(i)

    @blk.scalar
    def _(e):
        wge(e, sT)
        i = e.activation(sc_exp, num_ps[0:5, :], AF.Exp)
        sS.inc(i)

    @blk.tensor
    def _(e):
        wge(e, sS)
        for dc in range(DC):
            i = e.transpose(sc_ps0[:, dc * 8:dc * 8 + 5],
                            sc_exp[:, dc * 128:(dc + 1) * 128], idf_s[:, :])
        sT.inc(i)

    @blk.vector
    def _(e):
        wge(e, sT)
        for dc in range(DC):
            i = e.tensor_copy(scl_fm[:, dc, :], sc_ps0[:, dc * 8:dc * 8 + 5])
        wge(e, sV, sV.inc(i))
        i = e.tensor_mul(bqs_s[:, :], bqf_s[:, :], scl_fm[:, :, 4])
        sV.inc(i)
    scales_V = sV.n

    q_sp_tg0 = []

    def q_evac(e, og, tg, psl):
        wge(e, dSC, max(0, dSC.n - 16))
        wge(e, sV, scales_V)
        i = e.activation(stg[:, tg % 2, :], psl, AF.Relu,
                         scale=scl_fm[:, og, 4:5], bias=bqs_s[:, og:og + 1])
        m = sS.inc(i)
        wge(e, sS, m)
        i = e.dma_start(out=qt_sp.ap()[og, :, tg * TGW:(tg + 1) * TGW],
                        in_=stg[:, tg % 2, :])
        dSC.inc(i, 16)
        if tg == 0 and og == DC - 1:
            q_sp_tg0.append(dSC.n)
    q_tm = fm_projection(wqt, q_evac, pre_w=q_w0, pre_x0=q_x0[0])

    HC = DC // 2
    arena_h = arena.ap()[:, 0:2 * HC * EW].rearrange("p (b c w) -> p b c w", b=2, c=HC)
    jh_G, jh_V = {}, {}
    units_jh = [(j, h) for h in range(2) for j in range(NR)]
    for ui, (j, h) in enumerate(units_jh):
        pb = ui % 2
        @blk.gpsimd
        def _(e, j=j, h=h, ui=ui, pb=pb):
            if ui == 0:
                wge(e, sCC, 2)
                wge(e, sV)
            if ui == 1:
                wge(e, sT, p2g_T)
            if ui == NR:
                wge(e, sCC, 3)
            if ui >= 2:
                wge(e, sV, jh_V[ui - 2])
            cco = cc2a_out if h == 0 else cc2b_out
            i = e.dma_start(
                out=arena_h[:, pb, :, 0:EW],
                in_=cco.ap()[j * HROWS:(j + 1) * HROWS, :]
                .rearrange("(c p) e -> p c e", p=128))
            jh_G[ui] = dGP.inc(i, 16)
        @blk.vector
        def _(e, j=j, h=h, ui=ui, pb=pb):
            wge(e, dGP, jh_G[ui])
            wge(e, sV, scales_V)
            for c in range(HC):
                dc = h * HC + c
                i = e.scalar_tensor_tensor(ptl[:, dc, :], arena_h[:, pb, c, 0:EW],
                                           scl_fm[:, dc, j:j + 1], ptl[:, dc, :],
                                           ALU.mult, ALU.add)
            jh_V[ui] = sV.inc(i)

    p3_T_end, p3_V_end, p3_S_end, p3_den, y_dma, qlb_dma = [], [], [], [], [], []
    p3_xl = {}
    for cq in range(NT):
        @blk.scalar
        def _(e, cq=cq):
            if cq < 2:
                wge(e, dSC, q_sp_tg0[0])
                wge(e, sT, q_tm[(0, NTG - 1)])
            elif cq < 4:
                wge(e, dSC)
            else:
                wge(e, sT, p3_T_end[cq - 4])
            i = e.dma_start(out=qtb_s(cq % 4)[:, :, :],
                            in_=qt_sp.ap()[:, :, cq * 128:(cq + 1) * 128]
                            .rearrange("c p t -> p c t"))
            dSC.inc(i, 16)

        @blk.gpsimd
        def _(e, cq=cq):
            if cq < 2:
                wge(e, dSC, q_sp_tg0[0])
                wge(e, sT, q_tm[(0, NTG - 1)])
            elif cq < 4:
                wge(e, dSC)
            else:
                wge(e, sT, p3_T_end[cq - 4])
            i = e.dma_start(out=qlb_s(cq % 4)[:, :, :],
                            in_=ql_sp.ap()[:, :, cq * 128:(cq + 1) * 128]
                            .rearrange("c p t -> p c t"))
            qlb_dma.append(dGP.inc(i, 16))

        @blk.sync
        def _(e, cq=cq):
            if cq >= 2:
                wge(e, sV, p3_den[cq - 2])
            i = e.dma_start(out=xstr[:, cq % 2, :],
                            in_=x_in.ap()[cq * 128:(cq + 1) * 128, :])
            p3_xl[cq] = dXL.inc(i, 16)

        sc_T, sc_V, use_T = {}, {}, {}

        def emit_score(ck):
            psl = (sc_ps0 if ck % 2 == 0 else sc_ps1)[:, :]
            @blk.tensor
            def _(e, cq=cq, ck=ck, psl=psl):
                if ck == 0:
                    wge(e, dSC)
                if ck >= 2:
                    wge(e, sV, sc_V[ck - 2])
                for dc in range(DC):
                    i = e.matmul(psl, wk_fm[:, dc, ck * 128:(ck + 1) * 128],
                                 qtb_s(cq % 4)[:, dc, :], start=(dc == 0), stop=(dc == DC - 1))
                sc_T[ck] = sT.inc(i)
            @blk.vector
            def _(e, cq=cq, ck=ck, psl=psl):
                wge(e, sT, sc_T[ck])
                if ck >= 4:
                    wge(e, sT, use_T[ck - 4])
                if ck == cq:
                    i = e.tensor_mul(scb[:, ck % 4, :], psl, triu_s[:, :])
                else:
                    i = e.tensor_copy(scb[:, ck % 4, :], psl)
                sc_V[ck] = sV.inc(i)

        emit_score(0)
        if cq >= 1:
            emit_score(1)
        @blk.tensor
        def _(e, cq=cq):
            if cq > 0:
                wge(e, sV, p3_den[cq - 1])
                wge(e, sS, p3_S_end[cq - 1])
            else:
                wge(e, sV)
            for dc in range(DC):
                for (a, b) in PCS:
                    e.matmul(num_ps[:, a:b], qtb_s(cq % 4)[:, dc, :], ptl[:, dc, a:b],
                             start=(dc == 0), stop=False)
                i = e.matmul(den_ps[:, 0:4], qtb_s(cq % 4)[:, dc, :], ptl[:, dc, D:EW],
                             start=(dc == 0), stop=False)
            sT.inc(i)
        for ck in range(cq + 1):
            if ck + 2 <= cq:
                emit_score(ck + 2)
            @blk.tensor
            def _(e, cq=cq, ck=ck):
                wge(e, sV, sc_V[ck])
                last = (ck == cq)
                for (a, b) in PCS:
                    e.matmul(num_ps[:, a:b], scb[:, ck % 4, :], v_aug[:, ck, a:b],
                             start=False, stop=(last and b == D))
                i = e.matmul(den_ps[:, 0:4], scb[:, ck % 4, :], v_aug[:, ck, D:EW],
                             start=False, stop=last)
                use_T[ck] = sT.inc(i)
        @blk.tensor
        def _(e, cq=cq):
            wge(e, dGP, qlb_dma[cq])
            for dc in range(DC):
                for (a, b) in PCS:
                    i = e.matmul(ffn_ps[:, a:b], qlb_s(cq % 4)[:, dc, :], qrhs[:, dc, a:b],
                                 start=(dc == 0), stop=(dc == DC - 1))
            p3_T_end.append(sT.inc(i))

        @blk.vector
        def _(e, cq=cq):
            wge(e, sT, p3_T_end[cq])
            wge(e, dXL, p3_xl[cq])
            i = e.tensor_scalar_add(sm2[:, :], den_ps[:, 0:1], 1e-6)
            wge(e, sV, sV.inc(i))
            i = e.reciprocal(sm2[:, :], sm2[:, :])
            wge(e, sV, sV.inc(i))
            if cq > 0:
                wge(e, dSY, y_dma[cq - 1])
            i = e.scalar_tensor_tensor(outb[:, 0, :], num_ps[:, :], sm2[:, 0:1],
                                       xstr[:, cq % 2, :], ALU.mult, ALU.add)
            p3_den.append(sV.inc(i))

        @blk.scalar
        def _(e, cq=cq):
            wge(e, sT, p3_T_end[cq])
            i = e.activation(ub[:, 0, :], ffn_ps[:, :], AF.Copy, scale=lsc[:, cq:cq + 1])
            p3_S_end.append(sS.inc(i))

        @blk.vector
        def _(e, cq=cq):
            wge(e, sS, p3_S_end[cq])
            e.tensor_mul(ub[:, 1, :], ub[:, 0, :], rs_bc[:, :])
            i = e.tensor_add(outb[:, 0, :], outb[:, 0, :], ub[:, 1, :])
            p3_V_end.append(sV.inc(i))

        @blk.sync
        def _(e, cq=cq):
            wge(e, sV, p3_V_end[cq])
            i = e.dma_start(out=y_out.ap()[cq * 128:(cq + 1) * 128, :],
                            in_=outb[:, 0, :])
            y_dma.append(dSY.inc(i, 16))

    @blk.sync
    def _(e):
        wge(e, dSY)

    ctx.close()
    _merge_engine_blocks(nc)
    return nc


_CACHE = {}


def _get_nc(T, D, NR, NCORES):
    key = (T, D, NR, NCORES)
    if key not in _CACHE:
        _CACHE[key] = build_nc(T, D, NR, NCORES)
    return _CACHE[key]


def make_in_maps(x2d, W_q, b_q, W_k, b_k, W_v, b_v, W_a, b_a, rhs,
                 n_cores=8, NR=4):
    import ml_dtypes
    bf16 = ml_dtypes.bfloat16
    TT, D = x2d.shape
    T = TT // n_cores
    DC = D // 128
    f = np.float32
    shared = dict(
        wqt=np.ascontiguousarray(np.asarray(W_q, dtype=f).T),
        wkt=np.ascontiguousarray(np.asarray(W_k, dtype=f).T),
        wvt=np.ascontiguousarray(np.asarray(W_v, dtype=f).T),
        wat=np.ascontiguousarray(np.asarray(W_a, dtype=f).T),
        rhst=np.ascontiguousarray(np.asarray(rhs, dtype=f).T),
        bq_fm=np.ascontiguousarray(np.asarray(b_q, dtype=f).reshape(DC, 128).T),
        bk_fm=np.ascontiguousarray(np.asarray(b_k, dtype=f).reshape(DC, 128).T),
        ba_row=np.ascontiguousarray(np.asarray(b_a, dtype=f).reshape(1, D)),
        bv_row=np.ascontiguousarray(np.asarray(b_v, dtype=f).reshape(1, D)),
        tri_lo=np.tril(np.ones((128, 128))).astype(bf16),
        triu_m=np.triu(np.ones((128, 128))).astype(f),
        ident_f=np.eye(128, dtype=f),
        ident_b=np.eye(128).astype(bf16),
    )
    in_maps = []
    for c in range(n_cores):
        R = c % NR
        pm = np.zeros((5, 5), dtype=f)
        for kp in range(4):
            for j in range(4):
                if j < kp <= R:
                    pm[kp, j] = 1.0
            pm[kp, 4] = 1.0 if kp > R else 0.0
        for j in range(4):
            if j >= R:
                pm[4, j] = BIG_NEG
        m = dict(shared)
        m["x"] = np.ascontiguousarray(x2d[c * T:(c + 1) * T], dtype=f)
        m["pm"] = pm
        in_maps.append(m)
    return in_maps


def kernel(L0_attn_x_seq, L0_attn_W_q, L0_attn_b_q, L0_attn_W_k, L0_attn_b_k,
           L0_attn_W_v, L0_attn_b_v, L0_attn_W_a, L0_attn_b_a, L0_ffn_rhs):
    B, Tseq, D = L0_attn_x_seq.shape
    n_cores = 8
    NR = n_cores // B
    x2d = np.asarray(L0_attn_x_seq, dtype=np.float32).reshape(B * Tseq, D)
    T = (B * Tseq) // n_cores
    nc = _get_nc(T, D, NR, n_cores)
    in_maps = make_in_maps(
        x2d,
        L0_attn_W_q, L0_attn_b_q, L0_attn_W_k, L0_attn_b_k,
        L0_attn_W_v, L0_attn_b_v, L0_attn_W_a, L0_attn_b_a,
        L0_ffn_rhs, n_cores=n_cores, NR=NR)
    res = run_bass_kernel_spmd(nc, in_maps, core_ids=list(range(n_cores)))
    y = np.concatenate([r["y"] for r in res.results], axis=0)
    return np.ascontiguousarray(y.reshape(B, Tseq, D), dtype=np.float32)


# version 41
# speedup vs baseline: 1.0721x; 1.0104x over previous
import numpy as np
from contextlib import ExitStack

import concourse.bass as bass
import concourse.mybir as mybir
from concourse.bass_utils import run_bass_kernel_spmd

F32 = mybir.dt.float32
BF16 = mybir.dt.bfloat16
I32 = mybir.dt.int32
AF = mybir.ActivationFunctionType
ALU = mybir.AluOpType
AX = mybir.AxisListType

BIG_NEG = -1.0e9



def _merge_engine_blocks(nc):
    f = nc.m.functions[0]
    blocks = list(f.blocks)
    if len(blocks) <= 3:
        return
    by_name = {b.name: b for b in blocks}
    head, tail = blocks[0], blocks[-1]
    heads = [i.target for i in head.instructions
             if type(i).__name__ == "InstUnconditionalBranch"]
    new_blocks = [head]
    consumed = set()
    for h in heads:
        merged, cur, last_branch = [], h, None
        first = by_name[h]
        while cur != tail.name:
            b = by_name[cur]
            consumed.add(cur)
            nxt = None
            for i in b.instructions:
                if type(i).__name__ == "InstUnconditionalBranch":
                    nxt = i.target
                    last_branch = i
                else:
                    merged.append(i)
            cur = nxt
        merged.append(last_branch)
        first.instructions = merged
        new_blocks.append(first)
    for b in blocks[1:-1]:
        if b.name not in consumed and b not in new_blocks:
            new_blocks.append(b)
    new_blocks.append(tail)
    f.blocks = new_blocks


class Sem:
    def __init__(self, nc, ctx, name):
        self.h = ctx.enter_context(nc.semaphore(name))
        self.n = 0

    def inc(self, inst, k=1):
        inst.then_inc(self.h, k)
        self.n += k
        return self.n


def _pieces(D, cap=512):
    out, o = [], 0
    while o < D:
        w = min(cap, D - o)
        out.append((o, o + w))
        o += w
    return out


def build_nc(T=2048, D=1024, NR=4, NCORES=8):
    DC = D // 128
    NT = T // 128
    EW = D + 4
    TGW = min(512, T, D)
    NTG = T // TGW
    NH = max(1, D // TGW)
    WW = D // NH
    PCS = _pieces(D)

    nc = bass.Bass("TRN2", target_bir_lowering=False, detect_race_conditions=False)
    dp = nc.declare_dram_parameter

    x_in = dp("x", [T, D], F32, isOutput=False)
    wqt = dp("wqt", [D, D], F32, isOutput=False)
    wkt = dp("wkt", [D, D], F32, isOutput=False)
    wvt = dp("wvt", [D, D], F32, isOutput=False)
    wat = dp("wat", [D, D], F32, isOutput=False)
    rhst = dp("rhst", [D, D], F32, isOutput=False)
    bq_fm = dp("bq_fm", [128, DC], F32, isOutput=False)
    bk_fm = dp("bk_fm", [128, DC], F32, isOutput=False)
    ba_row = dp("ba_row", [1, D], F32, isOutput=False)
    bv_row = dp("bv_row", [1, D], F32, isOutput=False)
    tri_lo = dp("tri_lo", [128, 128], BF16, isOutput=False)
    triu_m = dp("triu_m", [128, 128], F32, isOutput=False)
    ident_f = dp("ident_f", [128, 128], F32, isOutput=False)
    ident_b = dp("ident_b", [128, 128], BF16, isOutput=False)
    pm_in = dp("pm", [5, 5], F32, isOutput=False)
    y_out = dp("y", [T, D], F32, isOutput=True)

    xn_sp = nc.dram_tensor("xn_sp", [DC, 128, T], BF16)
    qt_sp = nc.dram_tensor("qt_sp", [DC, 128, T], BF16)
    ql_sp = nc.dram_tensor("ql_sp", [DC, 128, T], BF16)
    rsc_dr = nc.dram_tensor("rsc_dr", [1, D], F32)
    cc1_in = nc.dram_tensor("cc1_in", [1, D], F32)
    cc1_out = nc.dram_tensor("cc1_out", [NR, D], F32)
    cc2_in = nc.dram_tensor("cc2_in", [DC * 128, EW], BF16)
    HROWS = (DC // 2) * 128
    cc2a_out = nc.dram_tensor("cc2a_out", [NR * HROWS, EW], BF16)
    cc2b_out = nc.dram_tensor("cc2b_out", [NR * HROWS, EW], BF16)

    ctx = ExitStack()
    sb = lambda name, shape, dt: ctx.enter_context(nc.sbuf_tensor(name, shape, dt))
    pst = lambda name, shape, dt=F32: ctx.enter_context(nc.psum_tensor(name, shape, dt))

    wk_fm = sb("wk_fm", [128, DC, T], BF16)
    v_aug = sb("v_aug", [128, NT, EW], BF16)
    qrhs = sb("qrhs", [128, DC, D], BF16)
    ptl = sb("ptl", [128, DC, EW], BF16)
    arena = sb("arena", [128, max(NT * D, DC * EW)], BF16)
    wwt = sb("wwt", [128, 2, DC, WW], BF16)
    xstr = sb("xstr", [128, 2, D], F32)
    outb = sb("outb", [128, 1, D], F32)
    xnb = sb("xnb", [128, 2, DC, TGW], BF16)
    etl = sb("etl", [128, DC, NT], F32)
    stg2 = sb("stg2", [128, 2, DC * 128], BF16)
    ub = sb("ub", [128, 2, D], F32)
    cb = sb("cb", [128, 2, D], F32)
    zi_t = sb("zi_t", [128, D], I32)
    qtmp = sb("qtmp", [128, D], BF16)
    qtmp2 = sb("qtmp2", [128, D], BF16)
    tri_s = sb("tri_s", [128, 128], BF16)
    triu_s = sb("triu_s", [128, 128], F32)
    idf_s = sb("idf_s", [5, 5], F32)
    idb_s = sb("idb_s", [128, 128], BF16)
    ones_c = sb("ones_c", [128, 1], BF16)
    ones_rb = sb("ones_rb", [1, 128], BF16)
    bqf_s = sb("bqf_s", [128, DC], F32)
    bkf_s = sb("bkf_s", [128, DC], F32)
    bqs_s = sb("bqs_s", [128, DC], F32)
    totals = sb("totals", [128, DC, NT], F32)
    tails = sb("tails", [128, DC, NT], F32)
    ctd_s = sb("ctd_s", [128, DC], F32)
    pm_s = sb("pm_s", [5, 5], F32)
    scl_fm = sb("scl_fm", [128, DC, 5], F32)
    lsc = sb("lsc", [128, NT], F32)
    rsc_fm = sb("rsc_fm", [128, DC], F32)
    rs_bc = sb("rs_bc", [128, D], F32)
    sm1 = sb("sm1", [128, 8, 1], F32)
    sm2 = sb("sm2", [128, 1], F32)
    sm3 = sb("sm3", [128, 1], F32)
    sm4 = sb("sm4", [128, 2], F32)
    smr = sb("smr", [128, 2], F32)
    c1em5 = sb("c1em5", [128, 1], F32)
    c_invD3 = sb("c_invD3", [128, 1], F32)
    c1em6 = sb("c1em6", [128, 1], F32)
    ct_aug = ub.ap()[0:5, 0, :]
    stg2f = stg2.ap().rearrange("p b n -> p (b n)")
    if 2 * DC * 128 >= 2048 and DC * 128 >= TGW:
        stg = stg2.ap()[:, :, 0:TGW]
        msb = stg2f[:, 0:1040].rearrange("p (b n) -> p b n", b=2)
        gsc = stg2f[:, 0:2048].rearrange("p (b n) -> p b n", b=2)
        scb = stg2f[:, 0:512].rearrange("p (b n) -> p b n", b=4)
    else:
        stg = sb("stg_t", [128, 2, TGW], BF16).ap()[:, :, :]
        msb = sb("msb_t", [128, 2, 520], BF16).ap()[:, :, :]
        gsc = sb("gsc_t", [128, 2, 512], BF16).ap()[:, :, :]
        scb = sb("scb_t", [128, 4, 128], BF16).ap()[:, :, :]
    if WW >= 512:
        qtb_s = lambda s: wwt.ap()[:, s // 2, :, (s % 2) * 128:(s % 2) * 128 + 128]
        qlb_s = lambda s: wwt.ap()[:, s // 2, :, 256 + (s % 2) * 128:256 + (s % 2) * 128 + 128]
    else:
        _qtb_t = sb("qtb_t", [128, 4, DC, 128], BF16)
        _qlb_t = sb("qlb_t", [128, 4, DC, 128], BF16)
        qtb_s = lambda s: _qtb_t.ap()[:, s, :, :]
        qlb_s = lambda s: _qlb_t.ap()[:, s, :, :]
    sc_exp = ub.ap()[0:5, 1, :]
    rs_row = ub.ap()[0:1, 1, :]
    b_row = qtmp2.ap()[0:1, :]

    num_ps = pst("num_ps", [128, D], F32)
    ffn_ps = pst("ffn_ps", [128, D], F32)
    den_ps = pst("den_ps", [128, 128], F32)
    sc_ps0 = pst("sc_ps0", [128, 128], F32)
    sc_ps1 = pst("sc_ps1", [128, 128], F32)
    tp_ps = pst("tp_ps", [128, DC * 128], BF16)

    dSY = Sem(nc, ctx, "dSY")
    dXL = Sem(nc, ctx, "dXL")
    dSC = Sem(nc, ctx, "dSC")
    dGP = Sem(nc, ctx, "dGP")
    sCC = Sem(nc, ctx, "sCC")
    sT = Sem(nc, ctx, "sT")
    sV = Sem(nc, ctx, "sV")
    sS = Sem(nc, ctx, "sS")
    sG = Sem(nc, ctx, "sG")

    blk = ctx.enter_context(nc.Block())

    def wge(e, sem, n=None):
        v = sem.n if n is None else n
        if v > 0:
            e.wait_ge(sem.h, v)

    @blk.sync
    def _(e):
        for dst, src in ((tri_s, tri_lo), (triu_s, triu_m), (idb_s, ident_b), (bqf_s, bq_fm), (bkf_s, bk_fm),
                         (pm_s, pm_in)):
            i = e.dma_start(out=dst[:, :], in_=src.ap())
            dSY.inc(i, 16)
        i = e.dma_start(out=idf_s[:, :], in_=ident_f.ap()[0:5, 0:5])
        dSY.inc(i, 16)

    @blk.gpsimd
    def _(e):
        for t_, val in ((ones_c, 1.0), (ones_rb, 1.0)):
            i = e.memset(t_[:, :], val)
            sG.inc(i)
        i = e.memset(v_aug[:, :, :], 0.0)
        sG.inc(i)
        i = e.memset(ptl[:, :, :], 0.0)
        sG.inc(i)
        i = e.memset(c1em5[:, :], float(np.float32(1e-5)) * (2.0 ** 20))
        sG.inc(i)
        i = e.memset(c1em6[:, :], 1e-6)
        sG.inc(i)
        i = e.memset(c_invD3[:, :], 1.0 / float(D))
        sG.inc(i)

    def round_cast(e_s, u_ap, c_ap):
        e_s.copy(zi_t[:, :], u_ap)
        return e_s.copy(c_ap, zi_t[:, :])

    def round_chain(e, u_ap, c_ap, out_ap):
        e.tensor_sub(u_ap, u_ap, c_ap)
        e.scalar_tensor_tensor(c_ap, u_ap, 0.5, c_ap,
                               ALU.is_gt, ALU.add)
        e.tensor_single_scalar(u_ap, u_ap, -0.5, ALU.is_lt)
        return e.tensor_sub(out_ap, c_ap, u_ap)

    def emit_rhs_quant():
        rhs_dm, rhs_rel = {}, {}
        for jc in range(DC):
            @blk.sync
            def _(e, jc=jc):
                if jc < 2:
                    wge(e, sV, p1_V_end)
                    wge(e, sS, xn_mark[NT - 1])
                else:
                    wge(e, sS, rhs_rel[jc - 2])
                i = e.dma_start(out=xstr[:, jc % 2, :],
                                in_=rhst.ap()[jc * 128:(jc + 1) * 128, :])
                rhs_dm[jc] = dSY.inc(i, 16)

            @blk.vector
            def _(e, jc=jc):
                wge(e, dSY, rhs_dm[jc])
                i = e.reduce_max(sm1[:, :, :], xstr[:, jc % 2, :].rearrange("p (a b) -> p a b", a=8),
                                 axis=AX.X, apply_absolute_value=True)
                wge(e, sV, sV.inc(i))
                i = e.reduce_max(sm2[:, :], sm1[:, :, 0], axis=AX.X, apply_absolute_value=True)
                wge(e, sV, sV.inc(i))
                i = e.tensor_scalar_max(sm2[:, :], sm2[:, :], 1e-12)
                wge(e, sV, sV.inc(i))
                e.tensor_scalar_mul(rsc_fm[:, jc:jc + 1], sm2[:, :], 1.0 / 127.0)
                i = e.reciprocal(sm3[:, :], sm2[:, :])
                wge(e, sV, sV.inc(i))
                i = e.tensor_scalar_mul(sm3[:, :], sm3[:, :], 127.0)
                sV.inc(i)

            @blk.scalar
            def _(e, jc=jc):
                wge(e, sV)
                i = e.activation(ub[:, 0, :], xstr[:, jc % 2, :], AF.Copy,
                                 scale=sm3[:, 0:1])
                rhs_rel[jc] = sS.inc(i)
                i = round_cast(e, ub[:, 0, :], cb[:, 0, :])
                sS.inc(i)

            @blk.vector
            def _(e, jc=jc):
                wge(e, sS)
                if jc == 0:
                    wge(e, sT, tql_mark[NT - 1])
                i = round_chain(e, ub[:, 0, :], cb[:, 0, :], qtmp[:, :])
                sV.inc(i)

            @blk.tensor
            def _(e, jc=jc):
                wge(e, sV)
                wge(e, dSY)
                for dc in range(DC):
                    i = e.transpose(tp_ps[:, dc * 128:(dc + 1) * 128],
                                    qtmp[:, dc * 128:(dc + 1) * 128], idb_s[:, :])
                sT.inc(i)

            @blk.vector
            def _(e, jc=jc):
                wge(e, sT)
                for dc in range(DC):
                    i = e.tensor_copy(qrhs[:, dc, jc * 128:(jc + 1) * 128],
                                      tp_ps[:, dc * 128:(dc + 1) * 128])
                sV.inc(i)


    p1_V = []
    ev_S = {}
    ql_dma, xn_dma = {}, {}
    xcp_mark, sq_mark, xl_mark, sqrt_mark, r_mark = {}, {}, {}, {}, {}
    cast1_mark, G_mark, xn_mark, tql_mark, txn_mark = {}, {}, {}, {}, {}
    G0 = sG.n
    D0 = dSY.n

    def w_load(e, w_dram, h, wb):
        i = e.dma_start(out=wwt[:, wb, :, :],
                        in_=w_dram.ap()[:, h * WW:(h + 1) * WW]
                        .rearrange("(c p) o -> p c o", p=128))
        return dGP.inc(i, 16)

    def x_load(e, tg, xb):
        i = e.dma_start(out=xnb[:, xb, :, :],
                        in_=xn_sp.ap()[:, :, tg * TGW:(tg + 1) * TGW]
                        .rearrange("c p t -> p c t"))
        return dSY.inc(i, 16)

    def k_evac(e, og, tg, psl):
        i = e.activation(wk_fm[:, og, tg * TGW:(tg + 1) * TGW], psl, AF.Relu,
                         bias=bkf_s[:, og:og + 1])
        sS.inc(i)

    k_tm, k_pe, k_w, k_x = {}, {}, {}, {}

    def emit_kproj_w():
        @blk.gpsimd
        def _(e):
            for h in range(NH):
                k_w[h] = w_load(e, wkt, h, h % 2)

    def emit_kproj_x(tg, tt):
        @blk.gpsimd
        def _(e, tg=tg, tt=tt):
            z = tg * (TGW // 128) + tt
            wge(e, dSC, xn_dma[z])
            if tt == 0 and tg >= 2:
                wge(e, sT, k_tm[(NH - 1, tg - 2)])
            i = e.dma_start(out=xnb[:, tg % 2, :, tt * 128:(tt + 1) * 128],
                            in_=xn_sp.ap()[:, :, z * 128:(z + 1) * 128]
                            .rearrange("c p t -> p c t"))
            k_x[tg] = dGP.inc(i, 16)

    def emit_kproj_h(tg, h):
        for oc in range(WW // 128):
            og = h * (WW // 128) + oc
            gi = (tg * NH + h) * (WW // 128) + oc
            psl = (num_ps if gi % 2 == 0 else ffn_ps)[:, 0:TGW]

            @blk.tensor
            def _(e, h=h, tg=tg, oc=oc, psl=psl, gi=gi):
                if oc == 0 and h == 0:
                    wge(e, dGP, k_x[tg])
                if gi >= 2:
                    wge(e, sS, k_pe[gi - 2])
                for dc in range(DC):
                    i = e.matmul(psl, wwt[:, h % 2, dc, oc * 128:(oc + 1) * 128],
                                 xnb[:, tg % 2, dc, :], start=(dc == 0), stop=(dc == DC - 1))
                k_tm[(h, tg)] = sT.inc(i)

            @blk.scalar
            def _(e, og=og, tg=tg, psl=psl, gi=gi, h=h):
                if gi == 0:
                    wge(e, dSY, D0)
                wge(e, sT, k_tm[(h, tg)])
                k_evac(e, og, tg, psl)
                k_pe[gi] = sS.n

    emit_kproj_w()

    TPG = TGW // 128
    k_x_at, k_h_at = {}, {}
    for tg in range(NTG):
        for tt in range(TPG):
            k_x_at.setdefault(tg * TPG + tt + 1, []).append((tg, tt))
        base = (tg + 1) * TPG
        for h in range(NH):
            k_h_at.setdefault(base + 2 + h, []).append((tg, h))

    def round_v(tc):
        @blk.vector
        def _(e, tc=tc):
            wge(e, sS, cast1_mark[tc])
            if tc > 0:
                wge(e, sT, tql_mark[tc - 1])
            p = tc % 2
            i = round_chain(e, ub[:, p, :], cb[:, p, :], qtmp[:, :])
            sV.inc(i)

    def round_stage(tc):
        @blk.tensor
        def _(e, tc=tc):
            wge(e, sV)
            if tc > 0:
                wge(e, sS, ev_S[(tc - 1, 1)])
            for dc in range(DC):
                i = e.transpose(tp_ps[:, dc * 128:(dc + 1) * 128],
                                qtmp[:, dc * 128:(dc + 1) * 128], idb_s[:, :])
            tql_mark[tc] = sT.inc(i)

        @blk.scalar
        def _(e, tc=tc):
            wge(e, sT)
            if tc > 0:
                wge(e, dSC, ql_dma[tc - 1])
            i = e.copy(stg2[:, 1, :], tp_ps[:, :])
            ev_S[(tc, 0)] = sS.inc(i)
            i = e.dma_start(out=ql_sp.ap()[:, :, tc * 128:(tc + 1) * 128].rearrange("c p t -> p c t"),
                            in_=stg2[:, 1, :].rearrange("p (c t) -> p c t", c=DC))
            ql_dma[tc] = dSC.inc(i, 16)

        @blk.tensor
        def _(e, tc=tc):
            wge(e, sS, ev_S[(tc, 0)])
            for dc in range(DC):
                i = e.transpose(tp_ps[:, dc * 128:(dc + 1) * 128],
                                qtmp2[:, dc * 128:(dc + 1) * 128], idb_s[:, :])
            txn_mark[tc] = sT.inc(i)

        @blk.scalar
        def _(e, tc=tc):
            wge(e, sT)
            if tc > 0:
                wge(e, dSC, xn_dma[tc - 1])
            i = e.copy(stg2[:, 0, :], tp_ps[:, :])
            ev_S[(tc, 1)] = sS.inc(i)
            i = e.dma_start(out=xn_sp.ap()[:, :, tc * 128:(tc + 1) * 128].rearrange("c p t -> p c t"),
                            in_=stg2[:, 0, :].rearrange("p (c t) -> p c t", c=DC))
            xn_dma[tc] = dSC.inc(i, 16)

    for tc in range(NT):
        @blk.sync
        def _(e, tc=tc):
            if tc >= 2:
                wge(e, sS, xn_mark[tc - 2])
            i = e.dma_start(out=xstr[:, tc % 2, :], in_=x_in.ap()[tc * 128:(tc + 1) * 128, :])
            xl_mark[tc] = dXL.inc(i, 16)

        @blk.vector
        def _(e, tc=tc):
            wge(e, dXL, xl_mark[tc])
            xv = xstr[:, tc % 2, :]
            xcv = xv
            i = e.reduce_sum(sm2[:, :], xv, axis=AX.X)
            wge(e, sV, sV.inc(i))
            i = e.tensor_scalar(xcv, xv, float(D), sm2[:, 0:1],
                                ALU.mult, ALU.subtract)
            xcp_mark[tc] = sV.inc(i)
            i = e.reduce_max(sm2[:, :], xcv, axis=AX.X, apply_absolute_value=True)
            wge(e, sV, sV.inc(i))
            i = e.reciprocal(smr[:, tc % 2:tc % 2 + 1], sm2[:, :])
            wge(e, sV, sV.inc(i))
            i = e.tensor_scalar_mul(smr[:, tc % 2:tc % 2 + 1], smr[:, tc % 2:tc % 2 + 1], 127.0)
            wge(e, sV, sV.inc(i))

        @blk.scalar
        def _(e, tc=tc):
            wge(e, sV, xcp_mark[tc])
            wge(e, sG, G0)
            i = e.activation(outb[:, 0, :], xstr[:, tc % 2, :], AF.Square,
                             accum_out=sm3[:, 0:1])
            sq_mark[tc] = sS.inc(i)
            wge(e, sS, sq_mark[tc])
            i = e.activation(sm4[:, tc % 2:tc % 2 + 1], sm3[:, :], AF.Sqrt,
                             bias=c1em5[:, 0:1], scale=c_invD3[:, 0:1])
            sqrt_mark[tc] = sS.inc(i)

        @blk.vector
        def _(e, tc=tc):
            wge(e, sS, sqrt_mark[tc])
            i = e.reciprocal(sm4[:, tc % 2:tc % 2 + 1], sm4[:, tc % 2:tc % 2 + 1])
            r_mark[tc] = sV.inc(i)
            wge(e, sV, r_mark[tc])
            i = e.tensor_scalar(lsc[:, tc:tc + 1], sm2[:, :], sm4[:, tc % 2:tc % 2 + 1],
                                1.0 / 127.0, ALU.mult, ALU.mult)
            sV.inc(i)

        @blk.scalar
        def _(e, tc=tc):
            wge(e, sV, r_mark[tc])
            p = tc % 2
            e.activation(ub[:, p, :], xstr[:, p, :], AF.Copy, scale=smr[:, p:p + 1])
            e.copy(zi_t[:, :], ub[:, p, :])
            i = e.copy(cb[:, p, :], zi_t[:, :])
            cast1_mark[tc] = sS.inc(i)

        if tc > 0:
            round_v(tc - 1)
            round_stage(tc - 1)

        @blk.scalar
        def _(e, tc=tc):
            p = tc % 2
            if tc > 0:
                wge(e, sT, txn_mark[tc - 1])
            i = e.activation(qtmp2[:, :], xstr[:, p, :], AF.Copy, scale=sm4[:, p:p + 1])
            xn_mark[tc] = sS.inc(i)

        for tg, tt in k_x_at.get(tc, []):
            emit_kproj_x(tg, tt)
        for tg, h in k_h_at.get(tc, []):
            emit_kproj_h(tg, h)

    round_v(NT - 1)
    round_stage(NT - 1)
    for t in range(NT, NT + TPG + NH + 3):
        for tg, tt in k_x_at.get(t, []):
            emit_kproj_x(tg, tt)
        for tg, h in k_h_at.get(t, []):
            emit_kproj_h(tg, h)

    def fm_projection(w_dram, evac, head_wait=None, pre_w=None, pre_x0=None):
        units = [(h, tg) for h in range(NH) for tg in range(NTG)]
        t_marks = {}
        pe_marks = {}
        w_marks = {}
        x_marks = {}
        for ui, (h, tg) in enumerate(units):
            wb, xb = h % 2, ui % 2
            if ui == 0:
                if pre_w is not None:
                    for hh in range(NH):
                        w_marks[hh] = pre_w[hh]
                else:
                    @blk.gpsimd
                    def _(e, h=h):
                        wge(e, sT)
                        w_marks[h] = w_load(e, w_dram, h, h % 2)
                if pre_x0 is not None:
                    x_marks[0] = pre_x0
                else:
                    @blk.sync
                    def _(e, tg=tg, xb=xb):
                        wge(e, sT)
                        wge(e, dSC, xn_dma[NT - 1])
                        x_marks[0] = x_load(e, tg, xb)
            if ui + 1 < len(units):
                h2, tg2 = units[ui + 1]
                if h2 != h and pre_w is None:
                    @blk.gpsimd
                    def _(e, h2=h2):
                        if h2 >= 2:
                            wge(e, sT, t_marks[(h2 - 2, NTG - 1)])
                        w_marks[h2] = w_load(e, w_dram, h2, h2 % 2)
                @blk.sync
                def _(e, ui=ui, tg2=tg2, xb2=((ui + 1) % 2)):
                    if ui >= 1:
                        wge(e, sT, t_marks[units[ui - 1]])
                    x_marks[ui + 1] = x_load(e, tg2, xb2)
            for oc in range(WW // 128):
                og = h * (WW // 128) + oc
                psl = (num_ps if oc % 2 == 0 else ffn_ps)[:, 0:TGW]
                @blk.tensor
                def _(e, h=h, tg=tg, oc=oc, og=og, psl=psl, wb=wb, xb=xb, ui=ui):
                    if oc == 0:
                        wge(e, dSY, x_marks[ui])
                        wge(e, dGP, w_marks[h])
                    gi = ui * (WW // 128) + oc
                    if gi == 0:
                        if head_wait is None:
                            wge(e, sS)
                            wge(e, sV)
                        else:
                            head_wait(e)
                    elif gi >= 2:
                        wge(e, sS, pe_marks[gi - 2])
                    for dc in range(DC):
                        i = e.matmul(psl, wwt[:, h % 2, dc, oc * 128:(oc + 1) * 128],
                                     xnb[:, xb, dc, :], start=(dc == 0), stop=(dc == DC - 1))
                    t_marks[(h, tg)] = sT.inc(i)
                @blk.scalar
                def _(e, og=og, tg=tg, psl=psl, ui=ui, oc=oc):
                    wge(e, sT)
                    evac(e, og, tg, psl)
                    pe_marks[ui * (WW // 128) + oc] = sS.n
        return t_marks

    def tm_projection(w_dram, b_row_s, evac, w0_wait=None, x0_wait=None,
                      w1_wait=None, x1_wait=None, pre_w0=None, psum_wait=None):
        units = [(h, tg) for h in range(NH) for tg in range(NTG)]
        t_marks = {}
        pe_marks = {}
        w_marks = {}
        x_marks = {}
        for ui, (h, tg) in enumerate(units):
            wb, xb = h % 2, ui % 2
            if ui == 0:
                if pre_w0 is not None:
                    w_marks[h] = pre_w0
                else:
                    @blk.gpsimd
                    def _(e, h=h):
                        if w0_wait is not None:
                            wge(e, sT, w0_wait)
                        else:
                            wge(e, sT)
                        w_marks[h] = w_load(e, w_dram, h, h % 2)
                @blk.sync
                def _(e, tg=tg, xb=xb):
                    if x0_wait is not None:
                        wge(e, sT, x0_wait)
                    else:
                        wge(e, sT)
                    wge(e, dSC, xn_dma[NT - 1])
                    x_marks[0] = x_load(e, tg, xb)
            if ui + 1 < len(units):
                h2, tg2 = units[ui + 1]
                if h2 != h:
                    @blk.gpsimd
                    def _(e, h2=h2):
                        if h2 == 1 and w1_wait is not None:
                            wge(e, sT, w1_wait)
                        if h2 >= 2:
                            wge(e, sT, t_marks[(h2 - 2, NTG - 1)])
                        w_marks[h2] = w_load(e, w_dram, h2, h2 % 2)
                @blk.sync
                def _(e, ui=ui, tg2=tg2):
                    if ui == 0 and x1_wait is not None:
                        wge(e, sT, x1_wait)
                    if ui >= 1:
                        wge(e, sT, t_marks[units[ui - 1]])
                    x_marks[ui + 1] = x_load(e, tg2, (ui + 1) % 2)
            for tw in range(TGW // 128):
                tc = tg * (TGW // 128) + tw
                gi = ui * (TGW // 128) + tw
                psl = (num_ps if tc % 2 == 0 else ffn_ps)[:, 0:WW]
                @blk.tensor
                def _(e, h=h, tg=tg, tw=tw, psl=psl, wb=wb, xb=xb, ui=ui, gi=gi):
                    if tw == 0:
                        wge(e, dSY, x_marks[ui])
                        wge(e, dGP, w_marks[h])
                        wge(e, sG)
                    if gi >= 2:
                        wge(e, sS, pe_marks[gi - 2])
                    elif psum_wait is not None:
                        wge(e, sS, psum_wait)
                    else:
                        wge(e, sS)
                    for dc in range(DC):
                        e.matmul(psl, xnb[:, xb, dc, tw * 128:(tw + 1) * 128],
                                 wwt[:, h % 2, dc, :], start=(dc == 0), stop=False)
                    i = e.matmul(psl, ones_rb[:, :], b_row_s[:, h * WW:(h + 1) * WW],
                                 start=False, stop=True)
                    t_marks[(h, tg)] = sT.inc(i)
                @blk.scalar
                def _(e, tc=tc, h=h, psl=psl, gi=gi):
                    wge(e, sT)
                    evac(e, tc, h, psl)
                    pe_marks[gi] = sS.n
        return t_marks, pe_marks

    rs_dma = []

    def emit_rs_spill():
        @blk.scalar
        def _(e):
            wge(e, sV)
            i = e.dma_start(out=rsc_dr.ap().rearrange("o (c p) -> p (o c)", p=128),
                            in_=rsc_fm[:, :], allow_slow_non_contiguous=True)
            dSC.inc(i, 16)
            wge(e, dSC)
            i = e.dma_start(out=rs_row, in_=rsc_dr.ap())
            rs_dma.append(dSC.inc(i, 16))

    p1_V_end = sV.n
    emit_rhs_quant()
    emit_rs_spill()


    arena_la = arena.ap()[:, 0:NT * D].rearrange("p (c d) -> p c d", c=NT)
    @blk.gpsimd
    def _(e):
        wge(e, sT, txn_mark[NT - 1])
        i = e.dma_start(out=b_row, in_=ba_row.ap())
        dGP.inc(i, 16)

    def a_evac(e, tc, h, psl):
        i = e.activation(arena_la[:, tc, h * WW:(h + 1) * WW], psl, AF.Sigmoid)
        sS.inc(i)
    k_done = k_tm[(NH - 1, NTG - 1)]
    a_tm, a_pe = tm_projection(wat, b_row, a_evac,
                               w0_wait=k_tm[(0, NTG - 1)],
                               x0_wait=k_tm[(NH - 1, max(0, NTG - 2))],
                               w1_wait=k_done, x1_wait=k_done,
                               psum_wait=k_pe[NTG * NH * (WW // 128) - 1])

    for tc in range(NT):
        @blk.scalar
        def _(e, tc=tc):
            i = e.activation(arena_la[:, tc, 0:D], arena_la[:, tc, 0:D], AF.Ln,
                             bias=c1em6[:, 0:1])
            sS.inc(i)

    a_done = a_tm[(NH - 1, NTG - 1)]
    v_w0m = []

    @blk.gpsimd
    def _(e):
        wge(e, sT, a_tm[(0, NTG - 1)])
        v_w0m.append(w_load(e, wvt, 0, 0))

    @blk.gpsimd
    def _(e):
        wge(e, sT, a_done)
        i = e.dma_start(out=b_row, in_=bv_row.ap())
        dGP.inc(i, 16)

    def v_evac(e, tc, h, psl):
        i = e.activation(v_aug[:, tc, h * WW:(h + 1) * WW], psl, AF.Copy)
        sS.inc(i)
    tm_projection(wvt, b_row, v_evac, pre_w0=v_w0m[0],
                  x0_wait=a_tm[(NH - 1, max(0, NTG - 2))],
                  w1_wait=a_done, x1_wait=a_done,
                  psum_wait=a_pe[NH * NTG * (TGW // 128) - 1])

    @blk.vector
    def _(e):
        wge(e, sG)
        for tc in range(NT):
            i = e.memset(v_aug[:, tc, D:D + 1], 1.0)
            sV.inc(i)

    q_w0, q_x0 = [], []

    @blk.gpsimd
    def _(e):
        wge(e, sT)
        for h in range(NH):
            q_w0.append(w_load(e, wqt, h, h % 2))

    @blk.sync
    def _(e):
        wge(e, sT)
        q_x0.append(x_load(e, 0, 0))


    @blk.tensor
    def _(e):
        wge(e, sS)
        wge(e, sG)
        for dc in range(DC):
            for tc in range(NT):
                i = e.matmul(den_ps[:, dc * NT + tc:dc * NT + tc + 1],
                             arena_la[:, tc, dc * 128:(dc + 1) * 128], ones_c[:, :],
                             start=True, stop=True)
        sT.inc(i)

    @blk.vector
    def _(e):
        wge(e, sT)
        i = e.tensor_copy(totals[:, :, :],
                          den_ps[:, 0:DC * NT].rearrange("p (c t) -> p c t", c=DC))
        wge(e, sV, sV.inc(i))
        i = e.memset(tails[:, :, NT - 1:NT], 0.0)
        wge(e, sV, sV.inc(i))
        for c in range(NT - 2, -1, -1):
            i = e.tensor_add(tails[:, :, c:c + 1], tails[:, :, c + 1:c + 2],
                             totals[:, :, c + 1:c + 2])
            wge(e, sV, sV.inc(i))
        i = e.tensor_add(ctd_s[:, :], tails[:, :, 0], totals[:, :, 0])
        sV.inc(i)

    @blk.gpsimd
    def _(e):
        wge(e, sV)
        i = e.dma_start(out=cc1_in.ap().rearrange("o (c p) -> p (o c)", p=128),
                        in_=ctd_s[:, :], allow_slow_non_contiguous=True)
        dGP.inc(i, 16)
        wge(e, dGP)
        i = e.collective_compute(
            "AllGather", ALU.bypass,
            replica_groups=[[g * NR + r for r in range(NR)] for g in range(NCORES // NR)],
            ins=[cc1_in.ap().opt()], outs=[cc1_out.ap().opt()])
        sCC.inc(i)

    def emit_rs_bcast():
        @blk.gpsimd
        def _(e):
            i = e.memset(ct_aug, 1.0)
            sG.inc(i)

        @blk.tensor
        def _(e):
            wge(e, dSC, rs_dma[0])
            wge(e, sG)
            for (a, b) in PCS:
                i = e.matmul(num_ps[:, a:b], ub.ap()[0:1, 0, 0:128], rs_row[:, a:b],
                             start=True, stop=True)
            sT.inc(i)

        @blk.vector
        def _(e):
            wge(e, sT)
            i = e.tensor_copy(rs_bc[:, :], num_ps[:, :])
            sV.inc(i)


    @blk.scalar
    def _(e):
        wge(e, sV)
        i = e.activation(etl[:, :, :], tails[:, :, :], AF.Exp)
        sS.inc(i)
    GW = max(g for g in (4, 2, 1) if NT % g == 0 and g * 128 <= D)
    groups = [(dc, tc0) for dc in range(DC) for tc0 in range(0, NT, GW)]
    exp_S = {}
    for gi, (dc, tc0) in enumerate(groups):
        psl = (num_ps if gi % 2 == 0 else ffn_ps)
        @blk.tensor
        def _(e, dc=dc, tc0=tc0, psl=psl, gi=gi):
            if gi < 2:
                wge(e, sS)
            else:
                wge(e, sS, exp_S[gi - 2])
            for tw in range(GW):
                i = e.matmul(psl[:, tw * 128:(tw + 1) * 128],
                             arena_la[:, tc0 + tw, dc * 128:(dc + 1) * 128], tri_s[:, :],
                             start=True, stop=True)
            sT.inc(i)
        @blk.scalar
        def _(e, dc=dc, tc0=tc0, psl=psl, gi=gi):
            wge(e, sT)
            if gi >= 2:
                wge(e, sV)
            i = e.activation(gsc[:, gi % 2, 0:GW * 128], psl[:, 0:GW * 128], AF.Exp)
            exp_S[gi] = sS.inc(i)
        @blk.vector
        def _(e, dc=dc, tc0=tc0, gi=gi):
            wge(e, sS, exp_S[gi])
            for tw in range(GW):
                tc = tc0 + tw
                i = e.scalar_tensor_tensor(
                    wk_fm[:, dc, tc * 128:(tc + 1) * 128],
                    gsc[:, gi % 2, tw * 128:(tw + 1) * 128].rearrange("p a b -> p (a b)") if False else gsc[:, gi % 2, tw * 128:(tw + 1) * 128],
                    etl[:, dc, tc:tc + 1],
                    wk_fm[:, dc, tc * 128:(tc + 1) * 128],
                    ALU.mult, ALU.mult)
            sV.inc(i)

    arena_wt = arena.ap()[:, 0:NT * D].rearrange("p (c d) -> p c d", c=NT)
    for tc in range(NT):
        @blk.tensor
        def _(e, tc=tc):
            wge(e, sV)
            for dc in range(DC):
                i = e.transpose(tp_ps[:, dc * 128:(dc + 1) * 128],
                                wk_fm[:, dc, tc * 128:(tc + 1) * 128], idb_s[:, :])
            sT.inc(i)

        @blk.vector
        def _(e, tc=tc):
            wge(e, sT)
            i = e.tensor_copy(arena_wt[:, tc, :], tp_ps[:, :])
            sV.inc(i)

    m_pieces = PCS + [(D, EW)]
    m_units = [(dc, a, b) for dc in range(DC) for (a, b) in m_pieces]
    mV, mG = {}, {}
    for mi, (dc, a, b) in enumerate(m_units):
        w = b - a
        pb = mi % 2
        mps = (num_ps if pb == 0 else ffn_ps)
        @blk.tensor
        def _(e, dc=dc, a=a, b=b, w=w, mi=mi, pb=pb, mps=mps):
            if mi == 0:
                wge(e, sV)
                wge(e, sS)
            if mi >= 2:
                wge(e, sV, mV[mi - 2])
            for tc in range(NT):
                i = e.matmul(mps[:, 0:w],
                             arena_wt[:, tc, dc * 128:(dc + 1) * 128],
                             v_aug[:, tc, a:b], start=(tc == 0), stop=(tc == NT - 1))
            sT.inc(i)

        @blk.vector
        def _(e, w=w, mi=mi, pb=pb, mps=mps):
            wge(e, sT)
            if mi >= 2:
                wge(e, dGP, mG[mi - 2])
            i = e.tensor_copy(msb[:, pb, 0:w], mps[:, 0:w])
            mV[mi] = sV.inc(i)

        @blk.gpsimd
        def _(e, dc=dc, a=a, b=b, w=w, mi=mi, pb=pb):
            wge(e, sV, mV[mi])
            i = e.dma_start(out=cc2_in.ap()[dc * 128:(dc + 1) * 128, a:b],
                            in_=msb[:, pb, 0:w])
            mG[mi] = dGP.inc(i, 16)
            if mi == DC // 2 * len(m_pieces) - 1:
                wge(e, dGP)
                i = e.collective_compute(
                    "AllGather", ALU.bypass,
                    replica_groups=[[g * NR + r for r in range(NR)]
                                    for g in range(NCORES // NR)],
                    ins=[cc2_in.ap()[0:HROWS, :].opt()], outs=[cc2a_out.ap().opt()])
                sCC.inc(i)

    p2g_T = sT.n

    emit_rs_bcast()

    @blk.gpsimd
    def _(e):
        wge(e, dGP, mG[DC * len(m_pieces) - 1])
        i = e.collective_compute(
            "AllGather", ALU.bypass,
            replica_groups=[[g * NR + r for r in range(NR)] for g in range(NCORES // NR)],
            ins=[cc2_in.ap()[HROWS:2 * HROWS, :].opt()], outs=[cc2b_out.ap().opt()])
        sCC.inc(i)


    @blk.sync
    def _(e):
        wge(e, sCC, 1)
        wge(e, sG)
        wge(e, sV)
        i = e.dma_start(out=ct_aug[0:NR, :], in_=cc1_out.ap())
        dSY.inc(i, 16)

    @blk.tensor
    def _(e):
        wge(e, dSY)
        wge(e, sG)
        wge(e, sS)
        wge(e, sV)
        for (a, b) in PCS:
            i = e.matmul(num_ps[0:5, a:b], pm_s[:, :], ct_aug[:, a:b],
                         start=True, stop=True)
        sT.inc(i)

    @blk.scalar
    def _(e):
        wge(e, sT)
        i = e.activation(sc_exp, num_ps[0:5, :], AF.Exp)
        sS.inc(i)

    @blk.tensor
    def _(e):
        wge(e, sS)
        for dc in range(DC):
            i = e.transpose(sc_ps0[:, dc * 8:dc * 8 + 5],
                            sc_exp[:, dc * 128:(dc + 1) * 128], idf_s[:, :])
        sT.inc(i)

    @blk.vector
    def _(e):
        wge(e, sT)
        for dc in range(DC):
            i = e.tensor_copy(scl_fm[:, dc, :], sc_ps0[:, dc * 8:dc * 8 + 5])
        wge(e, sV, sV.inc(i))
        i = e.tensor_mul(bqs_s[:, :], bqf_s[:, :], scl_fm[:, :, 4])
        sV.inc(i)
    scales_V = sV.n

    q_sp_tg0 = []

    def q_evac(e, og, tg, psl):
        wge(e, dSC, max(0, dSC.n - 16))
        wge(e, sV, scales_V)
        i = e.activation(stg[:, tg % 2, :], psl, AF.Relu,
                         scale=scl_fm[:, og, 4:5], bias=bqs_s[:, og:og + 1])
        m = sS.inc(i)
        wge(e, sS, m)
        i = e.dma_start(out=qt_sp.ap()[og, :, tg * TGW:(tg + 1) * TGW],
                        in_=stg[:, tg % 2, :])
        dSC.inc(i, 16)
        if tg == 0 and og == DC - 1:
            q_sp_tg0.append(dSC.n)
    q_tm = fm_projection(wqt, q_evac, pre_w=q_w0, pre_x0=q_x0[0])

    HC = DC // 2
    arena_h = arena.ap()[:, 0:2 * HC * EW].rearrange("p (b c w) -> p b c w", b=2, c=HC)
    jh_G, jh_V = {}, {}
    units_jh = [(j, h) for h in range(2) for j in range(NR)]
    for ui, (j, h) in enumerate(units_jh):
        pb = ui % 2
        @blk.gpsimd
        def _(e, j=j, h=h, ui=ui, pb=pb):
            if ui == 0:
                wge(e, sCC, 2)
                wge(e, sV)
            if ui == 1:
                wge(e, sT, p2g_T)
            if ui == NR:
                wge(e, sCC, 3)
            if ui >= 2:
                wge(e, sV, jh_V[ui - 2])
            cco = cc2a_out if h == 0 else cc2b_out
            i = e.dma_start(
                out=arena_h[:, pb, :, 0:EW],
                in_=cco.ap()[j * HROWS:(j + 1) * HROWS, :]
                .rearrange("(c p) e -> p c e", p=128))
            jh_G[ui] = dGP.inc(i, 16)
        @blk.vector
        def _(e, j=j, h=h, ui=ui, pb=pb):
            wge(e, dGP, jh_G[ui])
            wge(e, sV, scales_V)
            for c in range(HC):
                dc = h * HC + c
                i = e.scalar_tensor_tensor(ptl[:, dc, :], arena_h[:, pb, c, 0:EW],
                                           scl_fm[:, dc, j:j + 1], ptl[:, dc, :],
                                           ALU.mult, ALU.add)
            jh_V[ui] = sV.inc(i)

    p3_T_end, p3_V_end, p3_S_end, p3_den, y_dma, qlb_dma = [], [], [], [], [], []
    p3_xl = {}
    for cq in range(NT):
        @blk.scalar
        def _(e, cq=cq):
            if cq < 2:
                wge(e, dSC, q_sp_tg0[0])
                wge(e, sT, q_tm[(0, NTG - 1)])
            elif cq < 4:
                wge(e, dSC)
            else:
                wge(e, sT, p3_T_end[cq - 4])
            i = e.dma_start(out=qtb_s(cq % 4)[:, :, :],
                            in_=qt_sp.ap()[:, :, cq * 128:(cq + 1) * 128]
                            .rearrange("c p t -> p c t"))
            dSC.inc(i, 16)

        @blk.gpsimd
        def _(e, cq=cq):
            if cq < 2:
                wge(e, dSC, q_sp_tg0[0])
                wge(e, sT, q_tm[(0, NTG - 1)])
            elif cq < 4:
                wge(e, dSC)
            else:
                wge(e, sT, p3_T_end[cq - 4])
            i = e.dma_start(out=qlb_s(cq % 4)[:, :, :],
                            in_=ql_sp.ap()[:, :, cq * 128:(cq + 1) * 128]
                            .rearrange("c p t -> p c t"))
            qlb_dma.append(dGP.inc(i, 16))

        @blk.sync
        def _(e, cq=cq):
            if cq >= 2:
                wge(e, sV, p3_den[cq - 2])
            i = e.dma_start(out=xstr[:, cq % 2, :],
                            in_=x_in.ap()[cq * 128:(cq + 1) * 128, :])
            p3_xl[cq] = dXL.inc(i, 16)

        sc_T, sc_V, use_T = {}, {}, {}

        def emit_score(ck):
            psl = (sc_ps0 if ck % 2 == 0 else sc_ps1)[:, :]
            @blk.tensor
            def _(e, cq=cq, ck=ck, psl=psl):
                if ck == 0:
                    wge(e, dSC)
                if ck >= 2:
                    wge(e, sV, sc_V[ck - 2])
                for dc in range(DC):
                    i = e.matmul(psl, wk_fm[:, dc, ck * 128:(ck + 1) * 128],
                                 qtb_s(cq % 4)[:, dc, :], start=(dc == 0), stop=(dc == DC - 1))
                sc_T[ck] = sT.inc(i)
            @blk.vector
            def _(e, cq=cq, ck=ck, psl=psl):
                wge(e, sT, sc_T[ck])
                if ck >= 4:
                    wge(e, sT, use_T[ck - 4])
                if ck == cq:
                    i = e.tensor_mul(scb[:, ck % 4, :], psl, triu_s[:, :])
                else:
                    i = e.tensor_copy(scb[:, ck % 4, :], psl)
                sc_V[ck] = sV.inc(i)

        emit_score(0)
        if cq >= 1:
            emit_score(1)
        @blk.tensor
        def _(e, cq=cq):
            if cq > 0:
                wge(e, sV, p3_den[cq - 1])
                wge(e, sS, p3_S_end[cq - 1])
            else:
                wge(e, sV)
            for dc in range(DC):
                for (a, b) in PCS:
                    e.matmul(num_ps[:, a:b], qtb_s(cq % 4)[:, dc, :], ptl[:, dc, a:b],
                             start=(dc == 0), stop=False)
                i = e.matmul(den_ps[:, 0:4], qtb_s(cq % 4)[:, dc, :], ptl[:, dc, D:EW],
                             start=(dc == 0), stop=False)
            sT.inc(i)
        for ck in range(cq + 1):
            if ck + 2 <= cq:
                emit_score(ck + 2)
            @blk.tensor
            def _(e, cq=cq, ck=ck):
                wge(e, sV, sc_V[ck])
                last = (ck == cq)
                for (a, b) in PCS:
                    e.matmul(num_ps[:, a:b], scb[:, ck % 4, :], v_aug[:, ck, a:b],
                             start=False, stop=(last and b == D))
                i = e.matmul(den_ps[:, 0:4], scb[:, ck % 4, :], v_aug[:, ck, D:EW],
                             start=False, stop=last)
                use_T[ck] = sT.inc(i)
        @blk.tensor
        def _(e, cq=cq):
            wge(e, dGP, qlb_dma[cq])
            for dc in range(DC):
                for (a, b) in PCS:
                    i = e.matmul(ffn_ps[:, a:b], qlb_s(cq % 4)[:, dc, :], qrhs[:, dc, a:b],
                                 start=(dc == 0), stop=(dc == DC - 1))
            p3_T_end.append(sT.inc(i))

        @blk.vector
        def _(e, cq=cq):
            wge(e, sT, p3_T_end[cq])
            wge(e, dXL, p3_xl[cq])
            i = e.tensor_scalar_add(sm2[:, :], den_ps[:, 0:1], 1e-6)
            wge(e, sV, sV.inc(i))
            i = e.reciprocal(sm2[:, :], sm2[:, :])
            wge(e, sV, sV.inc(i))
            if cq > 0:
                wge(e, dSY, y_dma[cq - 1])
            i = e.scalar_tensor_tensor(outb[:, 0, :], num_ps[:, :], sm2[:, 0:1],
                                       xstr[:, cq % 2, :], ALU.mult, ALU.add)
            p3_den.append(sV.inc(i))

        @blk.scalar
        def _(e, cq=cq):
            wge(e, sT, p3_T_end[cq])
            i = e.activation(ub[:, 0, :], ffn_ps[:, :], AF.Copy, scale=lsc[:, cq:cq + 1])
            p3_S_end.append(sS.inc(i))

        @blk.vector
        def _(e, cq=cq):
            wge(e, sS, p3_S_end[cq])
            e.tensor_mul(ub[:, 1, :], ub[:, 0, :], rs_bc[:, :])
            i = e.tensor_add(outb[:, 0, :], outb[:, 0, :], ub[:, 1, :])
            p3_V_end.append(sV.inc(i))

        @blk.sync
        def _(e, cq=cq):
            wge(e, sV, p3_V_end[cq])
            i = e.dma_start(out=y_out.ap()[cq * 128:(cq + 1) * 128, :],
                            in_=outb[:, 0, :])
            y_dma.append(dSY.inc(i, 16))

    @blk.sync
    def _(e):
        wge(e, dSY)

    ctx.close()
    _merge_engine_blocks(nc)
    return nc


_CACHE = {}


def _get_nc(T, D, NR, NCORES):
    key = (T, D, NR, NCORES)
    if key not in _CACHE:
        _CACHE[key] = build_nc(T, D, NR, NCORES)
    return _CACHE[key]


def make_in_maps(x2d, W_q, b_q, W_k, b_k, W_v, b_v, W_a, b_a, rhs,
                 n_cores=8, NR=4):
    import ml_dtypes
    bf16 = ml_dtypes.bfloat16
    TT, D = x2d.shape
    T = TT // n_cores
    DC = D // 128
    f = np.float32
    shared = dict(
        wqt=np.ascontiguousarray(np.asarray(W_q, dtype=f).T),
        wkt=np.ascontiguousarray(np.asarray(W_k, dtype=f).T),
        wvt=np.ascontiguousarray(np.asarray(W_v, dtype=f).T),
        wat=np.ascontiguousarray(np.asarray(W_a, dtype=f).T),
        rhst=np.ascontiguousarray(np.asarray(rhs, dtype=f).T),
        bq_fm=np.ascontiguousarray(np.asarray(b_q, dtype=f).reshape(DC, 128).T),
        bk_fm=np.ascontiguousarray(np.asarray(b_k, dtype=f).reshape(DC, 128).T),
        ba_row=np.ascontiguousarray(np.asarray(b_a, dtype=f).reshape(1, D)),
        bv_row=np.ascontiguousarray(np.asarray(b_v, dtype=f).reshape(1, D)),
        tri_lo=np.tril(np.ones((128, 128))).astype(bf16),
        triu_m=np.triu(np.ones((128, 128))).astype(f),
        ident_f=np.eye(128, dtype=f),
        ident_b=np.eye(128).astype(bf16),
    )
    in_maps = []
    for c in range(n_cores):
        R = c % NR
        pm = np.zeros((5, 5), dtype=f)
        for kp in range(4):
            for j in range(4):
                if j < kp <= R:
                    pm[kp, j] = 1.0
            pm[kp, 4] = 1.0 if kp > R else 0.0
        for j in range(4):
            if j >= R:
                pm[4, j] = BIG_NEG
        m = dict(shared)
        m["x"] = np.ascontiguousarray(x2d[c * T:(c + 1) * T], dtype=f)
        m["pm"] = pm
        in_maps.append(m)
    return in_maps


def kernel(L0_attn_x_seq, L0_attn_W_q, L0_attn_b_q, L0_attn_W_k, L0_attn_b_k,
           L0_attn_W_v, L0_attn_b_v, L0_attn_W_a, L0_attn_b_a, L0_ffn_rhs):
    B, Tseq, D = L0_attn_x_seq.shape
    n_cores = 8
    NR = n_cores // B
    x2d = np.asarray(L0_attn_x_seq, dtype=np.float32).reshape(B * Tseq, D)
    T = (B * Tseq) // n_cores
    nc = _get_nc(T, D, NR, n_cores)
    in_maps = make_in_maps(
        x2d,
        L0_attn_W_q, L0_attn_b_q, L0_attn_W_k, L0_attn_b_k,
        L0_attn_W_v, L0_attn_b_v, L0_attn_W_a, L0_attn_b_a,
        L0_ffn_rhs, n_cores=n_cores, NR=NR)
    res = run_bass_kernel_spmd(nc, in_maps, core_ids=list(range(n_cores)))
    y = np.concatenate([r["y"] for r in res.results], axis=0)
    return np.ascontiguousarray(y.reshape(B, Tseq, D), dtype=np.float32)


# version 42
# speedup vs baseline: 1.0925x; 1.0190x over previous
import numpy as np
from contextlib import ExitStack

import concourse.bass as bass
import concourse.mybir as mybir
from concourse.bass_utils import run_bass_kernel_spmd

F32 = mybir.dt.float32
BF16 = mybir.dt.bfloat16
I32 = mybir.dt.int32
AF = mybir.ActivationFunctionType
ALU = mybir.AluOpType
AX = mybir.AxisListType

BIG_NEG = -1.0e9



def _merge_engine_blocks(nc):
    f = nc.m.functions[0]
    blocks = list(f.blocks)
    if len(blocks) <= 3:
        return
    by_name = {b.name: b for b in blocks}
    head, tail = blocks[0], blocks[-1]
    heads = [i.target for i in head.instructions
             if type(i).__name__ == "InstUnconditionalBranch"]
    new_blocks = [head]
    consumed = set()
    for h in heads:
        merged, cur, last_branch = [], h, None
        first = by_name[h]
        while cur != tail.name:
            b = by_name[cur]
            consumed.add(cur)
            nxt = None
            for i in b.instructions:
                if type(i).__name__ == "InstUnconditionalBranch":
                    nxt = i.target
                    last_branch = i
                else:
                    merged.append(i)
            cur = nxt
        merged.append(last_branch)
        first.instructions = merged
        new_blocks.append(first)
    for b in blocks[1:-1]:
        if b.name not in consumed and b not in new_blocks:
            new_blocks.append(b)
    new_blocks.append(tail)
    f.blocks = new_blocks


class Sem:
    def __init__(self, nc, ctx, name):
        self.h = ctx.enter_context(nc.semaphore(name))
        self.n = 0

    def inc(self, inst, k=1):
        inst.then_inc(self.h, k)
        self.n += k
        return self.n


def _pieces(D, cap=512):
    out, o = [], 0
    while o < D:
        w = min(cap, D - o)
        out.append((o, o + w))
        o += w
    return out


def build_nc(T=2048, D=1024, NR=4, NCORES=8):
    DC = D // 128
    NT = T // 128
    EW = D + 4
    TGW = min(512, T, D)
    NTG = T // TGW
    NH = max(1, D // TGW)
    WW = D // NH
    PCS = _pieces(D)

    nc = bass.Bass("TRN2", target_bir_lowering=False, detect_race_conditions=False)
    dp = nc.declare_dram_parameter

    x_in = dp("x", [T, D], F32, isOutput=False)
    wqt = dp("wqt", [D, D], F32, isOutput=False)
    wkt = dp("wkt", [D, D], F32, isOutput=False)
    wvt = dp("wvt", [D, D], F32, isOutput=False)
    wat = dp("wat", [D, D], F32, isOutput=False)
    rhst = dp("rhst", [D, D], F32, isOutput=False)
    bq_fm = dp("bq_fm", [128, DC], F32, isOutput=False)
    bk_fm = dp("bk_fm", [128, DC], F32, isOutput=False)
    ba_row = dp("ba_row", [1, D], F32, isOutput=False)
    bv_row = dp("bv_row", [1, D], F32, isOutput=False)
    tri_lo = dp("tri_lo", [128, 128], BF16, isOutput=False)
    triu_m = dp("triu_m", [128, 128], F32, isOutput=False)
    ident_f = dp("ident_f", [128, 128], F32, isOutput=False)
    ident_b = dp("ident_b", [128, 128], BF16, isOutput=False)
    pm_in = dp("pm", [5, 5], F32, isOutput=False)
    y_out = dp("y", [T, D], F32, isOutput=True)

    xn_sp = nc.dram_tensor("xn_sp", [DC, 128, T], BF16)
    qt_sp = nc.dram_tensor("qt_sp", [DC, 128, T], BF16)
    ql_sp = nc.dram_tensor("ql_sp", [DC, 128, T], BF16)
    rsc_dr = nc.dram_tensor("rsc_dr", [1, D], F32)
    cc1_in = nc.dram_tensor("cc1_in", [1, D], F32)
    cc1_out = nc.dram_tensor("cc1_out", [NR, D], F32)
    cc2_in = nc.dram_tensor("cc2_in", [DC * 128, EW], BF16)
    HROWS = (DC // 2) * 128
    cc2a_out = nc.dram_tensor("cc2a_out", [NR * HROWS, EW], BF16)
    cc2b_out = nc.dram_tensor("cc2b_out", [NR * HROWS, EW], BF16)

    ctx = ExitStack()
    sb = lambda name, shape, dt: ctx.enter_context(nc.sbuf_tensor(name, shape, dt))
    pst = lambda name, shape, dt=F32: ctx.enter_context(nc.psum_tensor(name, shape, dt))

    wk_fm = sb("wk_fm", [128, DC, T], BF16)
    v_aug = sb("v_aug", [128, NT, EW], BF16)
    qrhs = sb("qrhs", [128, DC, D], BF16)
    ptl = sb("ptl", [128, DC, EW], BF16)
    arena = sb("arena", [128, max(NT * D, DC * EW)], BF16)
    wwt = sb("wwt", [128, 2, DC, WW], BF16)
    xstr = sb("xstr", [128, 2, D], F32)
    outb = sb("outb", [128, 1, D], F32)
    xnb = sb("xnb", [128, 2, DC, TGW], BF16)
    etl = sb("etl", [128, DC, NT], F32)
    stg2 = sb("stg2", [128, 2, DC * 128], BF16)
    ub = sb("ub", [128, 2, D], F32)
    cb = sb("cb", [128, 2, D], F32)
    zi_t = sb("zi_t", [128, D], I32)
    qtmp = sb("qtmp", [128, D], BF16)
    qtmp2 = sb("qtmp2", [128, D], BF16)
    tri_s = sb("tri_s", [128, 128], BF16)
    triu_s = sb("triu_s", [128, 128], F32)
    idf_s = sb("idf_s", [5, 5], F32)
    idb_s = sb("idb_s", [128, 128], BF16)
    ones_c = sb("ones_c", [128, 1], BF16)
    ones_rb = sb("ones_rb", [1, 128], BF16)
    bqf_s = sb("bqf_s", [128, DC], F32)
    bkf_s = sb("bkf_s", [128, DC], F32)
    bqs_s = sb("bqs_s", [128, DC], F32)
    totals = sb("totals", [128, DC, NT], F32)
    tails = sb("tails", [128, DC, NT], F32)
    ctd_s = sb("ctd_s", [128, DC], F32)
    pm_s = sb("pm_s", [5, 5], F32)
    scl_fm = sb("scl_fm", [128, DC, 5], F32)
    lsc = sb("lsc", [128, NT], F32)
    rsc_fm = sb("rsc_fm", [128, DC], F32)
    rs_bc = sb("rs_bc", [128, D], F32)
    sm1 = sb("sm1", [128, 8, 1], F32)
    sm2 = sb("sm2", [128, 1], F32)
    sm3 = sb("sm3", [128, 1], F32)
    sm4 = sb("sm4", [128, 2], F32)
    smr = sb("smr", [128, 2], F32)
    c1em5 = sb("c1em5", [128, 1], F32)
    c_invD3 = sb("c_invD3", [128, 1], F32)
    c1em6 = sb("c1em6", [128, 1], F32)
    ct_aug = ub.ap()[0:5, 0, :]
    stg2f = stg2.ap().rearrange("p b n -> p (b n)")
    if 2 * DC * 128 >= 2048 and DC * 128 >= TGW:
        stg = stg2.ap()[:, :, 0:TGW]
        msb = stg2f[:, 0:1040].rearrange("p (b n) -> p b n", b=2)
        gsc = stg2f[:, 0:2048].rearrange("p (b n) -> p b n", b=2)
        scb = stg2f[:, 0:512].rearrange("p (b n) -> p b n", b=4)
    else:
        stg = sb("stg_t", [128, 2, TGW], BF16).ap()[:, :, :]
        msb = sb("msb_t", [128, 2, 520], BF16).ap()[:, :, :]
        gsc = sb("gsc_t", [128, 2, 512], BF16).ap()[:, :, :]
        scb = sb("scb_t", [128, 4, 128], BF16).ap()[:, :, :]
    if WW >= 512:
        qtb_s = lambda s: wwt.ap()[:, s // 2, :, (s % 2) * 128:(s % 2) * 128 + 128]
        qlb_s = lambda s: wwt.ap()[:, s // 2, :, 256 + (s % 2) * 128:256 + (s % 2) * 128 + 128]
    else:
        _qtb_t = sb("qtb_t", [128, 4, DC, 128], BF16)
        _qlb_t = sb("qlb_t", [128, 4, DC, 128], BF16)
        qtb_s = lambda s: _qtb_t.ap()[:, s, :, :]
        qlb_s = lambda s: _qlb_t.ap()[:, s, :, :]
    sc_exp = ub.ap()[0:5, 1, :]
    rs_row = ub.ap()[0:1, 1, :]
    b_row = qtmp2.ap()[0:1, :]

    num_ps = pst("num_ps", [128, D], F32)
    ffn_ps = pst("ffn_ps", [128, D], F32)
    den_ps = pst("den_ps", [128, 128], F32)
    sc_ps0 = pst("sc_ps0", [128, 128], F32)
    sc_ps1 = pst("sc_ps1", [128, 128], F32)
    tp_ps = pst("tp_ps", [128, DC * 128], BF16)

    dSY = Sem(nc, ctx, "dSY")
    dXL = Sem(nc, ctx, "dXL")
    dSC = Sem(nc, ctx, "dSC")
    dGP = Sem(nc, ctx, "dGP")
    sCC = Sem(nc, ctx, "sCC")
    sT = Sem(nc, ctx, "sT")
    sV = Sem(nc, ctx, "sV")
    sS = Sem(nc, ctx, "sS")
    sG = Sem(nc, ctx, "sG")

    blk = ctx.enter_context(nc.Block())

    def wge(e, sem, n=None):
        v = sem.n if n is None else n
        if v > 0:
            e.wait_ge(sem.h, v)

    @blk.sync
    def _(e):
        for dst, src in ((tri_s, tri_lo), (triu_s, triu_m), (idb_s, ident_b), (bqf_s, bq_fm), (bkf_s, bk_fm),
                         (pm_s, pm_in)):
            i = e.dma_start(out=dst[:, :], in_=src.ap())
            dSY.inc(i, 16)
        i = e.dma_start(out=idf_s[:, :], in_=ident_f.ap()[0:5, 0:5])
        dSY.inc(i, 16)

    @blk.gpsimd
    def _(e):
        for t_, val in ((ones_c, 1.0), (ones_rb, 1.0)):
            i = e.memset(t_[:, :], val)
            sG.inc(i)
        i = e.memset(v_aug[:, :, :], 0.0)
        sG.inc(i)
        i = e.memset(ptl[:, :, :], 0.0)
        sG.inc(i)
        i = e.memset(c1em5[:, :], float(np.float32(1e-5)) * (2.0 ** 20))
        sG.inc(i)
        i = e.memset(c1em6[:, :], 1e-6)
        sG.inc(i)
        i = e.memset(c_invD3[:, :], 1.0 / float(D))
        sG.inc(i)

    def round_cast(e_s, u_ap, c_ap):
        e_s.copy(zi_t[:, :], u_ap)
        return e_s.copy(c_ap, zi_t[:, :])

    def round_chain(e, u_ap, c_ap, out_ap):
        e.tensor_sub(u_ap, u_ap, c_ap)
        e.scalar_tensor_tensor(c_ap, u_ap, 0.5, c_ap,
                               ALU.is_gt, ALU.add)
        e.tensor_single_scalar(u_ap, u_ap, -0.5, ALU.is_lt)
        return e.tensor_sub(out_ap, c_ap, u_ap)

    def emit_rhs_quant():
        rhs_dm, rhs_rel = {}, {}
        for jc in range(DC):
            @blk.sync
            def _(e, jc=jc):
                if jc < 2:
                    wge(e, sV, p1_V_end)
                    wge(e, sS, xn_mark[NT - 1])
                else:
                    wge(e, sS, rhs_rel[jc - 2])
                i = e.dma_start(out=xstr[:, jc % 2, :],
                                in_=rhst.ap()[jc * 128:(jc + 1) * 128, :])
                rhs_dm[jc] = dSY.inc(i, 16)

            @blk.vector
            def _(e, jc=jc):
                wge(e, dSY, rhs_dm[jc])
                i = e.reduce_max(sm1[:, :, :], xstr[:, jc % 2, :].rearrange("p (a b) -> p a b", a=8),
                                 axis=AX.X, apply_absolute_value=True)
                wge(e, sV, sV.inc(i))
                i = e.reduce_max(sm2[:, :], sm1[:, :, 0], axis=AX.X, apply_absolute_value=True)
                wge(e, sV, sV.inc(i))
                i = e.tensor_scalar_max(sm2[:, :], sm2[:, :], 1e-12)
                wge(e, sV, sV.inc(i))
                e.tensor_scalar_mul(rsc_fm[:, jc:jc + 1], sm2[:, :], 1.0 / 127.0)
                i = e.reciprocal(sm3[:, :], sm2[:, :])
                wge(e, sV, sV.inc(i))
                i = e.tensor_scalar_mul(sm3[:, :], sm3[:, :], 127.0)
                sV.inc(i)

            @blk.scalar
            def _(e, jc=jc):
                wge(e, sV)
                i = e.activation(ub[:, 0, :], xstr[:, jc % 2, :], AF.Copy,
                                 scale=sm3[:, 0:1])
                rhs_rel[jc] = sS.inc(i)
                i = round_cast(e, ub[:, 0, :], cb[:, 0, :])
                sS.inc(i)

            @blk.vector
            def _(e, jc=jc):
                wge(e, sS)
                if jc == 0:
                    wge(e, sT, tql_mark[NT - 1])
                i = round_chain(e, ub[:, 0, :], cb[:, 0, :], qtmp[:, :])
                sV.inc(i)

            @blk.tensor
            def _(e, jc=jc):
                wge(e, sV)
                wge(e, dSY)
                for dc in range(DC):
                    i = e.transpose(tp_ps[:, dc * 128:(dc + 1) * 128],
                                    qtmp[:, dc * 128:(dc + 1) * 128], idb_s[:, :])
                sT.inc(i)

            @blk.vector
            def _(e, jc=jc):
                wge(e, sT)
                for dc in range(DC):
                    i = e.tensor_copy(qrhs[:, dc, jc * 128:(jc + 1) * 128],
                                      tp_ps[:, dc * 128:(dc + 1) * 128])
                sV.inc(i)


    p1_V = []
    ev_S = {}
    ql_dma, xn_dma = {}, {}
    xcp_mark, sq_mark, xl_mark, sqrt_mark, r_mark = {}, {}, {}, {}, {}
    cast1_mark, G_mark, xn_mark, tql_mark, txn_mark = {}, {}, {}, {}, {}
    G0 = sG.n
    D0 = dSY.n

    def w_load(e, w_dram, h, wb):
        i = e.dma_start(out=wwt[:, wb, :, :],
                        in_=w_dram.ap()[:, h * WW:(h + 1) * WW]
                        .rearrange("(c p) o -> p c o", p=128))
        return dGP.inc(i, 16)

    def x_load(e, tg, xb):
        i = e.dma_start(out=xnb[:, xb, :, :],
                        in_=xn_sp.ap()[:, :, tg * TGW:(tg + 1) * TGW]
                        .rearrange("c p t -> p c t"))
        return dSY.inc(i, 16)

    def k_evac(e, og, tg, psl):
        i = e.activation(wk_fm[:, og, tg * TGW:(tg + 1) * TGW], psl, AF.Relu,
                         bias=bkf_s[:, og:og + 1])
        sS.inc(i)

    k_tm, k_pe, k_w, k_x = {}, {}, {}, {}

    def emit_kproj_w():
        @blk.gpsimd
        def _(e):
            for h in range(NH):
                k_w[h] = w_load(e, wkt, h, h % 2)

    def emit_kproj_x(tg, tt):
        @blk.gpsimd
        def _(e, tg=tg, tt=tt):
            z = tg * (TGW // 128) + tt
            wge(e, dSC, xn_dma[z])
            if tt == 0 and tg >= 2:
                wge(e, sT, k_tm[(NH - 1, tg - 2)])
            i = e.dma_start(out=xnb[:, tg % 2, :, tt * 128:(tt + 1) * 128],
                            in_=xn_sp.ap()[:, :, z * 128:(z + 1) * 128]
                            .rearrange("c p t -> p c t"))
            k_x[tg] = dGP.inc(i, 16)

    def emit_kproj_h(tg, h):
        for oc in range(WW // 128):
            og = h * (WW // 128) + oc
            gi = (tg * NH + h) * (WW // 128) + oc
            psl = (num_ps if gi % 2 == 0 else ffn_ps)[:, 0:TGW]

            @blk.tensor
            def _(e, h=h, tg=tg, oc=oc, psl=psl, gi=gi):
                if oc == 0 and h == 0:
                    wge(e, dGP, k_x[tg])
                if gi >= 2:
                    wge(e, sS, k_pe[gi - 2])
                for dc in range(DC):
                    i = e.matmul(psl, wwt[:, h % 2, dc, oc * 128:(oc + 1) * 128],
                                 xnb[:, tg % 2, dc, :], start=(dc == 0), stop=(dc == DC - 1))
                k_tm[(h, tg)] = sT.inc(i)

            @blk.scalar
            def _(e, og=og, tg=tg, psl=psl, gi=gi, h=h):
                if gi == 0:
                    wge(e, dSY, D0)
                wge(e, sT, k_tm[(h, tg)])
                k_evac(e, og, tg, psl)
                k_pe[gi] = sS.n

    emit_kproj_w()

    TPG = TGW // 128
    k_x_at, k_h_at = {}, {}
    for tg in range(NTG):
        for tt in range(TPG):
            k_x_at.setdefault(tg * TPG + tt + 1, []).append((tg, tt))
        base = (tg + 1) * TPG
        for h in range(NH):
            k_h_at.setdefault(base + 2 + h, []).append((tg, h))

    def round_v(tc):
        @blk.vector
        def _(e, tc=tc):
            wge(e, sS, cast1_mark[tc])
            if tc > 0:
                wge(e, sT, tql_mark[tc - 1])
            p = tc % 2
            i = round_chain(e, ub[:, p, :], cb[:, p, :], qtmp[:, :])
            sV.inc(i)

    def round_stage(tc):
        @blk.tensor
        def _(e, tc=tc):
            wge(e, sV)
            if tc > 0:
                wge(e, sS, ev_S[(tc - 1, 1)])
            for dc in range(DC):
                i = e.transpose(tp_ps[:, dc * 128:(dc + 1) * 128],
                                qtmp[:, dc * 128:(dc + 1) * 128], idb_s[:, :])
            tql_mark[tc] = sT.inc(i)

        @blk.scalar
        def _(e, tc=tc):
            wge(e, sT)
            if tc > 0:
                wge(e, dSC, ql_dma[tc - 1])
            i = e.copy(stg2[:, 1, :], tp_ps[:, :])
            ev_S[(tc, 0)] = sS.inc(i)
            i = e.dma_start(out=ql_sp.ap()[:, :, tc * 128:(tc + 1) * 128].rearrange("c p t -> p c t"),
                            in_=stg2[:, 1, :].rearrange("p (c t) -> p c t", c=DC))
            ql_dma[tc] = dSC.inc(i, 16)

        @blk.tensor
        def _(e, tc=tc):
            wge(e, sS, ev_S[(tc, 0)])
            for dc in range(DC):
                i = e.transpose(tp_ps[:, dc * 128:(dc + 1) * 128],
                                qtmp2[:, dc * 128:(dc + 1) * 128], idb_s[:, :])
            txn_mark[tc] = sT.inc(i)

        @blk.scalar
        def _(e, tc=tc):
            wge(e, sT)
            if tc > 0:
                wge(e, dSC, xn_dma[tc - 1])
            i = e.copy(stg2[:, 0, :], tp_ps[:, :])
            ev_S[(tc, 1)] = sS.inc(i)
            i = e.dma_start(out=xn_sp.ap()[:, :, tc * 128:(tc + 1) * 128].rearrange("c p t -> p c t"),
                            in_=stg2[:, 0, :].rearrange("p (c t) -> p c t", c=DC))
            xn_dma[tc] = dSC.inc(i, 16)

    for tc in range(NT):
        @blk.sync
        def _(e, tc=tc):
            if tc >= 2:
                wge(e, sS, xn_mark[tc - 2])
            i = e.dma_start(out=xstr[:, tc % 2, :], in_=x_in.ap()[tc * 128:(tc + 1) * 128, :])
            xl_mark[tc] = dXL.inc(i, 16)

        @blk.vector
        def _(e, tc=tc):
            wge(e, dXL, xl_mark[tc])
            xv = xstr[:, tc % 2, :]
            xcv = xv
            i = e.reduce_sum(sm2[:, :], xv, axis=AX.X)
            wge(e, sV, sV.inc(i))
            i = e.tensor_scalar(xcv, xv, float(D), sm2[:, 0:1],
                                ALU.mult, ALU.subtract)
            xcp_mark[tc] = sV.inc(i)
            i = e.reduce_max(sm2[:, :], xcv, axis=AX.X, apply_absolute_value=True)
            wge(e, sV, sV.inc(i))
            i = e.reciprocal(smr[:, tc % 2:tc % 2 + 1], sm2[:, :])
            wge(e, sV, sV.inc(i))
            i = e.tensor_scalar_mul(smr[:, tc % 2:tc % 2 + 1], smr[:, tc % 2:tc % 2 + 1], 127.0)
            wge(e, sV, sV.inc(i))

        @blk.scalar
        def _(e, tc=tc):
            wge(e, sV, xcp_mark[tc])
            wge(e, sG, G0)
            i = e.activation(outb[:, 0, :], xstr[:, tc % 2, :], AF.Square,
                             accum_out=sm3[:, 0:1])
            sq_mark[tc] = sS.inc(i)
            wge(e, sS, sq_mark[tc])
            i = e.activation(sm4[:, tc % 2:tc % 2 + 1], sm3[:, :], AF.Sqrt,
                             bias=c1em5[:, 0:1], scale=c_invD3[:, 0:1])
            sqrt_mark[tc] = sS.inc(i)

        @blk.vector
        def _(e, tc=tc):
            wge(e, sS, sqrt_mark[tc])
            i = e.reciprocal(sm4[:, tc % 2:tc % 2 + 1], sm4[:, tc % 2:tc % 2 + 1])
            r_mark[tc] = sV.inc(i)
            wge(e, sV, r_mark[tc])
            i = e.tensor_scalar(lsc[:, tc:tc + 1], sm2[:, :], sm4[:, tc % 2:tc % 2 + 1],
                                1.0 / 127.0, ALU.mult, ALU.mult)
            sV.inc(i)

        @blk.scalar
        def _(e, tc=tc):
            wge(e, sV, r_mark[tc])
            p = tc % 2
            e.activation(ub[:, p, :], xstr[:, p, :], AF.Copy, scale=smr[:, p:p + 1])
            e.copy(zi_t[:, :], ub[:, p, :])
            i = e.copy(cb[:, p, :], zi_t[:, :])
            cast1_mark[tc] = sS.inc(i)

        if tc > 0:
            round_v(tc - 1)
            round_stage(tc - 1)

        @blk.scalar
        def _(e, tc=tc):
            p = tc % 2
            if tc > 0:
                wge(e, sT, txn_mark[tc - 1])
            i = e.activation(qtmp2[:, :], xstr[:, p, :], AF.Copy, scale=sm4[:, p:p + 1])
            xn_mark[tc] = sS.inc(i)

        for tg, tt in k_x_at.get(tc, []):
            emit_kproj_x(tg, tt)
        for tg, h in k_h_at.get(tc, []):
            emit_kproj_h(tg, h)

    round_v(NT - 1)
    round_stage(NT - 1)
    for t in range(NT, NT + TPG + NH + 3):
        for tg, tt in k_x_at.get(t, []):
            emit_kproj_x(tg, tt)
        for tg, h in k_h_at.get(t, []):
            emit_kproj_h(tg, h)

    def fm_projection(w_dram, evac, head_wait=None, pre_w=None, pre_x0=None):
        units = [(h, tg) for tg in range(NTG) for h in range(NH)]
        t_marks = {}
        pe_marks = {}
        w_marks = {}
        x_marks = {}
        for ui, (h, tg) in enumerate(units):
            wb, xb = h % 2, tg % 2
            if ui == 0:
                if pre_w is not None:
                    for hh in range(NH):
                        w_marks[hh] = pre_w[hh]
                else:
                    @blk.gpsimd
                    def _(e, h=h):
                        wge(e, sT)
                        w_marks[h] = w_load(e, w_dram, h, h % 2)
                if pre_x0 is not None:
                    x_marks[0] = pre_x0
                else:
                    @blk.sync
                    def _(e, tg=tg, xb=xb):
                        wge(e, sT)
                        wge(e, dSC, xn_dma[NT - 1])
                        x_marks[0] = x_load(e, tg, xb)
                if NTG > 1:
                    @blk.sync
                    def _(e):
                        wge(e, sT)
                        wge(e, dSC, xn_dma[NT - 1])
                        x_marks[1] = x_load(e, 1, 1)
            if h == 0 and tg >= 1 and tg + 1 < NTG:
                @blk.sync
                def _(e, tg=tg):
                    wge(e, sT, t_marks[(NH - 1, tg - 1)])
                    x_marks[tg + 1] = x_load(e, tg + 1, (tg + 1) % 2)
            for oc in range(WW // 128):
                og = h * (WW // 128) + oc
                psl = (num_ps if oc % 2 == 0 else ffn_ps)[:, 0:TGW]
                @blk.tensor
                def _(e, h=h, tg=tg, oc=oc, og=og, psl=psl, wb=wb, xb=xb, ui=ui):
                    if oc == 0:
                        wge(e, dSY, x_marks[tg])
                        wge(e, dGP, w_marks[h])
                    gi = ui * (WW // 128) + oc
                    if gi == 0:
                        if head_wait is None:
                            wge(e, sS)
                            wge(e, sV)
                        else:
                            head_wait(e)
                    elif gi >= 2:
                        wge(e, sS, pe_marks[gi - 2])
                    for dc in range(DC):
                        i = e.matmul(psl, wwt[:, h % 2, dc, oc * 128:(oc + 1) * 128],
                                     xnb[:, xb, dc, :], start=(dc == 0), stop=(dc == DC - 1))
                    t_marks[(h, tg)] = sT.inc(i)
                @blk.scalar
                def _(e, og=og, tg=tg, psl=psl, ui=ui, oc=oc):
                    wge(e, sT)
                    evac(e, og, tg, psl)
                    pe_marks[ui * (WW // 128) + oc] = sS.n
        return t_marks

    def tm_projection(w_dram, b_row_s, evac, w0_wait=None, x0_wait=None,
                      w1_wait=None, x1_wait=None, pre_w0=None, psum_wait=None):
        units = [(h, tg) for h in range(NH) for tg in range(NTG)]
        t_marks = {}
        pe_marks = {}
        w_marks = {}
        x_marks = {}
        for ui, (h, tg) in enumerate(units):
            wb, xb = h % 2, ui % 2
            if ui == 0:
                if pre_w0 is not None:
                    w_marks[h] = pre_w0
                else:
                    @blk.gpsimd
                    def _(e, h=h):
                        if w0_wait is not None:
                            wge(e, sT, w0_wait)
                        else:
                            wge(e, sT)
                        w_marks[h] = w_load(e, w_dram, h, h % 2)
                @blk.sync
                def _(e, tg=tg, xb=xb):
                    if x0_wait is not None:
                        wge(e, sT, x0_wait)
                    else:
                        wge(e, sT)
                    wge(e, dSC, xn_dma[NT - 1])
                    x_marks[0] = x_load(e, tg, xb)
            if ui + 1 < len(units):
                h2, tg2 = units[ui + 1]
                if h2 != h:
                    @blk.gpsimd
                    def _(e, h2=h2):
                        if h2 == 1 and w1_wait is not None:
                            wge(e, sT, w1_wait)
                        if h2 >= 2:
                            wge(e, sT, t_marks[(h2 - 2, NTG - 1)])
                        w_marks[h2] = w_load(e, w_dram, h2, h2 % 2)
                @blk.sync
                def _(e, ui=ui, tg2=tg2):
                    if ui == 0 and x1_wait is not None:
                        wge(e, sT, x1_wait)
                    if ui >= 1:
                        wge(e, sT, t_marks[units[ui - 1]])
                    x_marks[ui + 1] = x_load(e, tg2, (ui + 1) % 2)
            for tw in range(TGW // 128):
                tc = tg * (TGW // 128) + tw
                gi = ui * (TGW // 128) + tw
                psl = (num_ps if tc % 2 == 0 else ffn_ps)[:, 0:WW]
                @blk.tensor
                def _(e, h=h, tg=tg, tw=tw, psl=psl, wb=wb, xb=xb, ui=ui, gi=gi):
                    if tw == 0:
                        wge(e, dSY, x_marks[ui])
                        wge(e, dGP, w_marks[h])
                        wge(e, sG)
                    if gi >= 2:
                        wge(e, sS, pe_marks[gi - 2])
                    elif psum_wait is not None:
                        wge(e, sS, psum_wait)
                    else:
                        wge(e, sS)
                    for dc in range(DC):
                        e.matmul(psl, xnb[:, xb, dc, tw * 128:(tw + 1) * 128],
                                 wwt[:, h % 2, dc, :], start=(dc == 0), stop=False)
                    i = e.matmul(psl, ones_rb[:, :], b_row_s[:, h * WW:(h + 1) * WW],
                                 start=False, stop=True)
                    t_marks[(h, tg)] = sT.inc(i)
                @blk.scalar
                def _(e, tc=tc, h=h, psl=psl, gi=gi):
                    wge(e, sT)
                    evac(e, tc, h, psl)
                    pe_marks[gi] = sS.n
        return t_marks, pe_marks

    rs_dma = []

    def emit_rs_spill():
        @blk.scalar
        def _(e):
            wge(e, sV)
            i = e.dma_start(out=rsc_dr.ap().rearrange("o (c p) -> p (o c)", p=128),
                            in_=rsc_fm[:, :], allow_slow_non_contiguous=True)
            dSC.inc(i, 16)
            wge(e, dSC)
            i = e.dma_start(out=rs_row, in_=rsc_dr.ap())
            rs_dma.append(dSC.inc(i, 16))

    p1_V_end = sV.n
    emit_rhs_quant()
    emit_rs_spill()


    arena_la = arena.ap()[:, 0:NT * D].rearrange("p (c d) -> p c d", c=NT)
    @blk.gpsimd
    def _(e):
        wge(e, sT, txn_mark[NT - 1])
        i = e.dma_start(out=b_row, in_=ba_row.ap())
        dGP.inc(i, 16)

    def a_evac(e, tc, h, psl):
        i = e.activation(arena_la[:, tc, h * WW:(h + 1) * WW], psl, AF.Sigmoid)
        sS.inc(i)
    k_done = k_tm[(NH - 1, NTG - 1)]
    a_tm, a_pe = tm_projection(wat, b_row, a_evac,
                               w0_wait=k_tm[(0, NTG - 1)],
                               x0_wait=k_tm[(NH - 1, max(0, NTG - 2))],
                               w1_wait=k_done, x1_wait=k_done,
                               psum_wait=k_pe[NTG * NH * (WW // 128) - 1])

    for tc in range(NT):
        @blk.scalar
        def _(e, tc=tc):
            i = e.activation(arena_la[:, tc, 0:D], arena_la[:, tc, 0:D], AF.Ln,
                             bias=c1em6[:, 0:1])
            sS.inc(i)

    a_done = a_tm[(NH - 1, NTG - 1)]
    v_w0m = []

    @blk.gpsimd
    def _(e):
        wge(e, sT, a_tm[(0, NTG - 1)])
        v_w0m.append(w_load(e, wvt, 0, 0))

    @blk.gpsimd
    def _(e):
        wge(e, sT, a_done)
        i = e.dma_start(out=b_row, in_=bv_row.ap())
        dGP.inc(i, 16)

    def v_evac(e, tc, h, psl):
        i = e.activation(v_aug[:, tc, h * WW:(h + 1) * WW], psl, AF.Copy)
        sS.inc(i)
    tm_projection(wvt, b_row, v_evac, pre_w0=v_w0m[0],
                  x0_wait=a_tm[(NH - 1, max(0, NTG - 2))],
                  w1_wait=a_done, x1_wait=a_done,
                  psum_wait=a_pe[NH * NTG * (TGW // 128) - 1])

    @blk.vector
    def _(e):
        wge(e, sG)
        for tc in range(NT):
            i = e.memset(v_aug[:, tc, D:D + 1], 1.0)
            sV.inc(i)

    q_w0, q_x0 = [], []

    @blk.gpsimd
    def _(e):
        wge(e, sT)
        for h in range(NH):
            q_w0.append(w_load(e, wqt, h, h % 2))

    @blk.sync
    def _(e):
        wge(e, sT)
        q_x0.append(x_load(e, 0, 0))


    @blk.tensor
    def _(e):
        wge(e, sS)
        wge(e, sG)
        for dc in range(DC):
            for tc in range(NT):
                i = e.matmul(den_ps[:, dc * NT + tc:dc * NT + tc + 1],
                             arena_la[:, tc, dc * 128:(dc + 1) * 128], ones_c[:, :],
                             start=True, stop=True)
        sT.inc(i)

    @blk.vector
    def _(e):
        wge(e, sT)
        i = e.tensor_copy(totals[:, :, :],
                          den_ps[:, 0:DC * NT].rearrange("p (c t) -> p c t", c=DC))
        wge(e, sV, sV.inc(i))
        i = e.memset(tails[:, :, NT - 1:NT], 0.0)
        wge(e, sV, sV.inc(i))
        for c in range(NT - 2, -1, -1):
            i = e.tensor_add(tails[:, :, c:c + 1], tails[:, :, c + 1:c + 2],
                             totals[:, :, c + 1:c + 2])
            wge(e, sV, sV.inc(i))
        i = e.tensor_add(ctd_s[:, :], tails[:, :, 0], totals[:, :, 0])
        sV.inc(i)

    @blk.gpsimd
    def _(e):
        wge(e, sV)
        i = e.dma_start(out=cc1_in.ap().rearrange("o (c p) -> p (o c)", p=128),
                        in_=ctd_s[:, :], allow_slow_non_contiguous=True)
        dGP.inc(i, 16)
        wge(e, dGP)
        i = e.collective_compute(
            "AllGather", ALU.bypass,
            replica_groups=[[g * NR + r for r in range(NR)] for g in range(NCORES // NR)],
            ins=[cc1_in.ap().opt()], outs=[cc1_out.ap().opt()])
        sCC.inc(i)

    def emit_rs_bcast():
        @blk.gpsimd
        def _(e):
            i = e.memset(ct_aug, 1.0)
            sG.inc(i)

        @blk.tensor
        def _(e):
            wge(e, dSC, rs_dma[0])
            wge(e, sG)
            for (a, b) in PCS:
                i = e.matmul(num_ps[:, a:b], ub.ap()[0:1, 0, 0:128], rs_row[:, a:b],
                             start=True, stop=True)
            sT.inc(i)

        @blk.vector
        def _(e):
            wge(e, sT)
            i = e.tensor_copy(rs_bc[:, :], num_ps[:, :])
            sV.inc(i)


    @blk.scalar
    def _(e):
        wge(e, sV)
        i = e.activation(etl[:, :, :], tails[:, :, :], AF.Exp)
        sS.inc(i)
    GW = max(g for g in (4, 2, 1) if NT % g == 0 and g * 128 <= D)
    groups = [(dc, tc0) for dc in range(DC) for tc0 in range(0, NT, GW)]
    exp_S = {}
    for gi, (dc, tc0) in enumerate(groups):
        psl = (num_ps if gi % 2 == 0 else ffn_ps)
        @blk.tensor
        def _(e, dc=dc, tc0=tc0, psl=psl, gi=gi):
            if gi < 2:
                wge(e, sS)
            else:
                wge(e, sS, exp_S[gi - 2])
            for tw in range(GW):
                i = e.matmul(psl[:, tw * 128:(tw + 1) * 128],
                             arena_la[:, tc0 + tw, dc * 128:(dc + 1) * 128], tri_s[:, :],
                             start=True, stop=True)
            sT.inc(i)
        @blk.scalar
        def _(e, dc=dc, tc0=tc0, psl=psl, gi=gi):
            wge(e, sT)
            if gi >= 2:
                wge(e, sV)
            i = e.activation(gsc[:, gi % 2, 0:GW * 128], psl[:, 0:GW * 128], AF.Exp)
            exp_S[gi] = sS.inc(i)
        @blk.vector
        def _(e, dc=dc, tc0=tc0, gi=gi):
            wge(e, sS, exp_S[gi])
            for tw in range(GW):
                tc = tc0 + tw
                i = e.scalar_tensor_tensor(
                    wk_fm[:, dc, tc * 128:(tc + 1) * 128],
                    gsc[:, gi % 2, tw * 128:(tw + 1) * 128].rearrange("p a b -> p (a b)") if False else gsc[:, gi % 2, tw * 128:(tw + 1) * 128],
                    etl[:, dc, tc:tc + 1],
                    wk_fm[:, dc, tc * 128:(tc + 1) * 128],
                    ALU.mult, ALU.mult)
            sV.inc(i)

    arena_wt = arena.ap()[:, 0:NT * D].rearrange("p (c d) -> p c d", c=NT)
    for tc in range(NT):
        @blk.tensor
        def _(e, tc=tc):
            wge(e, sV)
            for dc in range(DC):
                i = e.transpose(tp_ps[:, dc * 128:(dc + 1) * 128],
                                wk_fm[:, dc, tc * 128:(tc + 1) * 128], idb_s[:, :])
            sT.inc(i)

        @blk.vector
        def _(e, tc=tc):
            wge(e, sT)
            i = e.tensor_copy(arena_wt[:, tc, :], tp_ps[:, :])
            sV.inc(i)

    m_pieces = PCS + [(D, EW)]
    m_units = [(dc, a, b) for dc in range(DC) for (a, b) in m_pieces]
    mV, mG = {}, {}
    for mi, (dc, a, b) in enumerate(m_units):
        w = b - a
        pb = mi % 2
        mps = (num_ps if pb == 0 else ffn_ps)
        @blk.tensor
        def _(e, dc=dc, a=a, b=b, w=w, mi=mi, pb=pb, mps=mps):
            if mi == 0:
                wge(e, sV)
                wge(e, sS)
            if mi >= 2:
                wge(e, sV, mV[mi - 2])
            for tc in range(NT):
                i = e.matmul(mps[:, 0:w],
                             arena_wt[:, tc, dc * 128:(dc + 1) * 128],
                             v_aug[:, tc, a:b], start=(tc == 0), stop=(tc == NT - 1))
            sT.inc(i)

        @blk.vector
        def _(e, w=w, mi=mi, pb=pb, mps=mps):
            wge(e, sT)
            if mi >= 2:
                wge(e, dGP, mG[mi - 2])
            i = e.tensor_copy(msb[:, pb, 0:w], mps[:, 0:w])
            mV[mi] = sV.inc(i)

        @blk.gpsimd
        def _(e, dc=dc, a=a, b=b, w=w, mi=mi, pb=pb):
            wge(e, sV, mV[mi])
            i = e.dma_start(out=cc2_in.ap()[dc * 128:(dc + 1) * 128, a:b],
                            in_=msb[:, pb, 0:w])
            mG[mi] = dGP.inc(i, 16)
            if mi == DC // 2 * len(m_pieces) - 1:
                wge(e, dGP)
                i = e.collective_compute(
                    "AllGather", ALU.bypass,
                    replica_groups=[[g * NR + r for r in range(NR)]
                                    for g in range(NCORES // NR)],
                    ins=[cc2_in.ap()[0:HROWS, :].opt()], outs=[cc2a_out.ap().opt()])
                sCC.inc(i)

    p2g_T = sT.n

    emit_rs_bcast()

    @blk.gpsimd
    def _(e):
        wge(e, dGP, mG[DC * len(m_pieces) - 1])
        i = e.collective_compute(
            "AllGather", ALU.bypass,
            replica_groups=[[g * NR + r for r in range(NR)] for g in range(NCORES // NR)],
            ins=[cc2_in.ap()[HROWS:2 * HROWS, :].opt()], outs=[cc2b_out.ap().opt()])
        sCC.inc(i)


    @blk.sync
    def _(e):
        wge(e, sCC, 1)
        wge(e, sG)
        wge(e, sV)
        i = e.dma_start(out=ct_aug[0:NR, :], in_=cc1_out.ap())
        dSY.inc(i, 16)

    @blk.tensor
    def _(e):
        wge(e, dSY)
        wge(e, sG)
        wge(e, sS)
        wge(e, sV)
        for (a, b) in PCS:
            i = e.matmul(num_ps[0:5, a:b], pm_s[:, :], ct_aug[:, a:b],
                         start=True, stop=True)
        sT.inc(i)

    @blk.scalar
    def _(e):
        wge(e, sT)
        i = e.activation(sc_exp, num_ps[0:5, :], AF.Exp)
        sS.inc(i)

    @blk.tensor
    def _(e):
        wge(e, sS)
        for dc in range(DC):
            i = e.transpose(sc_ps0[:, dc * 8:dc * 8 + 5],
                            sc_exp[:, dc * 128:(dc + 1) * 128], idf_s[:, :])
        sT.inc(i)

    @blk.vector
    def _(e):
        wge(e, sT)
        for dc in range(DC):
            i = e.tensor_copy(scl_fm[:, dc, :], sc_ps0[:, dc * 8:dc * 8 + 5])
        wge(e, sV, sV.inc(i))
        i = e.tensor_mul(bqs_s[:, :], bqf_s[:, :], scl_fm[:, :, 4])
        sV.inc(i)
    scales_V = sV.n

    q_sp_tg0 = []

    def q_evac(e, og, tg, psl):
        wge(e, dSC, max(0, dSC.n - 16))
        wge(e, sV, scales_V)
        i = e.activation(stg[:, tg % 2, :], psl, AF.Relu,
                         scale=scl_fm[:, og, 4:5], bias=bqs_s[:, og:og + 1])
        m = sS.inc(i)
        wge(e, sS, m)
        i = e.dma_start(out=qt_sp.ap()[og, :, tg * TGW:(tg + 1) * TGW],
                        in_=stg[:, tg % 2, :])
        dSC.inc(i, 16)
        if tg == 0 and og == DC - 1:
            q_sp_tg0.append(dSC.n)
    q_tm = fm_projection(wqt, q_evac, pre_w=q_w0, pre_x0=q_x0[0])

    HC = DC // 2
    arena_h = arena.ap()[:, 0:2 * HC * EW].rearrange("p (b c w) -> p b c w", b=2, c=HC)
    jh_G, jh_V = {}, {}
    units_jh = [(j, h) for h in range(2) for j in range(NR)]
    for ui, (j, h) in enumerate(units_jh):
        pb = ui % 2
        @blk.gpsimd
        def _(e, j=j, h=h, ui=ui, pb=pb):
            if ui == 0:
                wge(e, sCC, 2)
                wge(e, sV)
            if ui == 1:
                wge(e, sT, p2g_T)
            if ui == NR:
                wge(e, sCC, 3)
            if ui >= 2:
                wge(e, sV, jh_V[ui - 2])
            cco = cc2a_out if h == 0 else cc2b_out
            i = e.dma_start(
                out=arena_h[:, pb, :, 0:EW],
                in_=cco.ap()[j * HROWS:(j + 1) * HROWS, :]
                .rearrange("(c p) e -> p c e", p=128))
            jh_G[ui] = dGP.inc(i, 16)
        @blk.vector
        def _(e, j=j, h=h, ui=ui, pb=pb):
            wge(e, dGP, jh_G[ui])
            wge(e, sV, scales_V)
            for c in range(HC):
                dc = h * HC + c
                i = e.scalar_tensor_tensor(ptl[:, dc, :], arena_h[:, pb, c, 0:EW],
                                           scl_fm[:, dc, j:j + 1], ptl[:, dc, :],
                                           ALU.mult, ALU.add)
            jh_V[ui] = sV.inc(i)

    p3_T_end, p3_V_end, p3_S_end, p3_den, y_dma, qlb_dma = [], [], [], [], [], []
    p3_xl = {}
    for cq in range(NT):
        @blk.scalar
        def _(e, cq=cq):
            if cq < 2:
                wge(e, dSC, q_sp_tg0[0])
                wge(e, sT, q_tm[(0, NTG - 1)])
            elif cq < 4:
                wge(e, dSC)
            else:
                wge(e, sT, p3_T_end[cq - 4])
            i = e.dma_start(out=qtb_s(cq % 4)[:, :, :],
                            in_=qt_sp.ap()[:, :, cq * 128:(cq + 1) * 128]
                            .rearrange("c p t -> p c t"))
            dSC.inc(i, 16)

        @blk.gpsimd
        def _(e, cq=cq):
            if cq < 2:
                wge(e, dSC, q_sp_tg0[0])
                wge(e, sT, q_tm[(0, NTG - 1)])
            elif cq < 4:
                wge(e, dSC)
            else:
                wge(e, sT, p3_T_end[cq - 4])
            i = e.dma_start(out=qlb_s(cq % 4)[:, :, :],
                            in_=ql_sp.ap()[:, :, cq * 128:(cq + 1) * 128]
                            .rearrange("c p t -> p c t"))
            qlb_dma.append(dGP.inc(i, 16))

        @blk.sync
        def _(e, cq=cq):
            if cq >= 2:
                wge(e, sV, p3_den[cq - 2])
            i = e.dma_start(out=xstr[:, cq % 2, :],
                            in_=x_in.ap()[cq * 128:(cq + 1) * 128, :])
            p3_xl[cq] = dXL.inc(i, 16)

        sc_T, sc_V, use_T = {}, {}, {}

        def emit_score(ck):
            psl = (sc_ps0 if ck % 2 == 0 else sc_ps1)[:, :]
            @blk.tensor
            def _(e, cq=cq, ck=ck, psl=psl):
                if ck == 0:
                    wge(e, dSC)
                if ck >= 2:
                    wge(e, sV, sc_V[ck - 2])
                for dc in range(DC):
                    i = e.matmul(psl, wk_fm[:, dc, ck * 128:(ck + 1) * 128],
                                 qtb_s(cq % 4)[:, dc, :], start=(dc == 0), stop=(dc == DC - 1))
                sc_T[ck] = sT.inc(i)
            @blk.vector
            def _(e, cq=cq, ck=ck, psl=psl):
                wge(e, sT, sc_T[ck])
                if ck >= 4:
                    wge(e, sT, use_T[ck - 4])
                if ck == cq:
                    i = e.tensor_mul(scb[:, ck % 4, :], psl, triu_s[:, :])
                else:
                    i = e.tensor_copy(scb[:, ck % 4, :], psl)
                sc_V[ck] = sV.inc(i)

        emit_score(0)
        if cq >= 1:
            emit_score(1)
        @blk.tensor
        def _(e, cq=cq):
            if cq > 0:
                wge(e, sV, p3_den[cq - 1])
                wge(e, sS, p3_S_end[cq - 1])
            else:
                wge(e, sV)
            for dc in range(DC):
                for (a, b) in PCS:
                    e.matmul(num_ps[:, a:b], qtb_s(cq % 4)[:, dc, :], ptl[:, dc, a:b],
                             start=(dc == 0), stop=False)
                i = e.matmul(den_ps[:, 0:4], qtb_s(cq % 4)[:, dc, :], ptl[:, dc, D:EW],
                             start=(dc == 0), stop=False)
            sT.inc(i)
        for ck in range(cq + 1):
            if ck + 2 <= cq:
                emit_score(ck + 2)
            @blk.tensor
            def _(e, cq=cq, ck=ck):
                wge(e, sV, sc_V[ck])
                last = (ck == cq)
                for (a, b) in PCS:
                    e.matmul(num_ps[:, a:b], scb[:, ck % 4, :], v_aug[:, ck, a:b],
                             start=False, stop=(last and b == D))
                i = e.matmul(den_ps[:, 0:4], scb[:, ck % 4, :], v_aug[:, ck, D:EW],
                             start=False, stop=last)
                use_T[ck] = sT.inc(i)
        @blk.tensor
        def _(e, cq=cq):
            wge(e, dGP, qlb_dma[cq])
            for dc in range(DC):
                for (a, b) in PCS:
                    i = e.matmul(ffn_ps[:, a:b], qlb_s(cq % 4)[:, dc, :], qrhs[:, dc, a:b],
                                 start=(dc == 0), stop=(dc == DC - 1))
            p3_T_end.append(sT.inc(i))

        @blk.vector
        def _(e, cq=cq):
            wge(e, sT, p3_T_end[cq])
            wge(e, dXL, p3_xl[cq])
            i = e.tensor_scalar_add(sm2[:, :], den_ps[:, 0:1], 1e-6)
            wge(e, sV, sV.inc(i))
            i = e.reciprocal(sm2[:, :], sm2[:, :])
            wge(e, sV, sV.inc(i))
            if cq > 0:
                wge(e, dSY, y_dma[cq - 1])
            i = e.scalar_tensor_tensor(outb[:, 0, :], num_ps[:, :], sm2[:, 0:1],
                                       xstr[:, cq % 2, :], ALU.mult, ALU.add)
            p3_den.append(sV.inc(i))

        @blk.scalar
        def _(e, cq=cq):
            wge(e, sT, p3_T_end[cq])
            i = e.activation(ub[:, 0, :], ffn_ps[:, :], AF.Copy, scale=lsc[:, cq:cq + 1])
            p3_S_end.append(sS.inc(i))

        @blk.vector
        def _(e, cq=cq):
            wge(e, sS, p3_S_end[cq])
            e.tensor_mul(ub[:, 1, :], ub[:, 0, :], rs_bc[:, :])
            i = e.tensor_add(outb[:, 0, :], outb[:, 0, :], ub[:, 1, :])
            p3_V_end.append(sV.inc(i))

        @blk.sync
        def _(e, cq=cq):
            wge(e, sV, p3_V_end[cq])
            i = e.dma_start(out=y_out.ap()[cq * 128:(cq + 1) * 128, :],
                            in_=outb[:, 0, :])
            y_dma.append(dSY.inc(i, 16))

    @blk.sync
    def _(e):
        wge(e, dSY)

    ctx.close()
    _merge_engine_blocks(nc)
    return nc


_CACHE = {}


def _get_nc(T, D, NR, NCORES):
    key = (T, D, NR, NCORES)
    if key not in _CACHE:
        _CACHE[key] = build_nc(T, D, NR, NCORES)
    return _CACHE[key]


def make_in_maps(x2d, W_q, b_q, W_k, b_k, W_v, b_v, W_a, b_a, rhs,
                 n_cores=8, NR=4):
    import ml_dtypes
    bf16 = ml_dtypes.bfloat16
    TT, D = x2d.shape
    T = TT // n_cores
    DC = D // 128
    f = np.float32
    shared = dict(
        wqt=np.ascontiguousarray(np.asarray(W_q, dtype=f).T),
        wkt=np.ascontiguousarray(np.asarray(W_k, dtype=f).T),
        wvt=np.ascontiguousarray(np.asarray(W_v, dtype=f).T),
        wat=np.ascontiguousarray(np.asarray(W_a, dtype=f).T),
        rhst=np.ascontiguousarray(np.asarray(rhs, dtype=f).T),
        bq_fm=np.ascontiguousarray(np.asarray(b_q, dtype=f).reshape(DC, 128).T),
        bk_fm=np.ascontiguousarray(np.asarray(b_k, dtype=f).reshape(DC, 128).T),
        ba_row=np.ascontiguousarray(np.asarray(b_a, dtype=f).reshape(1, D)),
        bv_row=np.ascontiguousarray(np.asarray(b_v, dtype=f).reshape(1, D)),
        tri_lo=np.tril(np.ones((128, 128))).astype(bf16),
        triu_m=np.triu(np.ones((128, 128))).astype(f),
        ident_f=np.eye(128, dtype=f),
        ident_b=np.eye(128).astype(bf16),
    )
    in_maps = []
    for c in range(n_cores):
        R = c % NR
        pm = np.zeros((5, 5), dtype=f)
        for kp in range(4):
            for j in range(4):
                if j < kp <= R:
                    pm[kp, j] = 1.0
            pm[kp, 4] = 1.0 if kp > R else 0.0
        for j in range(4):
            if j >= R:
                pm[4, j] = BIG_NEG
        m = dict(shared)
        m["x"] = np.ascontiguousarray(x2d[c * T:(c + 1) * T], dtype=f)
        m["pm"] = pm
        in_maps.append(m)
    return in_maps


def kernel(L0_attn_x_seq, L0_attn_W_q, L0_attn_b_q, L0_attn_W_k, L0_attn_b_k,
           L0_attn_W_v, L0_attn_b_v, L0_attn_W_a, L0_attn_b_a, L0_ffn_rhs):
    B, Tseq, D = L0_attn_x_seq.shape
    n_cores = 8
    NR = n_cores // B
    x2d = np.asarray(L0_attn_x_seq, dtype=np.float32).reshape(B * Tseq, D)
    T = (B * Tseq) // n_cores
    nc = _get_nc(T, D, NR, n_cores)
    in_maps = make_in_maps(
        x2d,
        L0_attn_W_q, L0_attn_b_q, L0_attn_W_k, L0_attn_b_k,
        L0_attn_W_v, L0_attn_b_v, L0_attn_W_a, L0_attn_b_a,
        L0_ffn_rhs, n_cores=n_cores, NR=NR)
    res = run_bass_kernel_spmd(nc, in_maps, core_ids=list(range(n_cores)))
    y = np.concatenate([r["y"] for r in res.results], axis=0)
    return np.ascontiguousarray(y.reshape(B, Tseq, D), dtype=np.float32)
